# version 01
# baseline (speedup 1.0000x reference)
import numpy as np
from contextlib import ExitStack

import concourse.bass as bass
import concourse.mybir as mybir
from concourse.bass_utils import run_bass_kernel_spmd

F32 = mybir.dt.float32
BF16 = mybir.dt.bfloat16
AF = mybir.ActivationFunctionType
ALU = mybir.AluOpType
AX = mybir.AxisListType

P = 128
SEG = 4096
TH = SEG + 3
TC = 512
NCH = SEG // TC
DM = 1024
ND = DM // P
DI = 2048
NDI = DI // P
E4 = 4096
NST = 16
DTR = 64
NT = DI * NST // P
FI = 3072
NFI = FI // P
MAGIC = 12582912.0
RG8 = [[0, 1, 2, 3, 4, 5, 6, 7]]

_DBG = ""
_PHASES = 5


class KB:
    ENG = ("sync", "vector", "scalar", "tensor", "gpsimd")

    def __init__(self, nc, es):
        self.nc = nc
        self.es = es
        self.sem = {e: es.enter_context(nc.semaphore("sp_" + e)) for e in self.ENG}
        self.cnt = {e: 0 for e in self.ENG}
        self.ops = {e: [] for e in self.ENG}
        self._last_wait = {}
        self.dsem = {}

    def op(self, eng, fn, waits=(), inc=1, strict=False):
        ws = []
        if strict:
            ws.append((self.sem[eng], self.cnt[eng]))
        for t in waits:
            if t is None:
                continue
            (s, v) = t
            if s is self.sem[eng]:
                continue
            key = (eng, s.name)
            if self._last_wait.get(key, -1) >= v:
                continue
            self._last_wait[key] = v
            ws.append((s, v))
        self.cnt[eng] += inc
        self.ops[eng].append((fn, tuple(ws), inc))
        return (self.sem[eng], self.cnt[eng])

    def dma(self, fn, waits=(), stream="misc", eng="sync"):
        if stream not in self.dsem:
            self.dsem[stream] = [self.es.enter_context(
                self.nc.semaphore("sd_" + stream)), 0]
        sem, cntv = self.dsem[stream]
        self.dsem[stream][1] = cntv + 16
        tok = (sem, cntv + 16)

        def wrapped(e, fn=fn, sem=sem):
            inst = fn(e)
            inst.then_inc(sem, 16)
            return inst
        self.op(eng, wrapped, waits, inc=0)
        return tok

    def dtok(self, stream):
        if stream not in self.dsem:
            return None
        sem, cntv = self.dsem[stream]
        return (sem, cntv)

    def tok(self, eng):
        return (self.sem[eng], self.cnt[eng])

    def toks(self):
        return [self.tok(e) for e in self.ENG]

    def barrier(self):
        toks = self.toks() + [(s, c) for (s, c) in self.dsem.values()]
        for e in self.ENG:
            self.op(e, None, waits=toks, inc=0)

    def finalize(self):
        with self.nc.Block() as block:
            for e in self.ENG:
                getattr(block, e)(self._mk_runner(e))

    def _mk_runner(self, e):
        ops, sem = self.ops[e], self.sem[e]

        def run(eng):
            for fn, waits, inc in ops:
                for (s, v) in waits:
                    eng.wait_ge(s, v)
                if fn is None:
                    continue
                inst = fn(eng)
                if inc:
                    assert inst is not None, f"op on {e} returned None, inc={inc}"
                    inst.then_inc(sem, inc)
        return run


def build_program():
    nc = bass.Bass("TRN2", target_bir_lowering=False, debug=False,
                   detect_race_conditions=False)
    es = ExitStack()
    kb = KB(nc, es)

    def hbm(name, shape, dt=F32, kind="Internal"):
        return nc.dram_tensor(name, shape, dt, kind=kind).ap()

    def sb(name, shape, dt=F32, stack=None):
        return (stack or es).enter_context(nc.sbuf_tensor(name, shape, dt))

    x_h = hbm("x_seg", [DM, TH], F32, "ExternalInput")
    w_inT_h = hbm("w_inT", [DM, E4], F32, "ExternalInput")
    w_xTd_h = hbm("w_xTd", [DI, 97], F32, "ExternalInput")
    w_dtT_h = hbm("w_dtT", [DTR, DI], F32, "ExternalInput")
    convw_h = hbm("convw_t", [P, NDI, 4], F32, "ExternalInput")
    convb_h = hbm("convb_t", [P, NDI], F32, "ExternalInput")
    bdt_h = hbm("bdt_t", [P, NDI], F32, "ExternalInput")
    dp_h = hbm("dp_t", [P, NDI], F32, "ExternalInput")
    alogc_h = hbm("alog_cols", [P, NT], F32, "ExternalInput")
    alogT_h = hbm("alogT", [NST, DI], F32, "ExternalInput")
    w_o1T_h = hbm("w_o1T", [E4, DI], F32, "ExternalInput")
    w_o2T_h = hbm("w_o2T", [DI, DM], F32, "ExternalInput")
    wg_h = hbm("wg", [FI, DM], F32, "ExternalInput")
    wu_h = hbm("wu", [FI, DM], F32, "ExternalInput")
    wd_h = hbm("wd", [DM, FI], F32, "ExternalInput")
    blkfull_h = hbm("blkfull", [P, NDI * P], F32, "ExternalInput")
    sel8f_h = hbm("sel8full", [P, NDI * P], F32, "ExternalInput")
    ident_h = hbm("ident", [P, P], F32, "ExternalInput")
    hmask_h = hbm("hmask", [P, 1], F32, "ExternalInput")
    mseg8_h = hbm("mseg8", [P, 8], F32, "ExternalInput")
    out_h = hbm("xout", [P, ND, SEG], F32, "ExternalOutput")

    st_delta = hbm("st_delta", [P, NDI, SEG], BF16)
    st_w = hbm("st_w", [P, NDI, SEG], BF16)
    st_u = hbm("st_u", [P, NDI, SEG], BF16)
    st_zs = hbm("st_zs", [P, NDI, SEG], BF16)
    st_yl = hbm("st_yl", [P, NDI, SEG], BF16)
    st_yc = hbm("st_yc", [P, NDI, SEG], BF16)
    st_x1 = hbm("st_x1", [P, ND, SEG], F32)
    st_v = hbm("st_v", [P, ND, SEG], F32)
    st_h = hbm("st_h", [P, NDI, SEG], BF16)

    st_wg = hbm("st_wg", [P, ND, FI], BF16)
    st_wu = hbm("st_wu", [P, ND, FI], BF16)
    st_wd = hbm("st_wd", [P, NFI, DM], BF16)
    st_gate = hbm("st_gate", [P, NFI, SEG], BF16)
    st_up = hbm("st_up", [P, NFI, SEG], BF16)
    st_ku = hbm("st_ku", [P, NFI, SEG], BF16)
    st_t2 = hbm("st_t2", [P, NFI, SEG], BF16)
    st_o2 = hbm("st_o2", [P, NFI, SEG], BF16)

    cc_rp_in = hbm("cc_rp_in", [1, 2 * P * NT], F32)
    cc_rp_out = hbm("cc_rp_out", [8, 2 * P * NT], F32)
    cc_mx_in = hbm("cc_mx_in", [1, 2], F32)
    cc_mx_out = hbm("cc_mx_out", [1, 2], F32)
    cc_m2_in = hbm("cc_m2_in", [1, 2], F32)
    cc_m2_out = hbm("cc_m2_out", [1, 2], F32)
    cc_m3_in = hbm("cc_m3_in", [1, 2], F32)
    cc_m3_out = hbm("cc_m3_out", [1, 2], F32)
    cc_m4_in = hbm("cc_m4_in", [1, 2], F32)
    cc_m4_out = hbm("cc_m4_out", [1, 2], F32)

    dbg = {}
    if _DBG == "p1":
        for nm, src in (("d_delta", st_delta), ("d_w", st_w),
                        ("d_u", st_u), ("d_zs", st_zs)):
            dbg[nm] = (hbm(nm, [P, NDI, SEG], BF16, "ExternalOutput"), src)
    if _DBG in ("p15", "p2"):
        dbg["d_yl"] = (hbm("d_yl", [P, NDI, SEG], BF16, "ExternalOutput"), st_yl)
    if _DBG == "p2":
        dbg["d_yc"] = (hbm("d_yc", [P, NDI, SEG], BF16, "ExternalOutput"), st_yc)
    if _DBG == "p4":
        dbg["d_gate"] = (hbm("d_gate", [P, NFI, SEG], BF16, "ExternalOutput"), st_gate)
        dbg["d_up"] = (hbm("d_up", [P, NFI, SEG], BF16, "ExternalOutput"), st_up)
        dbg["d_wg"] = (hbm("d_wg", [P, ND, FI], BF16, "ExternalOutput"), st_wg)
        dbg["d_t2"] = (hbm("d_t2", [P, NFI, SEG], BF16, "ExternalOutput"), st_t2)
        dbg["d_o2"] = (hbm("d_o2", [P, NFI, SEG], BF16, "ExternalOutput"), st_o2)
        dbg["d_ku"] = (hbm("d_ku", [P, NFI, SEG], BF16, "ExternalOutput"), st_ku)
        d_sgn_h = hbm("d_sgn", [P, NT], F32, "ExternalOutput")
        d_lab_h = hbm("d_lab", [P, NT], F32, "ExternalOutput")
        d_sin_h = hbm("d_sin", [P, NT], F32, "ExternalOutput")
        d_e2_h = hbm("d_e2", [P, TC], F32, "ExternalOutput")
        d_dbc_h = hbm("d_dbc", [P, TC], BF16, "ExternalOutput")
        d_sgs_h = hbm("d_sgs", [P, P], BF16, "ExternalOutput")

    bcd_bf = sb("bcd_bf", [P, SEG], BF16)
    cdpu_bf = sb("cdpu_bf", [17, SEG], BF16)
    a_cols = sb("a_cols", [P, NT])
    lnabs_c = sb("lnabs_c", [P, NT])
    sign_c = sb("sign_c", [P, NT])
    state_c = sb("state_c", [P, NT])
    rseg_c = sb("rseg_c", [P, NT])
    pseg_c = sb("pseg_c", [P, NT])
    convw = sb("convw", [P, NDI, 4])
    convb = sb("convb", [P, NDI])
    bdt = sb("bdt", [P, NDI])
    dpc = sb("dpc", [P, NDI])
    hm_col = sb("hm_col", [P, 1])
    mseg_col = sb("mseg_col", [P, 8])
    ones_bf = sb("ones_bf", [P, P], BF16)
    blkf_bf = sb("blkf_bf", [P, NDI * P], BF16)
    sel8_bf = sb("sel8_bf", [P, NDI * P], BF16)
    dsum = sb("dsum", [P, NDI])
    dsum_bf = sb("dsum_bf", [P, NDI], BF16)
    dsr = sb("dsr", [P, 1])
    ident_bf = sb("ident_bf", [P, P], BF16)
    lhsT_yt = sb("lhsT_yt", [17, DI], BF16)

    ps = [es.enter_context(nc.psum_tensor(f"ps{i}", [P, TC], F32)) for i in range(8)]

    _cc_store = sb("cc_store", [P, 16])
    _cc_map = {}

    def cc(val):
        if val not in _cc_map:
            i = len(_cc_map)
            assert i < 16
            kb.op("vector", lambda e, i=i, v=val: e.memset(_cc_store[:, i:i + 1], v))
            _cc_map[val] = _cc_store[:, i:i + 1]
        return _cc_map[val]

    CINV_DM = cc(1.0 / DM)
    CEPS = cc(1e-5)
    CNEG1 = cc(-1.0)
    CONE = cc(1.0)

    p1w = ExitStack()
    p0w = ExitStack()
    w_in_bf = sb("w_in_bf", [P, ND, E4], BF16, p1w)
    w_x_bf = sb("w_x_bf", [P, NDI, 97], BF16, p1w)
    w_dt_bf = sb("w_dt_bf", [DTR, DI], BF16, p1w)
    wstg = [sb(f"wstg{i}", [P, E4], F32, p0w) for i in range(2)]

    for (dst, src) in ((convw, convw_h), (convb, convb_h), (bdt, bdt_h),
                       (dpc, dp_h)):
        kb.dma(lambda e, d=dst, s=src: e.dma_start(
            out=d[:, :, :] if len(d.shape) == 3 else d[:, :], in_=s), stream="ini")
    OFF_BLK, OFF_ID, OFF_AT = 0, NDI * P, 0
    kb.dma(lambda e: e.dma_start(out=a_cols[:], in_=alogc_h), stream="ini")
    kb.dma(lambda e: e.dma_start(
        out=wstg[1][0:P, OFF_BLK:OFF_BLK + NDI * P], in_=blkfull_h), stream="ini")
    kb.dma(lambda e: e.dma_start(
        out=wstg[0][0:P, DI:DI + NDI * P], in_=sel8f_h), stream="ini")
    kb.dma(lambda e: e.dma_start(
        out=wstg[1][0:P, OFF_ID:OFF_ID + P], in_=ident_h), stream="ini")
    kb.dma(lambda e: e.dma_start(
        out=wstg[0][0:NST, OFF_AT:OFF_AT + DI], in_=alogT_h), stream="ini")
    kb.dma(lambda e: e.dma_start(out=hm_col[:], in_=hmask_h), stream="ini")
    kb.dma(lambda e: e.dma_start(out=mseg_col[:], in_=mseg8_h), stream="ini")
    t_ini = kb.dtok("ini")
    t_alog = t_blk = t_id = t_alT = t_hm = t_ini

    kb.op("vector", lambda e: e.memset(ones_bf[:], 1.0))
    kb.op("vector", lambda e: e.tensor_copy(
        out=blkf_bf[:], in_=wstg[1][0:P, OFF_BLK:OFF_BLK + NDI * P]), waits=[t_blk])
    kb.op("vector", lambda e: e.tensor_copy(
        out=sel8_bf[:], in_=wstg[0][0:P, DI:DI + NDI * P]), waits=[t_blk])
    kb.op("vector", lambda e: e.memset(dsum[:], 0.0))
    kb.op("vector", lambda e: e.tensor_copy(
        out=ident_bf[:], in_=wstg[1][0:P, OFF_ID:OFF_ID + P]), waits=[t_id])
    t_ac = kb.op("scalar", lambda e: e.activation(
        out=a_cols[:], in_=a_cols[:], func=AF.Exp), waits=[t_alog])
    kb.op("vector", lambda e: e.tensor_scalar(
        out=a_cols[:], in0=a_cols[:], scalar1=-1.0, scalar2=None, op0=ALU.mult),
        waits=[t_ac])
    t_yt = kb.op("scalar", lambda e: e.activation(
        out=wstg[0][0:NST, OFF_AT:OFF_AT + DI],
        in_=wstg[0][0:NST, OFF_AT:OFF_AT + DI], func=AF.Exp), waits=[t_alT])
    kb.op("vector", lambda e: e.memset(lhsT_yt[0:17, :], 1.0))
    kb.op("vector", lambda e: e.tensor_scalar(
        out=lhsT_yt[0:NST, :], in0=wstg[0][0:NST, OFF_AT:OFF_AT + DI],
        scalar1=-1.0, scalar2=None, op0=ALU.mult), waits=[t_yt])
    kb.op("vector", lambda e: e.memset(state_c[:], 0.0))

    tw = []
    for dk in range(ND):
        tl = kb.dma(lambda e, dk=dk: e.dma_start(
            out=wstg[dk % 2][:, :], in_=w_inT_h[P * dk:P * (dk + 1), :]),
            waits=([tw[dk - 2]] if dk >= 2 else [kb.tok("vector")]),
            stream="wA" if dk % 2 == 0 else "wB")
        tw.append(kb.op("vector", lambda e, dk=dk: e.tensor_copy(
            out=w_in_bf[:, dk, :], in_=wstg[dk % 2][:, :]), waits=[tl]))
    t_wx = kb.dma(lambda e: e.dma_start(
        out=wstg[0][:, 0:NDI * 97].rearrange("p (k e) -> p k e", e=97),
        in_=w_xTd_h.rearrange("(k p) e -> p k e", p=P)),
        waits=[tw[-2], kb.tok("vector"), kb.tok("scalar")], stream="wA")
    t_wx = kb.op("vector", lambda e: e.tensor_copy(
        out=w_x_bf[:].rearrange("p a b -> p (a b)"), in_=wstg[0][:, 0:NDI * 97]),
        waits=[t_wx])
    t_wdt = kb.dma(lambda e: e.dma_start(out=wstg[1][0:DTR, 0:DI], in_=w_dtT_h),
                   waits=[tw[-1], kb.tok("vector")], stream="wB")
    t_wdt = kb.op("vector", lambda e: e.tensor_copy(
        out=w_dt_bf[:], in_=wstg[1][0:DTR, 0:DI]), waits=[t_wdt])

    kb.barrier()
    p0w.close()

    p1 = ExitStack()
    x_ld = [sb(f"x_ld{i}", [P, TC], F32, p1) for i in range(2)]
    x_bf = sb("x_bf", [P, ND, TC], BF16, p1)
    xh_bf = sb("xh_bf", [P, ND, TC], BF16, p1)
    stat = {k: sb("sta_" + k, [P, TC], F32, p1) for k in ("m", "rstd", "t1", "t2")}
    xi_bf = sb("xi_bf", [P, NDI, TC + 3], BF16, p1)
    zsdl_bf = sb("zsdl_bf", [P, NDI, TC], BF16, p1)
    u_bf = sb("u_bf", [P, NDI, TC], BF16, p1)
    cacc = sb("cacc", [P, TC], F32, p1)
    tts = sb("tts", [P, TC], F32, p1)
    sps = sb("sps", [P, TC], F32, p1)
    dls = sb("dls", [P, TC], F32, p1)
    dtr_bf = sb("dtr_bf", [DTR, TC], BF16, p1)

    st = {"prev_bank": [None] * 8, "halo_free": None}

    def emit_ln(width, off, deps):
        t_casts = []
        for dk in range(ND):
            tl = kb.dma(lambda e, dk=dk, w=width, o=off: e.dma_start(
                out=x_ld[dk % 2][:, 0:w], in_=x_h[P * dk:P * (dk + 1), o:o + w]),
                waits=deps + ([t_casts[dk - 2]] if dk >= 2 else []),
                stream="x0" if dk % 2 == 0 else "x1")
            t_casts.append(kb.op("vector", lambda e, dk=dk, w=width: e.tensor_copy(
                out=x_bf[:, dk, 0:w], in_=x_ld[dk % 2][:, 0:w]), waits=[tl]))
        t_sq = kb.op("scalar", lambda e, w=width: e.activation(
            out=xh_bf[:, :, 0:w], in_=x_bf[:, :, 0:w], func=AF.Square),
            waits=[t_casts[-1]])
        tmm = None
        for dk in range(ND):
            tmm = kb.op("tensor", lambda e, dk=dk, w=width: e.matmul(
                out=ps[4][:, 0:w], lhsT=ones_bf[:], rhs=x_bf[:, dk, 0:w],
                start=(dk == 0), stop=(dk == ND - 1)),
                waits=[t_casts[dk], st["prev_bank"][4]])
        for dk in range(ND):
            tmm = kb.op("tensor", lambda e, dk=dk, w=width: e.matmul(
                out=ps[5][:, 0:w], lhsT=ones_bf[:], rhs=xh_bf[:, dk, 0:w],
                start=(dk == 0), stop=(dk == ND - 1)),
                waits=[t_sq, st["prev_bank"][5]])
        t_m = kb.op("scalar", lambda e, w=width: e.activation(
            out=stat["m"][:, 0:w], in_=ps[4][:, 0:w], func=AF.Identity, scale=CINV_DM),
            waits=[tmm])
        t_v = kb.op("scalar", lambda e, w=width: e.activation(
            out=stat["t2"][:, 0:w], in_=ps[5][:, 0:w], func=AF.Identity, scale=CINV_DM),
            waits=[tmm])
        st["prev_bank"][4] = t_m
        st["prev_bank"][5] = t_v
        t_ms = kb.op("scalar", lambda e, w=width: e.activation(
            out=stat["t1"][:, 0:w], in_=stat["m"][:, 0:w], func=AF.Square),
            waits=[t_m])
        strict = width < P
        t_var = kb.op("vector", lambda e, w=width: e.tensor_sub(
            out=stat["t2"][:, 0:w], in0=stat["t2"][:, 0:w], in1=stat["t1"][:, 0:w]),
            waits=[t_v, t_ms], strict=strict)
        t_sd = kb.op("scalar", lambda e, w=width: e.activation(
            out=stat["t1"][:, 0:w], in_=stat["t2"][:, 0:w], func=AF.Sqrt, bias=CEPS),
            waits=[t_var])
        t_r = kb.op("vector", lambda e, w=width: e.reciprocal(
            out=stat["rstd"][:, 0:w], in_=stat["t1"][:, 0:w]), waits=[t_sd],
            strict=strict)
        t_app = None
        for dk in range(ND):
            ta = kb.op("vector", lambda e, dk=dk, w=width: e.tensor_sub(
                out=stat["t1"][:, 0:w], in0=x_bf[:, dk, 0:w], in1=stat["m"][:, 0:w]),
                waits=[t_r], strict=strict)
            t_app = kb.op("vector", lambda e, dk=dk, w=width: e.tensor_mul(
                out=xh_bf[:, dk, 0:w], in0=stat["t1"][:, 0:w],
                in1=stat["rstd"][:, 0:w]), waits=[ta], strict=strict)
        return t_app

    def emit_xz(width, t_xh, is_halo, zs_free=None):
        evac = []
        for em in range(32):
            bank = em % 2
            tmm = None
            for dk in range(ND):
                tmm = kb.op("tensor", lambda e, em=em, dk=dk, bank=bank, w=width: e.matmul(
                    out=ps[bank][:, 0:w],
                    lhsT=w_in_bf[:, dk, P * em:P * (em + 1)],
                    rhs=xh_bf[:, dk, 0:w],
                    start=(dk == 0), stop=(dk == ND - 1)),
                    waits=[t_xh, st["prev_bank"][bank]])
            if em < NDI:
                if is_halo:
                    te = kb.op("scalar", lambda e, em=em, bank=bank: e.activation(
                        out=xi_bf[:, em, 0:3], in_=ps[bank][:, 0:3],
                        func=AF.Identity, scale=hm_col[:, 0:1]),
                        waits=[tmm, t_hm])
                else:
                    te = kb.op("scalar", lambda e, em=em, bank=bank, w=width: e.copy(
                        out=xi_bf[:, em, 3:3 + w], in_=ps[bank][:, 0:w]),
                        waits=[tmm, st["halo_free"]])
            else:
                if is_halo:
                    te = tmm
                else:
                    te = kb.op("scalar", lambda e, em=em, bank=bank, w=width: e.activation(
                        out=zsdl_bf[:, em - NDI, 0:w], in_=ps[bank][:, 0:w],
                        func=AF.Silu), waits=[tmm, zs_free])
            st["prev_bank"][bank] = te
            evac.append(te)
        return evac

    t_xh0 = emit_ln(3, 0, [tw[-1], t_wx, kb.tok("vector")])
    emit_xz(3, t_xh0, True)
    kb.barrier()
    if _DBG == "p1":
        d_xi_h = hbm("d_xi", [P, NDI, 3], BF16, "ExternalOutput")
        d_xh_h = hbm("d_xh", [P, ND, 3], BF16, "ExternalOutput")
        d_xb_h = hbm("d_xb", [P, ND, 3], BF16, "ExternalOutput")
        d_st_h = hbm("d_st", [P, 8], F32, "ExternalOutput")
        kb.dma(lambda e: e.dma_start(out=d_xi_h[:, :, :], in_=xi_bf[:, :, 0:3]),
               stream="dxi")
        kb.dma(lambda e: e.dma_start(out=d_xh_h[:, :, :], in_=xh_bf[:, :, 0:3]),
               stream="dxi")
        kb.dma(lambda e: e.dma_start(out=d_xb_h[:, :, :], in_=x_bf[:, :, 0:3]),
               stream="dxi")
        kb.dma(lambda e: e.dma_start(out=d_st_h[:, 0:3], in_=stat["m"][:, 0:3]),
               stream="dxi")
        kb.dma(lambda e: e.dma_start(out=d_st_h[:, 4:7], in_=stat["rstd"][:, 0:3]),
               stream="dxi")
        kb.barrier()

    prev = {"xdbl": None, "dt0": None, "dt1": None, "zs_dma": None,
            "u_dma": None, "dl_dma": None, "w_dma": None}

    for c in range(NCH):
        off = 3 + TC * c
        t_xh = emit_ln(TC, off, [])
        evac = emit_xz(TC, t_xh, False, zs_free=prev["dl_dma"])
        prev["zs_dma"] = kb.dma(lambda e, c=c: e.dma_start(
            out=st_zs[:, :, TC * c:TC * (c + 1)], in_=zsdl_bf[:, :, :]),
            waits=[evac[-1]], stream="szs")

        t_u = []
        for j in range(NDI):
            tk = kb.op("vector", lambda e, j=j: e.tensor_scalar(
                out=cacc[:, :], in0=xi_bf[:, j, 0:TC],
                scalar1=convw[:, j, 0:1], scalar2=None, op0=ALU.mult),
                waits=[evac[j], st["halo_free"], t_u[-1] if t_u else None])
            for k in range(1, 4):
                tk = kb.op("vector", lambda e, j=j, k=k: e.scalar_tensor_tensor(
                    out=cacc[:, :], in0=xi_bf[:, j, k:k + TC],
                    scalar=convw[:, j, k:k + 1], in1=cacc[:, :],
                    op0=ALU.mult, op1=ALU.add), waits=[tk])
            t_u.append(kb.op("scalar", lambda e, j=j: e.activation(
                out=u_bf[:, j, :], in_=cacc[:, :], func=AF.Silu,
                bias=convb[:, j:j + 1]), waits=[tk, prev["w_dma"]]))
        st["halo_free"] = kb.op("vector", lambda e: e.tensor_copy(
            out=xi_bf[:, :, 0:3], in_=xi_bf[:, :, TC:TC + 3]), waits=[t_u[-1]])
        prev["u_dma"] = kb.dma(lambda e, c=c: e.dma_start(
            out=st_u[:, :, TC * c:TC * (c + 1)], in_=u_bf[:, :, :]),
            waits=[t_u[-1]], stream="su")

        tmm = None
        for k in range(NDI):
            tmm = kb.op("tensor", lambda e, k=k: e.matmul(
                out=ps[6][0:97, :], lhsT=w_x_bf[:, k, :], rhs=u_bf[:, k, :],
                start=(k == 0), stop=(k == NDI - 1)),
                waits=[t_u[k], prev["xdbl"]])
        t_dtr = kb.op("scalar", lambda e: e.copy(
            out=dtr_bf[:, :], in_=ps[6][0:DTR, :]), waits=[tmm])
        t_bc = kb.op("scalar", lambda e, c=c: e.copy(
            out=bcd_bf[0:33, TC * c:TC * (c + 1)], in_=ps[6][DTR:97, :]), waits=[tmm])
        t_cd = kb.dma(lambda e, c=c: e.dma_start(
            out=cdpu_bf[0:17, TC * c:TC * (c + 1)],
            in_=bcd_bf[16:33, TC * c:TC * (c + 1)]), waits=[t_bc], stream="scd")
        prev["xdbl"] = t_bc

        t_dl_all = []
        for m in range(NDI):
            bk = 2 + (m % 2)
            tm = kb.op("tensor", lambda e, m=m, bk=bk: e.matmul(
                out=ps[bk][:, :], lhsT=w_dt_bf[:, P * m:P * (m + 1)],
                rhs=dtr_bf[:, :], start=True, stop=True),
                waits=[t_dtr, t_wdt, prev["dt0"] if bk == 2 else prev["dt1"]])
            te = kb.op("scalar", lambda e, m=m, bk=bk: e.activation(
                out=tts[:, :], in_=ps[bk][:, :], func=AF.Identity,
                bias=bdt[:, m:m + 1]), waits=[tm, kb.tok("vector")])
            if bk == 2:
                prev["dt0"] = te
            else:
                prev["dt1"] = te
            t_sp = kb.op("scalar", lambda e: e.activation(
                out=sps[:, :], in_=tts[:, :], func=AF.Abs), waits=[te])
            t_sp = kb.op("scalar", lambda e: e.activation(
                out=sps[:, :], in_=sps[:, :], func=AF.Exp, scale=CNEG1), waits=[t_sp])
            t_sp = kb.op("scalar", lambda e: e.activation(
                out=sps[:, :], in_=sps[:, :], func=AF.Ln, bias=CONE), waits=[t_sp])
            t_dl = kb.op("vector", lambda e: e.scalar_tensor_tensor(
                out=dls[:, :], in0=tts[:, :], scalar=0.0, in1=sps[:, :],
                op0=ALU.max, op1=ALU.add), waits=[t_sp])
            t_dsr = kb.op("vector", lambda e: e.reduce_sum(
                out=dsr[:, 0:1], in_=dls[:, :], axis=AX.X), waits=[t_dl])
            kb.op("vector", lambda e, m=m: e.tensor_add(
                out=dsum[:, m:m + 1], in0=dsum[:, m:m + 1], in1=dsr[:, 0:1]),
                waits=[t_dsr], strict=True)
            t_w = kb.op("vector", lambda e, m=m: e.tensor_mul(
                out=u_bf[:, m, :], in0=dls[:, :], in1=u_bf[:, m, :]),
                waits=[t_dl, tmm, prev["u_dma"]])
            t_db = kb.op("vector", lambda e, m=m: e.tensor_copy(
                out=zsdl_bf[:, m, :], in_=dls[:, :]),
                waits=[t_dl, prev["zs_dma"]])
            t_dl_all.append((t_w, t_db))
        prev["dl_dma"] = kb.dma(lambda e, c=c: e.dma_start(
            out=st_delta[:, :, TC * c:TC * (c + 1)], in_=zsdl_bf[:, :, :]),
            waits=[t_dl_all[-1][1]], stream="sdl")
        prev["w_dma"] = kb.dma(lambda e, c=c: e.dma_start(
            out=st_w[:, :, TC * c:TC * (c + 1)], in_=u_bf[:, :, :]),
            waits=[t_dl_all[-1][0]], stream="sw")
        kb.barrier()

    p1.close()
    p1w.close()
    psc = ExitStack()
    if _PHASES >= 2:
        delta_c = sb("delta_c", [P, NDI, TC], BF16, psc)
        w_c = sb("w_c", [P, NDI, TC], BF16, psc)
        bt_c = sb("bt_c", [P, TC], BF16, psc)
        ct_c = sb("ct_c", [P, TC], BF16, psc)
        dA_s = [sb(f"dA_s{i}", [P, TC], F32, psc) for i in range(2)]
        b_s = [sb(f"b_s{i}", [P, TC], BF16, psc) for i in range(2)]
        s_s = sb("s_s", [P, TC], BF16, psc)
        p_s = [sb(f"p_s{i}", [P, TC], BF16, psc) for i in range(2)]
        yl_s = sb("yl_s", [P, NDI, TC], BF16, psc)
        sgn_s = [sb(f"sgn_s{i}", [P, P], BF16, psc) for i in range(2)]
        dcar = sb("dcar", [P, NDI], F32, psc)
        drep = sb("drep", [P, NT], F32, psc)
        bias_c = sb("bias_c", [P, NT], F32, psc)
        gtmp = [sb(f"gtmp{i}", [P, NT], F32, psc) for i in range(2)]

        def scan_sweep(src_main, st_out, correction):
            prevb = {}
            for c in range(NCH):
                t_ld = kb.dma(lambda e, c=c: e.dma_start(
                    out=delta_c[:, :, :], in_=src_main[:, :, TC * c:TC * (c + 1)]),
                    stream="pld")
                if not correction:
                    t_lw = kb.dma(lambda e, c=c: e.dma_start(
                        out=w_c[:, :, :], in_=st_w[:, :, TC * c:TC * (c + 1)]),
                        stream="plw")
                else:
                    t_lw = None
                    for m in range(NDI):
                        t_lw = kb.op("vector", lambda e, m=m: e.tensor_tensor_scan(
                            out=w_c[:, m, :], data0=delta_c[:, m, :],
                            data1=delta_c[:, m, :], initial=0.0,
                            op0=ALU.add, op1=ALU.bypass), waits=[t_ld])
                    t_dcb = kb.op("vector", lambda e: e.tensor_copy(
                        out=dsum_bf[:, :], in_=dcar[:, :]), strict=True)
                    for jj in range(NDI):
                        tm = kb.op("tensor", lambda e, jj=jj: e.matmul(
                            out=ps[6][:, 0:NDI], lhsT=sel8_bf[:, P * jj:P * (jj + 1)],
                            rhs=dsum_bf[:, :], start=True, stop=True),
                            waits=[t_dcb, kb.tok("scalar")])
                        kb.op("scalar", lambda e, jj=jj: e.activation(
                            out=drep[:, jj::NDI], in_=ps[6][:, 0:NDI],
                            func=AF.Identity), waits=[tm])
                    t_bias = kb.op("vector", lambda e: e.tensor_mul(
                        out=bias_c[:, :], in0=a_cols[:, :], in1=drep[:, :]),
                        waits=[kb.tok("scalar")], strict=True)
                    t_bias = kb.op("vector", lambda e: e.tensor_add(
                        out=bias_c[:, :], in0=bias_c[:, :], in1=lnabs_c[:, :]),
                        waits=[], strict=True)
                    kb.op("vector", lambda e: e.tensor_add(
                        out=dcar[:, :], in0=dcar[:, :], in1=w_c[:, :, TC - 1]),
                        waits=[t_lw, kb.tok("scalar")], strict=True)
                for i8 in range(8):
                    t_bt = kb.dma(lambda e, c=c, i8=i8: e.dma_start(
                        out=bt_c[16 * i8:16 * (i8 + 1), :],
                        in_=bcd_bf[0:16, TC * c:TC * (c + 1)]), stream="pbt")
                    t_ct = kb.dma(lambda e, c=c, i8=i8: e.dma_start(
                        out=ct_c[16 * i8:16 * (i8 + 1), :],
                        in_=bcd_bf[16:32, TC * c:TC * (c + 1)]), stream="pct")
                for j in range(NT):
                    T_, jj = j // NDI, j % NDI
                    g = j // NDI
                    pb = j % 2
                    yb = 4 + (g % 2)
                    tm1 = kb.op("tensor", lambda e, T_=T_, jj=jj, pb=pb: e.matmul(
                        out=ps[pb][:, :], lhsT=sel8_bf[:, P * jj:P * (jj + 1)],
                        rhs=delta_c[:, T_, :] if not correction else w_c[:, T_, :],
                        start=True, stop=True),
                        waits=[t_ld if not correction else t_lw,
                               prevb.get(("bc", pb))])
                    if not correction:
                        t_dA = kb.op("scalar", lambda e, j=j, pb=pb: e.activation(
                            out=dA_s[pb][:, :], in_=ps[pb][:, :], func=AF.Exp,
                            scale=a_cols[:, j:j + 1]),
                            waits=[tm1, prevb.get(("dA", pb))])
                        prevb[("bc", pb)] = t_dA
                        tm2 = kb.op("tensor", lambda e, T_=T_, jj=jj, pb=pb: e.matmul(
                            out=ps[2 + pb][:, :], lhsT=sel8_bf[:, P * jj:P * (jj + 1)],
                            rhs=w_c[:, T_, :], start=True, stop=True),
                            waits=[t_lw, prevb.get(("wc", pb))])
                        t_b = kb.op("vector", lambda e, pb=pb: e.tensor_tensor(
                            out=b_s[pb][:, :], in0=ps[2 + pb][:, :], in1=bt_c[:, :],
                            op=ALU.mult),
                            waits=[tm2, t_bt, prevb.get(("b", pb))])
                        prevb[("wc", pb)] = t_b
                        t_scan = kb.op("vector", lambda e, j=j, pb=pb: e.tensor_tensor_scan(
                            out=s_s[:, :], data0=dA_s[pb][:, :], data1=b_s[pb][:, :],
                            initial=state_c[:, j:j + 1], op0=ALU.mult, op1=ALU.add),
                            waits=[t_dA, t_b], strict=True)
                        prevb[("dA", pb)] = t_scan
                        prevb[("b", pb)] = t_scan
                        t_st = kb.op("vector", lambda e, j=j: e.tensor_copy(
                            out=state_c[:, j:j + 1], in_=s_s[:, TC - 1:TC]),
                            waits=[], strict=True)
                        t_p = kb.op("vector", lambda e, pb=pb: e.tensor_tensor(
                            out=p_s[pb][:, :], in0=s_s[:, :], in1=ct_c[:, :],
                            op=ALU.mult), waits=[t_ct, prevb.get(("p", pb))])
                    else:
                        t_E = kb.op("scalar", lambda e, j=j, pb=pb: e.activation(
                            out=dA_s[pb][:, :], in_=ps[pb][:, :], func=AF.Exp,
                            scale=a_cols[:, j:j + 1], bias=bias_c[:, j:j + 1]),
                            waits=[tm1, t_bias, prevb.get(("dA", pb))])
                        if _DBG == "p2" and c == 0 and j == 0:
                            kb.dma(lambda e: e.dma_start(
                                out=d_e2_h[:, :], in_=dA_s[0][:, :]),
                                stream="de2", waits=[t_E])
                            kb.dma(lambda e: e.dma_start(
                                out=d_dbc_h[:, :], in_=w_c[:, 0, :]),
                                stream="de2", waits=[t_lw])
                            kb.barrier()
                        prevb[("bc", pb)] = t_E
                        t_p = kb.op("vector", lambda e, j=j, pb=pb: e.scalar_tensor_tensor(
                            out=p_s[pb][:, :], in0=dA_s[pb][:, :],
                            scalar=sign_c[:, j:j + 1], in1=ct_c[:, :],
                            op0=ALU.mult, op1=ALU.mult),
                            waits=[t_E, t_ct, prevb.get(("p", pb))])
                        prevb[("dA", pb)] = t_p
                        t_sg = None
                    tmm = kb.op("tensor", lambda e, jj=jj, pb=pb, yb=yb: e.matmul(
                        out=ps[yb][:, :], lhsT=blkf_bf[:, P * jj:P * (jj + 1)],
                        rhs=p_s[pb][:, :],
                        start=(jj == 0), stop=(jj == NDI - 1),
                        skip_group_check=True),
                        waits=[t_p, prevb.get(("y", yb))])
                    prevb[("p", pb)] = tmm
                    if jj == NDI - 1:
                        t_ev = kb.op("scalar", lambda e, g=g, yb=yb: e.copy(
                            out=yl_s[:, g, :], in_=ps[yb][:, :]), waits=[tmm])
                        prevb[("y", yb)] = t_ev
                t_sto = kb.dma(lambda e, c=c: e.dma_start(
                    out=st_out[:, :, TC * c:TC * (c + 1)], in_=yl_s[:, :, :]),
                    waits=[kb.tok("scalar")], stream="pyl")
                kb.barrier()

        scan_sweep(st_delta, st_yl, correction=False)

    if _PHASES >= 3:
        t_dsb = kb.op("vector", lambda e: e.tensor_copy(
            out=dsum_bf[:, :], in_=dsum[:, :]), strict=True)
        for jj in range(NDI):
            tm = kb.op("tensor", lambda e, jj=jj: e.matmul(
                out=ps[6][:, 0:NDI], lhsT=sel8_bf[:, P * jj:P * (jj + 1)],
                rhs=dsum_bf[:, :], start=True, stop=True),
                waits=[t_dsb, kb.tok("scalar")])
            kb.op("scalar", lambda e, jj=jj: e.activation(
                out=drep[:, jj::NDI], in_=ps[6][:, 0:NDI], func=AF.Identity),
                waits=[tm])
        t_ps = kb.op("vector", lambda e: e.tensor_mul(
            out=pseg_c[:, :], in0=a_cols[:, :], in1=drep[:, :]),
            waits=[kb.tok("scalar")], strict=True)
        t_ps = kb.op("scalar", lambda e: e.activation(
            out=pseg_c[:, :], in_=pseg_c[:, :], func=AF.Exp), waits=[t_ps])
        t_c1 = kb.dma(lambda e: e.dma_start(
            out=cc_rp_in[0, 0:P * NT], in_=state_c[:, :]),
            waits=[kb.tok("vector")], stream="cca")
        t_c2 = kb.dma(lambda e: e.dma_start(
            out=cc_rp_in[0, P * NT:2 * P * NT], in_=pseg_c[:, :]),
            waits=[t_ps], stream="cca")
        t_ag = kb.op("gpsimd", lambda e: e.collective_compute(
            "AllGather", ALU.bypass, replica_groups=RG8,
            ins=[cc_rp_in[:, :]], outs=[cc_rp_out[:, :]]),
            waits=[t_c1, t_c2])
        acc = state_c
        rr_s = gtmp[0]
        pr_s = gtmp[1]
        kb.op("vector", lambda e: e.memset(acc[:, :], 0.0), waits=[t_ag])
        for r in range(8):
            t_l1 = kb.dma(lambda e, r=r: e.dma_start(
                out=rr_s[:, :], in_=cc_rp_out[r, 0:P * NT]),
                waits=[t_ag, kb.tok("vector")], stream="ccr")
            t_l2 = kb.dma(lambda e, r=r: e.dma_start(
                out=pr_s[:, :], in_=cc_rp_out[r, P * NT:2 * P * NT]),
                waits=[t_ag, kb.tok("vector")], stream="ccp")
            kb.op("vector", lambda e: e.tensor_mul(
                out=bias_c[:, :], in0=pr_s[:, :], in1=acc[:, :]),
                waits=[t_l1, t_l2], strict=True)
            kb.op("vector", lambda e: e.tensor_add(
                out=bias_c[:, :], in0=bias_c[:, :], in1=rr_s[:, :]), strict=True)
            kb.op("vector", lambda e: e.tensor_sub(
                out=bias_c[:, :], in0=bias_c[:, :], in1=acc[:, :]), strict=True)
            kb.op("vector", lambda e, r=r: e.scalar_tensor_tensor(
                out=acc[:, :], in0=bias_c[:, :], scalar=mseg_col[:, r:r + 1],
                in1=acc[:, :], op0=ALU.mult, op1=ALU.add), strict=True)
        t_ab = kb.op("scalar", lambda e: e.activation(
            out=lnabs_c[:, :], in_=acc[:, :], func=AF.Abs),
            waits=[kb.tok("vector")])
        t_ab = kb.op("vector", lambda e: e.tensor_scalar(
            out=lnabs_c[:, :], in0=lnabs_c[:, :], scalar1=1e-38, scalar2=None,
            op0=ALU.max), waits=[t_ab], strict=True)
        t_ab = kb.op("scalar", lambda e: e.activation(
            out=lnabs_c[:, :], in_=lnabs_c[:, :], func=AF.Ln), waits=[t_ab])
        t_sg = kb.op("vector", lambda e: e.tensor_scalar(
            out=sign_c[:, :], in0=acc[:, :], scalar1=0.0, scalar2=None,
            op0=ALU.is_ge), waits=[kb.tok("scalar")], strict=True)
        t_sg = kb.op("vector", lambda e: e.tensor_scalar(
            out=sign_c[:, :], in0=sign_c[:, :], scalar1=2.0, scalar2=-1.0,
            op0=ALU.mult, op1=ALU.add), waits=[t_sg], strict=True)
        kb.op("vector", lambda e: e.memset(dcar[:, :], 0.0))
        kb.barrier()
        if _DBG == "p2":
            kb.dma(lambda e: e.dma_start(out=d_sgn_h[:, :], in_=sign_c[:, :]),
                   stream="dsg")
            kb.dma(lambda e: e.dma_start(out=d_lab_h[:, :], in_=lnabs_c[:, :]),
                   stream="dsg")
            kb.dma(lambda e: e.dma_start(out=d_sin_h[:, :], in_=state_c[:, :]),
                   stream="dsg")
            kb.barrier()
        scan_sweep(st_delta, st_yc, correction=True)

    if _PHASES >= 4:
        kb.barrier()
        psc.close()
        p3 = ExitStack()
        wo2_bf = sb("wo2_bf", [P, NDI, DM], BF16, p3)
        h_bf = sb("h_bf", [P, NDI, TC], BF16, p3)
        p3w = ExitStack()
        st_wo1 = hbm("st_wo1", [P, 32, DI], BF16)
        wo_stg = [sb(f"wo_stg{i}", [P, DI], F32, p3w) for i in range(2)]
        wo_bf = sb("wo_bf", [P, DI], BF16, p3w)
        t_pc = None
        for kk in range(32):
            tl = kb.dma(lambda e, kk=kk: e.dma_start(
                out=wo_stg[kk % 2][:, :], in_=w_o1T_h[P * kk:P * (kk + 1), :]),
                waits=[t_pc], stream="wo" + str(kk % 2))
            t_cst = kb.op("vector", lambda e, kk=kk: e.tensor_copy(
                out=wo_bf[:, :], in_=wo_stg[kk % 2][:, :]), waits=[tl, t_pc])
            t_pc = kb.dma(lambda e, kk=kk: e.dma_start(
                out=st_wo1[:, kk, :], in_=wo_bf[:, :]), waits=[t_cst],
                stream="wo2")
        for kk in range(NDI):
            tl = kb.dma(lambda e, kk=kk: e.dma_start(
                out=wo_stg[kk % 2][:, 0:DM], in_=w_o2T_h[P * kk:P * (kk + 1), :]),
                waits=[kb.tok("vector")], stream="wo" + str(kk % 2))
            kb.op("vector", lambda e, kk=kk: e.tensor_copy(
                out=wo2_bf[:, kk, :], in_=wo_stg[kk % 2][:, 0:DM]), waits=[tl])
        kb.barrier()
        p3w.close()

        p3a = ExitStack()
        yin = {k: sb("yin_" + k, [P, NDI, TC], BF16, p3a)
               for k in ("yl", "yc", "u", "zs")}
        ycat = sb("ycat", [P, 32, TC], BF16, p3a)
        wslab = [sb(f"wslab{i}", [P, 32, P], BF16, p3a) for i in range(2)]
        for c in range(NCH):
            cols = slice(TC * c, TC * (c + 1))
            tl = {}
            for k, srcs in (("yl", st_yl), ("yc", st_yc), ("u", st_u),
                            ("zs", st_zs)):
                tl[k] = kb.dma(lambda e, k=k, srcs=srcs, c=c: e.dma_start(
                    out=yin[k][:, :, :], in_=srcs[:, :, TC * c:TC * (c + 1)]),
                    stream="l3" + k)
            t_yf = None
            for m in range(NDI):
                t1 = kb.op("vector", lambda e, m=m: e.scalar_tensor_tensor(
                    out=ycat[:, m, :], in0=yin["u"][:, m, :],
                    scalar=dpc[:, m:m + 1], in1=yin["yl"][:, m, :],
                    op0=ALU.mult, op1=ALU.add),
                    waits=[tl["u"], tl["yl"]])
                t1 = kb.op("vector", lambda e, m=m: e.tensor_add(
                    out=ycat[:, m, :], in0=ycat[:, m, :], in1=yin["yc"][:, m, :]),
                    waits=[tl["yc"]])
                t_yf = kb.op("vector", lambda e, m=m: e.tensor_mul(
                    out=ycat[:, m, :], in0=ycat[:, m, :], in1=yin["zs"][:, m, :]),
                    waits=[tl["zs"]])
            for m in range(NDI):
                bk = m % 2
                tm = kb.op("tensor", lambda e, m=m, bk=bk, c=c: e.matmul(
                    out=ps[bk][:, :], lhsT=lhsT_yt[:, P * m:P * (m + 1)],
                    rhs=cdpu_bf[0:17, TC * c:TC * (c + 1)],
                    start=True, stop=True),
                    waits=[prev.get(("yt", bk)), kb.dtok("scd")])
                te = kb.op("scalar", lambda e, m=m, bk=bk: e.copy(
                    out=ycat[:, NDI + m, :], in_=ps[bk][:, :]), waits=[tm])
                prev[("yt", bk)] = te
            t_h = None
            for m in range(NDI):
                wb = m % 2
                t_ws = kb.dma(lambda e, m=m, wb=wb: e.dma_start(
                    out=wslab[wb][:, :, :],
                    in_=st_wo1[:, :, P * m:P * (m + 1)]),
                    waits=[prev.get(("ws", wb))], stream="lws" + str(wb))
                bk = 2 + m % 2
                tm = None
                for k in range(32):
                    tm = kb.op("tensor", lambda e, m=m, k=k, bk=bk, wb=wb: e.matmul(
                        out=ps[bk][:, :], lhsT=wslab[wb][:, k, :],
                        rhs=ycat[:, k, :], start=(k == 0), stop=(k == 31)),
                        waits=[t_ws, t_yf, kb.tok("scalar"),
                               prev.get(("h", bk))])
                prev[("ws", wb)] = tm
                t_h = kb.op("scalar", lambda e, m=m, bk=bk: e.activation(
                    out=h_bf[:, m, :], in_=ps[bk][:, :], func=AF.Relu), waits=[tm])
                prev[("h", bk)] = t_h
            kb.dma(lambda e, c=c: e.dma_start(
                out=st_h[:, :, TC * c:TC * (c + 1)], in_=h_bf[:, :, :]),
                waits=[t_h, kb.tok("scalar")], stream="sh")
            kb.barrier()

        kb.barrier()
        p3a.close()
        mix_f = sb("mix_f", [P, ND, TC], F32, p3)
        x1_f = sb("x1_f", [P, ND, TC], F32, p3)
        v_f = sb("v_f", [P, ND, TC], F32, p3)
        xr_f = sb("xr_f", [P, ND, TC], F32, p3)
        lnb = sb("lnb", [P, ND, TC], BF16, p3)
        lnsq = sb("lnsq", [P, ND, TC], BF16, p3)
        lst = {k: sb("l3_" + k, [P, TC], F32, p3) for k in ("m", "r", "t1", "t2")}
        vmax_c = sb("vmax_c", [P, 1], F32, p3)
        vmr = sb("vmr", [P, 1], F32, p3)
        kb.op("vector", lambda e: e.memset(vmax_c[:], 0.0))

        def emit_ln_tail(src_f, width, t_src):
            t_c = kb.op("vector", lambda e, w=width: e.tensor_copy(
                out=lnb[:, :, 0:w], in_=src_f[:, :, 0:w]), waits=[t_src])
            t_s = kb.op("scalar", lambda e, w=width: e.activation(
                out=lnsq[:, :, 0:w], in_=lnb[:, :, 0:w], func=AF.Square),
                waits=[t_c])
            tmm = None
            for dk in range(ND):
                tmm = kb.op("tensor", lambda e, dk=dk, w=width: e.matmul(
                    out=ps[4][:, 0:w], lhsT=ones_bf[:], rhs=lnb[:, dk, 0:w],
                    start=(dk == 0), stop=(dk == ND - 1)),
                    waits=[t_c, prev.get(("ln", 4))])
            for dk in range(ND):
                tmm = kb.op("tensor", lambda e, dk=dk, w=width: e.matmul(
                    out=ps[5][:, 0:w], lhsT=ones_bf[:], rhs=lnsq[:, dk, 0:w],
                    start=(dk == 0), stop=(dk == ND - 1)),
                    waits=[t_s, prev.get(("ln", 5))])
            t_m = kb.op("scalar", lambda e, w=width: e.activation(
                out=lst["m"][:, 0:w], in_=ps[4][:, 0:w], func=AF.Identity,
                scale=CINV_DM), waits=[tmm])
            t_v = kb.op("scalar", lambda e, w=width: e.activation(
                out=lst["t2"][:, 0:w], in_=ps[5][:, 0:w], func=AF.Identity,
                scale=CINV_DM), waits=[tmm])
            prev[("ln", 4)] = t_m
            prev[("ln", 5)] = t_v
            t_ms = kb.op("scalar", lambda e, w=width: e.activation(
                out=lst["t1"][:, 0:w], in_=lst["m"][:, 0:w], func=AF.Square),
                waits=[t_m])
            t_var = kb.op("vector", lambda e, w=width: e.tensor_sub(
                out=lst["t2"][:, 0:w], in0=lst["t2"][:, 0:w], in1=lst["t1"][:, 0:w]),
                waits=[t_v, t_ms])
            t_sd = kb.op("scalar", lambda e, w=width: e.activation(
                out=lst["t1"][:, 0:w], in_=lst["t2"][:, 0:w], func=AF.Sqrt,
                bias=CEPS), waits=[t_var])
            t_r = kb.op("vector", lambda e, w=width: e.reciprocal(
                out=lst["r"][:, 0:w], in_=lst["t1"][:, 0:w]), waits=[t_sd])
            return t_r

        for c in range(NCH):
            t_lh = kb.dma(lambda e, c=c: e.dma_start(
                out=h_bf[:, :, :], in_=st_h[:, :, TC * c:TC * (c + 1)]),
                stream="l3h")
            t_lx = kb.dma(lambda e, c=c: e.dma_start(
                out=xr_f[:, :, :],
                in_=x_h[:, 3 + TC * c:3 + TC * (c + 1)]
                .rearrange("(k p) t -> p k t", p=P)), stream="l3x")
            t_mx = None
            for m in range(ND):
                bk = m % 2
                tm = None
                for k in range(NDI):
                    tm = kb.op("tensor", lambda e, m=m, k=k, bk=bk: e.matmul(
                        out=ps[bk][:, :], lhsT=wo2_bf[:, k, P * m:P * (m + 1)],
                        rhs=h_bf[:, k, :], start=(k == 0), stop=(k == NDI - 1)),
                        waits=[t_lh, prev.get(("mx", bk))])
                t_mx = kb.op("scalar", lambda e, m=m, bk=bk: e.copy(
                    out=mix_f[:, m, :], in_=ps[bk][:, :]), waits=[tm])
                prev[("mx", bk)] = t_mx
            t_r2 = emit_ln_tail(mix_f, TC, t_mx)
            t_x1 = None
            for dk in range(ND):
                ta = kb.op("vector", lambda e, dk=dk: e.tensor_sub(
                    out=lst["t1"][:, :], in0=mix_f[:, dk, :], in1=lst["m"][:, :]),
                    waits=[t_r2])
                ta = kb.op("vector", lambda e, dk=dk: e.tensor_mul(
                    out=lst["t1"][:, :], in0=lst["t1"][:, :], in1=lst["r"][:, :]),
                    waits=[])
                t_x1 = kb.op("vector", lambda e, dk=dk: e.tensor_add(
                    out=x1_f[:, dk, :], in0=lst["t1"][:, :], in1=xr_f[:, dk, :]),
                    waits=[t_lx])
            kb.dma(lambda e, c=c: e.dma_start(
                out=st_x1[:, :, TC * c:TC * (c + 1)], in_=x1_f[:, :, :]),
                waits=[t_x1], stream="sx1")
            t_r3 = emit_ln_tail(x1_f, TC, t_x1)
            t_v3 = None
            for dk in range(ND):
                ta = kb.op("vector", lambda e, dk=dk: e.tensor_sub(
                    out=lst["t1"][:, :], in0=x1_f[:, dk, :], in1=lst["m"][:, :]),
                    waits=[t_r3])
                t_v3 = kb.op("vector", lambda e, dk=dk: e.tensor_mul(
                    out=v_f[:, dk, :], in0=lst["t1"][:, :], in1=lst["r"][:, :]),
                    waits=[])
            kb.dma(lambda e, c=c: e.dma_start(
                out=st_v[:, :, TC * c:TC * (c + 1)], in_=v_f[:, :, :]),
                waits=[t_v3], stream="sv")
            t_vm = kb.op("vector", lambda e: e.tensor_reduce(
                out=vmr[:, 0:1], in_=v_f[:, :, :].rearrange("p a b -> p (a b)"),
                axis=AX.X, op=ALU.max, apply_absolute_value=True),
                waits=[t_v3], strict=True)
            kb.op("vector", lambda e: e.tensor_max(
                out=vmax_c[:, 0:1], in0=vmax_c[:, 0:1], in1=vmr[:, 0:1]),
                waits=[t_vm], strict=True)
            kb.barrier()

        if _DBG == "p3":
            dbg["d_x1"] = (hbm("d_x1", [P, ND, SEG], F32, "ExternalOutput"), st_x1)
            dbg["d_v"] = (hbm("d_v", [P, ND, SEG], F32, "ExternalOutput"), st_v)
        es.callback(p3.close)

    if _PHASES >= 5:
        kb.barrier()
        p3.close()
        scol = sb("scol", [P, 24])
        vrow = sb("vrow", [1, P])
        fmr = sb("fmr", [P, 1])
        cc_sc = hbm("cc_sc", [1, P], F32)

        vone = sb("vone", [1, 2])
        ones1f = sb("ones1f", [1, P])
        kb.op("vector", lambda e: e.memset(ones1f[:], 1.0))

        def cross_core_max2(src_cols, cc_in, cc_out, dst_cols, t_src):
            toks = []
            tprev = t_src
            for i, (ten, col) in enumerate(src_cols):
                t1 = kb.dma(lambda e, ten=ten, col=col, i=i: e.dma_start(
                    out=cc_sc[0, 0:P], in_=ten[:, col:col + 1]),
                    waits=[t_src, tprev], stream="xm1")
                t2 = kb.dma(lambda e, i=i: e.dma_start(
                    out=vrow[0:1, :], in_=cc_sc[0, 0:P]), waits=[t1],
                    stream="xm2")
                t3 = kb.op("vector", lambda e: e.tensor_reduce(
                    out=vrow[0:1, 0:1], in_=vrow[0:1, :], axis=AX.X, op=ALU.max),
                    waits=[t2], strict=True)
                t4 = kb.dma(lambda e, i=i: e.dma_start(
                    out=cc_in[0, i:i + 1], in_=vrow[0:1, 0:1],
                    allow_slow_non_contiguous=True),
                    waits=[t3], stream="xm3")
                toks.append(t4)
                tprev = t4
            t_ag = kb.op("gpsimd", lambda e: e.collective_compute(
                "AllReduce", ALU.max, replica_groups=RG8,
                ins=[cc_in[:, :]], outs=[cc_out[:, :]]), waits=toks)
            outs = []
            tprev2 = t_ag
            for i, (ten, col) in enumerate(dst_cols):
                t5 = kb.dma(lambda e, i=i: e.dma_start(
                    out=vone[0:1, i:i + 1], in_=cc_out[0:1, i:i + 1],
                    allow_slow_non_contiguous=True),
                    waits=[t_ag, tprev2], stream="xm4")
                t6 = kb.op("tensor", lambda e, i=i: e.matmul(
                    out=ps[6][:, 0:1], lhsT=ones1f[0:1, :],
                    rhs=vone[0:1, i:i + 1], start=True, stop=True),
                    waits=[t5, kb.tok("scalar")])
                t7 = kb.op("scalar", lambda e, ten=ten, col=col: e.copy(
                    out=ten[:, col:col + 1], in_=ps[6][:, 0:1]), waits=[t6])
                outs.append(t7)
                tprev2 = t7
            kb.barrier()
            return outs

        p4w = ExitStack()
        dqw = [sb(f"dqw{i}", [P, FI], F32, p4w) for i in range(2)]
        dqq = sb("dqq", [P, FI], F32, p4w)
        dqsc = sb("dqsc", [P, FI // 32], F32, p4w)
        dqrs = sb("dqrs", [P, FI // 32], F32, p4w)
        dqbf = [sb(f"dqbf{i}", [P, FI], BF16, p4w) for i in range(2)]
        trbf = sb("trbf", [P, P], BF16, p4w)
        prevq = {}
        for (src_h, dst_h, O, I, jn) in ((wg_h, st_wg, FI, DM, "g"),
                                         (wu_h, st_wu, FI, DM, "u"),
                                         (wd_h, st_wd, DM, FI, "d")):
            for ot in range(O // P):
                bi = (len(prevq) + ot) % 2 if False else ot % 2
                tl = kb.dma(lambda e, s=src_h, ot=ot, bi=bi, I=I: e.dma_start(
                    out=dqw[bi][:, 0:I], in_=s[P * ot:P * (ot + 1), :]),
                    waits=[prevq.get(("dq", bi))], stream="fdl" + str(bi))
                tq = kb.op("vector", lambda e, bi=bi, I=I: e.tensor_reduce(
                    out=dqsc[:, 0:I // 32],
                    in_=dqw[bi][:, 0:I].rearrange("p (b k) -> p b k", k=32),
                    axis=AX.X, op=ALU.max, apply_absolute_value=True),
                    waits=[tl], strict=True)
                tq = kb.op("gpsimd", lambda e, I=I: e.tensor_scalar(
                    out=dqsc[:, 0:I // 32], in0=dqsc[:, 0:I // 32],
                    scalar1=1.0 / 7.0, scalar2=1e-12, op0=ALU.mult, op1=ALU.max),
                    waits=[tq], strict=True)
                tq2 = kb.op("vector", lambda e, I=I: e.reciprocal(
                    out=dqrs[:, 0:I // 32], in_=dqsc[:, 0:I // 32]),
                    waits=[tq], strict=True)
                tq = kb.op("gpsimd", lambda e, bi=bi, I=I: e.tensor_tensor(
                    out=dqq[:, 0:I], in0=dqw[bi][:, 0:I],
                    in1=dqrs[:, 0:I // 32, None].broadcast_to([P, I // 32, 32]),
                    op=ALU.mult), waits=[tq2, prevq.get("dqq")], strict=True)
                tq = kb.op("gpsimd", lambda e, I=I: e.tensor_scalar(
                    out=dqq[:, 0:I], in0=dqq[:, 0:I], scalar1=MAGIC, scalar2=MAGIC,
                    op0=ALU.add, op1=ALU.subtract), waits=[tq], strict=True)
                tq = kb.op("gpsimd", lambda e, I=I: e.tensor_scalar(
                    out=dqq[:, 0:I], in0=dqq[:, 0:I], scalar1=7.0, scalar2=-8.0,
                    op0=ALU.min, op1=ALU.max), waits=[tq], strict=True)
                tq = kb.op("gpsimd", lambda e, bi=bi, I=I: e.tensor_tensor(
                    out=dqbf[bi][:, 0:I], in0=dqq[:, 0:I],
                    in1=dqsc[:, 0:I // 32, None].broadcast_to([P, I // 32, 32]),
                    op=ALU.mult), waits=[tq, prevq.get(("bf", bi))], strict=True)
                prevq[("dq", bi)] = tq
                prevq["dqq"] = tq
                for kk in range(I // P):
                    tt = kb.op("tensor", lambda e, bi=bi, kk=kk: e.transpose(
                        out=ps[7][:, 0:P // 2].bitcast(BF16),
                        in_=dqbf[bi][:, P * kk:P * (kk + 1)],
                        identity=ident_bf[:]),
                        waits=[tq, prevq.get("tr")])
                    te = kb.op("scalar", lambda e: e.copy(
                        out=trbf[:, :], in_=ps[7][:, 0:P // 2].bitcast(BF16)),
                        waits=[tt, kb.dtok("fds")])
                    prevq["tr"] = te
                    prevq[("bf", bi)] = te
                    kb.dma(lambda e, d=dst_h, ot=ot, kk=kk: e.dma_start(
                        out=d[:, kk, P * ot:P * (ot + 1)], in_=trbf[:, :]),
                        waits=[te], stream="fds")
        kb.barrier()
        p4w.close()

        S1, RS1, GM, UM, SG, SU, RT, RO, TM, OM = range(10)
        t_mx = cross_core_max2([(vmax_c, 0)], cc_mx_in, cc_mx_out,
                               [(scol, S1)], kb.tok("vector"))
        t_s1 = kb.op("vector", lambda e: e.tensor_scalar(
            out=scol[:, S1:S1 + 1], in0=scol[:, S1:S1 + 1],
            scalar1=1.0 / 32767.0, scalar2=1e-12, op0=ALU.mult, op1=ALU.max),
            waits=t_mx, strict=True)
        t_rs1 = kb.op("vector", lambda e: e.reciprocal(
            out=scol[:, RS1:RS1 + 1], in_=scol[:, S1:S1 + 1]),
            waits=[t_s1], strict=True)
        kb.barrier()

        p4 = ExitStack()
        gl_bf = sb("gl_bf", [P, ND, FI], BF16, p4)
        ul_bf = sb("ul_bf", [P, ND, FI], BF16, p4)
        vf4 = sb("vf4", [P, ND, TC], F32, p4)
        kq_bf = sb("kq_bf", [P, ND, TC], BF16, p4)
        kt_f = sb("kt_f", [P, TC], F32, p4)
        fo_bf = sb("fo_bf", [P, NFI, TC], BF16, p4)
        for kk in range(ND):
            t1 = kb.dma(lambda e, kk=kk: e.dma_start(
                out=gl_bf[:, kk, :], in_=st_wg[:, kk, :]), stream="lgw")
            t2 = kb.dma(lambda e, kk=kk: e.dma_start(
                out=ul_bf[:, kk, :], in_=st_wu[:, kk, :]), stream="luw")
        t_wld = [kb.dtok("lgw"), kb.dtok("luw")]
        kb.op("vector", lambda e: e.memset(scol[:, GM:GM + 1], 0.0))
        kb.op("vector", lambda e: e.memset(scol[:, UM:UM + 1], 0.0))
        for c in range(NCH):
            t_lv = kb.dma(lambda e, c=c: e.dma_start(
                out=vf4[:, :, :], in_=st_v[:, :, TC * c:TC * (c + 1)]),
                waits=[prev.get("vwar")], stream="l4v")
            t_k = None
            for dk in range(ND):
                tk = kb.op("vector", lambda e, dk=dk: e.tensor_scalar(
                    out=kt_f[:, :], in0=vf4[:, dk, :],
                    scalar1=scol[:, RS1:RS1 + 1], scalar2=MAGIC,
                    op0=ALU.mult, op1=ALU.add),
                    waits=[t_lv, t_rs1], strict=(dk == 0))
                tk = kb.op("vector", lambda e, dk=dk: e.tensor_scalar(
                    out=kt_f[:, :], in0=kt_f[:, :], scalar1=MAGIC,
                    scalar2=32767.0, op0=ALU.subtract, op1=ALU.min), waits=[tk])
                t_k = kb.op("vector", lambda e, dk=dk: e.tensor_scalar(
                    out=kq_bf[:, dk, :], in0=kt_f[:, :], scalar1=-32768.0,
                    scalar2=None, op0=ALU.max), waits=[tk])
            prev["vwar"] = t_k
            for (wten, sto, mcol, st_nm) in ((gl_bf, st_gate, GM, "s4g"),
                                             (ul_bf, st_up, UM, "s4u")):
                t_fo = None
                fo_war = prev.get("fo_war") or []
                for o in range(NFI):
                    bk = o % 2
                    tm = None
                    for k in range(ND):
                        tm = kb.op("tensor", lambda e, o=o, k=k, bk=bk, w=wten: e.matmul(
                            out=ps[bk][:, :], lhsT=w[:, k, P * o:P * (o + 1)],
                            rhs=kq_bf[:, k, :], start=(k == 0), stop=(k == ND - 1)),
                            waits=[t_k, prev.get(("f4", bk))] + t_wld)
                    t_fo = kb.op("scalar", lambda e, o=o, bk=bk: e.activation(
                        out=fo_bf[:, o, :], in_=ps[bk][:, :], func=AF.Identity,
                        scale=scol[:, S1:S1 + 1]), waits=[tm, t_s1] + fo_war)
                    prev[("f4", bk)] = t_fo
                t_st = kb.dma(lambda e, c=c, sto=sto: e.dma_start(
                    out=sto[:, :, TC * c:TC * (c + 1)], in_=fo_bf[:, :, :]),
                    waits=[t_fo], stream=st_nm)
                t_mr = kb.op("vector", lambda e: e.tensor_reduce(
                    out=fmr[:, 0:1],
                    in_=fo_bf[:, :, :].rearrange("p a b -> p (a b)"),
                    axis=AX.X, op=ALU.max, apply_absolute_value=True),
                    waits=[t_fo], strict=True)
                kb.op("vector", lambda e, mcol=mcol: e.tensor_max(
                    out=scol[:, mcol:mcol + 1], in0=scol[:, mcol:mcol + 1],
                    in1=fmr[:, 0:1]), waits=[t_mr], strict=True)
                prev["fo_war"] = [t_st, t_mr]
            kb.barrier()
        kb.barrier()
        p4.close()

        t_mx = cross_core_max2([(scol, GM), (scol, UM)], cc_m2_in, cc_m2_out,
                               [(scol, GM), (scol, UM)], kb.tok("gpsimd"))
        t_sg = kb.op("vector", lambda e: e.tensor_scalar(
            out=scol[:, SG:SG + 1], in0=scol[:, GM:GM + 1],
            scalar1=1.0 / 32767.0, scalar2=1e-12, op0=ALU.mult, op1=ALU.max),
            waits=t_mx, strict=True)
        t_su = kb.op("vector", lambda e: e.tensor_scalar(
            out=scol[:, SU:SU + 1], in0=scol[:, UM:UM + 1],
            scalar1=1.0 / 32767.0, scalar2=1e-12, op0=ALU.mult, op1=ALU.max),
            waits=t_mx, strict=True)
        t_rg = kb.op("vector", lambda e: e.reciprocal(
            out=scol[:, GM:GM + 1], in_=scol[:, SG:SG + 1]), waits=[t_sg],
            strict=True)
        t_ru = kb.op("vector", lambda e: e.reciprocal(
            out=scol[:, UM:UM + 1], in_=scol[:, SU:SU + 1]), waits=[t_su],
            strict=True)
        kb.barrier()

        p5 = ExitStack()
        gin = sb("gin", [P, NFI, TC], BF16, p5)
        kgf = sb("kgf", [P, NFI, TC], F32, p5)
        sgm = sb("sgm", [P, NFI, TC], F32, p5)
        t2o = sb("t2o", [P, NFI, TC], BF16, p5)
        kb.op("vector", lambda e: e.memset(scol[:, TM:TM + 1], 0.0))
        flatn = lambda ap: ap.rearrange("p a b -> p (a b)")
        for c in range(NCH):
            t_lg = kb.dma(lambda e, c=c: e.dma_start(
                out=gin[:, :, :], in_=st_gate[:, :, TC * c:TC * (c + 1)]),
                waits=[prev.get("p5war")], stream="l5g")
            tk = kb.op("vector", lambda e: e.tensor_scalar(
                out=flatn(kgf[:, :, :]), in0=flatn(gin[:, :, :]),
                scalar1=scol[:, GM:GM + 1], scalar2=MAGIC,
                op0=ALU.mult, op1=ALU.add), waits=[t_lg, t_rg], strict=True)
            tk = kb.op("vector", lambda e: e.tensor_scalar(
                out=flatn(kgf[:, :, :]), in0=flatn(kgf[:, :, :]), scalar1=MAGIC,
                scalar2=32767.0, op0=ALU.subtract, op1=ALU.min), waits=[tk])
            tk = kb.op("vector", lambda e: e.tensor_scalar(
                out=flatn(kgf[:, :, :]), in0=flatn(kgf[:, :, :]),
                scalar1=-32768.0, scalar2=None, op0=ALU.max), waits=[tk])
            t_sig = kb.op("scalar", lambda e: e.activation(
                out=flatn(sgm[:, :, :]), in_=flatn(kgf[:, :, :]), func=AF.Sigmoid,
                scale=scol[:, SG:SG + 1]), waits=[tk, t_sg])
            t_sig = kb.op("vector", lambda e: e.tensor_scalar(
                out=flatn(sgm[:, :, :]), in0=flatn(sgm[:, :, :]),
                scalar1=65536.0, scalar2=MAGIC, op0=ALU.mult, op1=ALU.add),
                waits=[t_sig], strict=True)
            t_sig = kb.op("vector", lambda e: e.tensor_scalar(
                out=flatn(sgm[:, :, :]), in0=flatn(sgm[:, :, :]), scalar1=MAGIC,
                scalar2=65535.0, op0=ALU.subtract, op1=ALU.min), waits=[t_sig])
            t_t2 = kb.op("vector", lambda e: e.tensor_mul(
                out=flatn(t2o[:, :, :]), in0=flatn(kgf[:, :, :]),
                in1=flatn(sgm[:, :, :])), waits=[t_sig])
            t_st = kb.dma(lambda e, c=c: e.dma_start(
                out=st_t2[:, :, TC * c:TC * (c + 1)], in_=t2o[:, :, :]),
                waits=[t_t2], stream="s5t")
            t_mr = kb.op("vector", lambda e: e.tensor_reduce(
                out=fmr[:, 0:1], in_=flatn(t2o[:, :, :]),
                axis=AX.X, op=ALU.max, apply_absolute_value=True),
                waits=[t_t2], strict=True)
            kb.op("vector", lambda e: e.tensor_max(
                out=scol[:, TM:TM + 1], in0=scol[:, TM:TM + 1], in1=fmr[:, 0:1]),
                waits=[t_mr], strict=True)
            t_lu = kb.dma(lambda e, c=c: e.dma_start(
                out=gin[:, :, :], in_=st_up[:, :, TC * c:TC * (c + 1)]),
                waits=[tk, t_mr], stream="l5u")
            tu = kb.op("vector", lambda e: e.tensor_scalar(
                out=flatn(kgf[:, :, :]), in0=flatn(gin[:, :, :]),
                scalar1=scol[:, UM:UM + 1], scalar2=MAGIC,
                op0=ALU.mult, op1=ALU.add), waits=[t_lu, t_ru, t_t2], strict=True)
            tu = kb.op("vector", lambda e: e.tensor_scalar(
                out=flatn(kgf[:, :, :]), in0=flatn(kgf[:, :, :]), scalar1=MAGIC,
                scalar2=32767.0, op0=ALU.subtract, op1=ALU.min), waits=[tu])
            tu = kb.op("vector", lambda e: e.tensor_scalar(
                out=flatn(t2o[:, :, :]), in0=flatn(kgf[:, :, :]),
                scalar1=-32768.0, scalar2=None, op0=ALU.max),
                waits=[tu, t_st, t_mr])
            t_sku = kb.dma(lambda e, c=c: e.dma_start(
                out=st_ku[:, :, TC * c:TC * (c + 1)], in_=t2o[:, :, :]),
                waits=[tu, t_st], stream="s5k")
            prev["p5war"] = t_sku
            kb.barrier()
        kb.barrier()
        p5.close()

        t_mx = cross_core_max2([(scol, TM)], cc_m3_in, cc_m3_out,
                               [(scol, TM)], kb.tok("gpsimd"))
        kb.barrier()
        t_t = kb.op("vector", lambda e: e.tensor_scalar(
            out=scol[:, RT:RT + 1], in0=scol[:, TM:TM + 1],
            scalar1=scol[:, SG:SG + 1], scalar2=2.0 ** -16 / 32767.0,
            op0=ALU.mult, op1=ALU.mult), waits=t_mx, strict=True)
        t_t = kb.op("vector", lambda e: e.tensor_scalar(
            out=scol[:, RT:RT + 1], in0=scol[:, RT:RT + 1], scalar1=1e-12,
            scalar2=None, op0=ALU.max), waits=[t_t], strict=True)
        t_t2s = kb.op("vector", lambda e: e.reciprocal(
            out=fmr[:, 0:1], in_=scol[:, RT:RT + 1]), waits=[t_t], strict=True)
        t_rt = kb.op("vector", lambda e: e.tensor_scalar(
            out=scol[:, TM:TM + 1], in0=fmr[:, 0:1],
            scalar1=scol[:, SG:SG + 1], scalar2=2.0 ** -16,
            op0=ALU.mult, op1=ALU.mult), waits=[t_t2s], strict=True)
        kb.barrier()

        p6 = ExitStack()
        t2i = sb("t2i", [P, NFI, TC], BF16, p6)
        kui = sb("kui", [P, NFI, TC], BF16, p6)
        k2f = sb("k2f", [P, NFI, TC], F32, p6)
        o2o = sb("o2o", [P, NFI, TC], BF16, p6)
        kb.op("vector", lambda e: e.memset(scol[:, OM:OM + 1], 0.0))
        for c in range(NCH):
            t_l1 = kb.dma(lambda e, c=c: e.dma_start(
                out=t2i[:, :, :], in_=st_t2[:, :, TC * c:TC * (c + 1)]),
                waits=[prev.get("p6war")], stream="l6t")
            t_l2 = kb.dma(lambda e, c=c: e.dma_start(
                out=kui[:, :, :], in_=st_ku[:, :, TC * c:TC * (c + 1)]),
                waits=[prev.get("p6war")], stream="l6k")
            tk = kb.op("vector", lambda e: e.tensor_scalar(
                out=flatn(k2f[:, :, :]), in0=flatn(t2i[:, :, :]),
                scalar1=scol[:, TM:TM + 1], scalar2=MAGIC,
                op0=ALU.mult, op1=ALU.add), waits=[t_l1, t_rt], strict=True)
            tk = kb.op("vector", lambda e: e.tensor_scalar(
                out=flatn(k2f[:, :, :]), in0=flatn(k2f[:, :, :]), scalar1=MAGIC,
                scalar2=32767.0, op0=ALU.subtract, op1=ALU.min), waits=[tk])
            tk = kb.op("vector", lambda e: e.tensor_scalar(
                out=flatn(k2f[:, :, :]), in0=flatn(k2f[:, :, :]),
                scalar1=-32768.0, scalar2=None, op0=ALU.max), waits=[tk])
            t_o2 = kb.op("vector", lambda e: e.tensor_mul(
                out=flatn(o2o[:, :, :]), in0=flatn(k2f[:, :, :]),
                in1=flatn(kui[:, :, :])), waits=[tk, t_l2])
            t_st = kb.dma(lambda e, c=c: e.dma_start(
                out=st_o2[:, :, TC * c:TC * (c + 1)], in_=o2o[:, :, :]),
                waits=[t_o2], stream="s6o")
            prev["p6war"] = t_st
            t_mr = kb.op("vector", lambda e: e.tensor_reduce(
                out=fmr[:, 0:1], in_=flatn(o2o[:, :, :]),
                axis=AX.X, op=ALU.max, apply_absolute_value=True),
                waits=[t_o2], strict=True)
            kb.op("vector", lambda e: e.tensor_max(
                out=scol[:, OM:OM + 1], in0=scol[:, OM:OM + 1], in1=fmr[:, 0:1]),
                waits=[t_mr], strict=True)
            kb.barrier()
        kb.barrier()
        p6.close()

        t_mx = cross_core_max2([(scol, OM)], cc_m4_in, cc_m4_out,
                               [(scol, OM)], kb.tok("gpsimd"))
        t_o = kb.op("vector", lambda e: e.tensor_scalar(
            out=scol[:, RO:RO + 1], in0=scol[:, RT:RT + 1],
            scalar1=scol[:, SU:SU + 1], scalar2=None, op0=ALU.mult),
            waits=t_mx, strict=True)
        t_o = kb.op("vector", lambda e: e.tensor_scalar(
            out=scol[:, OM:OM + 1], in0=scol[:, OM:OM + 1],
            scalar1=scol[:, RO:RO + 1], scalar2=1.0 / 32767.0,
            op0=ALU.mult, op1=ALU.mult), waits=[t_o], strict=True)
        t_o = kb.op("vector", lambda e: e.tensor_scalar(
            out=scol[:, OM:OM + 1], in0=scol[:, OM:OM + 1], scalar1=1e-12,
            scalar2=None, op0=ALU.max), waits=[t_o], strict=True)
        t_or = kb.op("vector", lambda e: e.reciprocal(
            out=fmr[:, 0:1], in_=scol[:, OM:OM + 1]), waits=[t_o], strict=True)
        t_ro = kb.op("vector", lambda e: e.tensor_scalar(
            out=scol[:, RO:RO + 1], in0=fmr[:, 0:1],
            scalar1=scol[:, RO:RO + 1], scalar2=None, op0=ALU.mult),
            waits=[t_or], strict=True)
        kb.barrier()

        p7 = ExitStack()
        dl_bf = sb("dl_bf", [P, NFI, DM], BF16, p7)
        o2i = sb("o2i", [P, NFI, TC], BF16, p7)
        kof = sb("kof", [P, TC], F32, p7)
        ko_bf = sb("ko_bf", [P, NFI, TC], BF16, p7)
        ffn_f = sb("ffn_f", [P, ND, TC], F32, p7)
        x1r = sb("x1r", [P, ND, TC], F32, p7)
        out_f = sb("out_f", [P, ND, TC], F32, p7)
        l4b = sb("l4b", [P, ND, TC], BF16, p7)
        l4sq = sb("l4sq", [P, ND, TC], BF16, p7)
        l4st = {k: sb("l4_" + k, [P, TC], F32, p7) for k in ("m", "r", "t1", "t2")}
        for kk in range(NFI):
            kb.dma(lambda e, kk=kk: e.dma_start(
                out=dl_bf[:, kk, :], in_=st_wd[:, kk, :]), stream="ldw")
        t_dld = kb.dtok("ldw")

        def emit_ln4(src_f, t_src):
            t_c = kb.op("vector", lambda e: e.tensor_copy(
                out=l4b[:, :, :], in_=src_f[:, :, :]), waits=[t_src])
            t_s = kb.op("scalar", lambda e: e.activation(
                out=l4sq[:, :, :], in_=l4b[:, :, :], func=AF.Square), waits=[t_c])
            tmm = None
            for dk in range(ND):
                tmm = kb.op("tensor", lambda e, dk=dk: e.matmul(
                    out=ps[4][:, :], lhsT=ones_bf[:], rhs=l4b[:, dk, :],
                    start=(dk == 0), stop=(dk == ND - 1)),
                    waits=[t_c, prev.get(("l4", 4))])
            for dk in range(ND):
                tmm = kb.op("tensor", lambda e, dk=dk: e.matmul(
                    out=ps[5][:, :], lhsT=ones_bf[:], rhs=l4sq[:, dk, :],
                    start=(dk == 0), stop=(dk == ND - 1)),
                    waits=[t_s, prev.get(("l4", 5))])
            t_m = kb.op("scalar", lambda e: e.activation(
                out=l4st["m"][:, :], in_=ps[4][:, :], func=AF.Identity,
                scale=CINV_DM), waits=[tmm])
            t_v = kb.op("scalar", lambda e: e.activation(
                out=l4st["t2"][:, :], in_=ps[5][:, :], func=AF.Identity,
                scale=CINV_DM), waits=[tmm])
            prev[("l4", 4)] = t_m
            prev[("l4", 5)] = t_v
            t_ms = kb.op("scalar", lambda e: e.activation(
                out=l4st["t1"][:, :], in_=l4st["m"][:, :], func=AF.Square),
                waits=[t_m])
            t_var = kb.op("vector", lambda e: e.tensor_sub(
                out=l4st["t2"][:, :], in0=l4st["t2"][:, :], in1=l4st["t1"][:, :]),
                waits=[t_v, t_ms])
            t_sd = kb.op("scalar", lambda e: e.activation(
                out=l4st["t1"][:, :], in_=l4st["t2"][:, :], func=AF.Sqrt,
                bias=CEPS), waits=[t_var])
            return kb.op("vector", lambda e: e.reciprocal(
                out=l4st["r"][:, :], in_=l4st["t1"][:, :]), waits=[t_sd])

        for c in range(NCH):
            t_l1 = kb.dma(lambda e, c=c: e.dma_start(
                out=o2i[:, :, :], in_=st_o2[:, :, TC * c:TC * (c + 1)]),
                waits=[prev.get("p7war")], stream="l7o")
            tk = None
            for o in range(NFI):
                tk = kb.op("vector", lambda e, o=o: e.tensor_scalar(
                    out=kof[:, :], in0=o2i[:, o, :],
                    scalar1=scol[:, RO:RO + 1], scalar2=MAGIC,
                    op0=ALU.mult, op1=ALU.add), waits=[t_l1, t_ro],
                    strict=(o == 0))
                tk = kb.op("vector", lambda e, o=o: e.tensor_scalar(
                    out=kof[:, :], in0=kof[:, :], scalar1=MAGIC,
                    scalar2=32767.0, op0=ALU.subtract, op1=ALU.min), waits=[tk])
                tk = kb.op("vector", lambda e, o=o: e.tensor_scalar(
                    out=ko_bf[:, o, :], in0=kof[:, :],
                    scalar1=-32768.0, scalar2=None, op0=ALU.max), waits=[tk])
            prev["p7war"] = tk
            t_ff = None
            for m in range(ND):
                bk = m % 2
                tm = None
                for k in range(NFI):
                    tm = kb.op("tensor", lambda e, m=m, k=k, bk=bk: e.matmul(
                        out=ps[bk][:, :], lhsT=dl_bf[:, k, P * m:P * (m + 1)],
                        rhs=ko_bf[:, k, :], start=(k == 0), stop=(k == NFI - 1)),
                        waits=[tk, t_dld, prev.get(("p7", bk))])
                t_ff = kb.op("scalar", lambda e, m=m, bk=bk: e.activation(
                    out=ffn_f[:, m, :], in_=ps[bk][:, :], func=AF.Identity,
                    scale=scol[:, OM:OM + 1]), waits=[tm, t_o])
                prev[("p7", bk)] = t_ff
            t_lx1 = kb.dma(lambda e, c=c: e.dma_start(
                out=x1r[:, :, :], in_=st_x1[:, :, TC * c:TC * (c + 1)]),
                waits=[prev.get("p7xwar")], stream="l7x")
            t_r4 = emit_ln4(ffn_f, t_ff)
            t_out = None
            for dk in range(ND):
                ta = kb.op("vector", lambda e, dk=dk: e.tensor_sub(
                    out=l4st["t1"][:, :], in0=ffn_f[:, dk, :], in1=l4st["m"][:, :]),
                    waits=[t_r4])
                ta = kb.op("vector", lambda e, dk=dk: e.tensor_mul(
                    out=l4st["t1"][:, :], in0=l4st["t1"][:, :], in1=l4st["r"][:, :]),
                    waits=[])
                t_out = kb.op("vector", lambda e, dk=dk: e.tensor_add(
                    out=out_f[:, dk, :], in0=l4st["t1"][:, :], in1=x1r[:, dk, :]),
                    waits=[t_lx1])
            prev["p7xwar"] = t_out
            kb.dma(lambda e, c=c: e.dma_start(
                out=out_h[:, :, TC * c:TC * (c + 1)], in_=out_f[:, :, :]),
                waits=[t_out], stream="sout")
            kb.barrier()
        es.callback(p7.close)

    if _DBG == "p4":
        d_scol_h = hbm("d_scol", [P, 24], F32, "ExternalOutput")
        kb.dma(lambda e: e.dma_start(out=d_scol_h[:, :], in_=scol[:, :]),
               stream="dsc")
    if dbg:
        dts = []
        for nm, (oh, src) in dbg.items():
            dts.append(kb.dma(lambda e, o=oh, s=src: e.dma_start(
                out=o[:, :, :], in_=s[:, :, :]), stream="dbg" + nm))
        kb.op("gpsimd", None, waits=dts, inc=0)

    if _PHASES < 5:
        kb.dma(lambda e: e.dma_start(out=out_h[:, 0, 0:1], in_=hm_col[:, 0:1],
                                     allow_slow_non_contiguous=True),
               waits=[kb.tok("vector")], stream="out")
    kb.op("gpsimd", None,
          waits=[(s, cv) for (s, cv) in kb.dsem.values()], inc=0)

    kb.finalize()
    psc.close()
    es.close()
    return nc


def prepare_inputs(inputs):
    x = inputs["L0_attn_x"]
    W_in = inputs["L0_attn_W_in"]
    conv_w = inputs["L0_attn_conv_w"]
    conv_b = inputs["L0_attn_conv_b"]
    W_x = inputs["L0_attn_W_x"]
    W_dt = inputs["L0_attn_W_dt"]
    b_dt = inputs["L0_attn_b_dt"]
    A_log = inputs["L0_attn_A_log"]
    Dp = inputs["L0_attn_Dp"]

    C = np.ascontiguousarray
    blkfull = np.zeros((P, NDI, P), np.float32)
    for jj in range(NDI):
        for p in range(P):
            blkfull[p, jj, 8 * jj + p // NST] = 1.0
    shared = dict(
        w_inT=C(W_in.T),
        w_xTd=C(np.concatenate([W_x.T, Dp[:, None]], 1)),
        w_dtT=C(W_dt.T),
        convw_t=C(conv_w.reshape(NDI, P, 4).transpose(1, 0, 2)),
        convb_t=C(conv_b.reshape(NDI, P).T),
        bdt_t=C(b_dt.reshape(NDI, P).T),
        dp_t=C(Dp.reshape(NDI, P).T),
        alog_cols=C(A_log.reshape(NT, 8, NST).transpose(1, 2, 0).reshape(P, NT)),
        alogT=C(A_log.T),
        w_o1T=C(inputs["L0_attn_W_out1"].T),
        w_o2T=C(inputs["L0_attn_W_out2"].T),
        wg=C(inputs["L0_ffn_w_gate"]), wu=C(inputs["L0_ffn_w_up"]),
        wd=C(inputs["L0_ffn_w_down"]),
        blkfull=C(blkfull.reshape(P, NDI * P)),
        sel8full=C(np.stack(
            [(np.arange(P)[:, None] == 8 * jj + np.arange(P)[None, :] // NST)
             .astype(np.float32) for jj in range(NDI)], axis=1)
            .reshape(P, NDI * P)),
        ident=np.eye(P, dtype=np.float32),
    )
    ins = []
    for c in range(8):
        b, q = c // 4, c % 4
        lo = q * SEG
        halo = x[b, 0:3, :] if q == 0 else x[b, lo - 3:lo, :]
        seg = np.concatenate([halo, x[b, lo:lo + SEG, :]], 0)
        d = dict(shared)
        d["x_seg"] = C(seg.T)
        d["hmask"] = np.full((P, 1), 0.0 if q == 0 else 1.0, np.float32)
        d["mseg8"] = np.tile(np.array([[1.0 if (j // 4 == b and j % 4 < q)
                                        else 0.0 for j in range(8)]], np.float32),
                             (P, 1))
        ins.append(d)
    return ins


def kernel(**inputs):
    ins = prepare_inputs(inputs)
    nc = build_program()
    res = run_bass_kernel_spmd(nc, ins, core_ids=list(range(8)))
    out = np.zeros((2, 16384, 1024), np.float32)
    for c in range(8):
        b, q = c // 4, c % 4
        arr = np.asarray(res.results[c]["xout"], dtype=np.float32)
        out[b, q * SEG:(q + 1) * SEG, :] = arr.transpose(1, 0, 2).reshape(DM, SEG).T
    return out


def run_debug(inputs, trace=False):
    ins = prepare_inputs(inputs)
    nc = build_program()
    return run_bass_kernel_spmd(nc, ins, core_ids=list(range(8)), trace=trace)


# version 02
# speedup vs baseline: 1.2913x; 1.2913x over previous
import numpy as np
from contextlib import ExitStack

import concourse.bass as bass
import concourse.mybir as mybir
from concourse.bass_utils import run_bass_kernel_spmd

F32 = mybir.dt.float32
BF16 = mybir.dt.bfloat16
AF = mybir.ActivationFunctionType
ALU = mybir.AluOpType
AX = mybir.AxisListType

P = 128
SEG = 4096
TH = SEG + 3
TC = 512
NCH = SEG // TC
DM = 1024
ND = DM // P
DI = 2048
NDI = DI // P
E4 = 4096
NST = 16
DTR = 64
NT = DI * NST // P
FI = 3072
NFI = FI // P
MAGIC = 12582912.0
RG8 = [[0, 1, 2, 3, 4, 5, 6, 7]]

_DBG = ""
_PHASES = 5


class KB:
    ENG = ("sync", "vector", "scalar", "tensor", "gpsimd")

    def __init__(self, nc, es):
        self.nc = nc
        self.es = es
        self.sem = {e: es.enter_context(nc.semaphore("sp_" + e)) for e in self.ENG}
        self.cnt = {e: 0 for e in self.ENG}
        self.ops = {e: [] for e in self.ENG}
        self._last_wait = {}
        self.dsem = {}
        self.scope = "p0"

    def op(self, eng, fn, waits=(), inc=1, strict=False):
        ws = []
        if strict:
            ws.append((self.sem[eng], self.cnt[eng]))
        for t in waits:
            if t is None:
                continue
            (s, v) = t
            if s is self.sem[eng]:
                continue
            key = (eng, s.name)
            if self._last_wait.get(key, -1) >= v:
                continue
            self._last_wait[key] = v
            ws.append((s, v))
        self.cnt[eng] += inc
        self.ops[eng].append((fn, tuple(ws), inc, self.scope))
        return (self.sem[eng], self.cnt[eng])

    def dma(self, fn, waits=(), stream="misc", eng="sync"):
        if stream not in self.dsem:
            self.dsem[stream] = [self.es.enter_context(
                self.nc.semaphore("sd_" + stream)), 0]
        sem, cntv = self.dsem[stream]
        self.dsem[stream][1] = cntv + 16
        tok = (sem, cntv + 16)

        def wrapped(e, fn=fn, sem=sem):
            inst = fn(e)
            inst.then_inc(sem, 16)
            return inst
        self.op(eng, wrapped, waits, inc=0)
        return tok

    def dtok(self, stream):
        if stream not in self.dsem:
            return None
        sem, cntv = self.dsem[stream]
        return (sem, cntv)

    def tok(self, eng):
        return (self.sem[eng], self.cnt[eng])

    def toks(self):
        return [self.tok(e) for e in self.ENG]

    def barrier(self):
        toks = self.toks() + [(s, c) for (s, c) in self.dsem.values()]
        for e in self.ENG:
            self.op(e, None, waits=toks, inc=0)

    def finalize(self):
        with self.nc.Block() as block:
            for e in self.ENG:
                getattr(block, e)(self._mk_runner(e))

    def _mk_runner(self, e):
        ops, sem = self.ops[e], self.sem[e]
        nc = self.nc

        def run(eng):
            for fn, waits, inc, scope in ops:
                with nc.named_scope(scope):
                    for (s, v) in waits:
                        eng.wait_ge(s, v)
                    if fn is None:
                        continue
                    inst = fn(eng)
                    if inc:
                        assert inst is not None, f"op on {e} returned None"
                        inst.then_inc(sem, inc)
        return run


def build_program():
    nc = bass.Bass("TRN2", target_bir_lowering=False, debug=False,
                   detect_race_conditions=False)
    es = ExitStack()
    kb = KB(nc, es)

    def hbm(name, shape, dt=F32, kind="Internal"):
        return nc.dram_tensor(name, shape, dt, kind=kind).ap()

    def sb(name, shape, dt=F32, stack=None):
        return (stack or es).enter_context(nc.sbuf_tensor(name, shape, dt))

    x_h = hbm("x_seg", [DM, TH], F32, "ExternalInput")
    w_inT_h = hbm("w_inT", [DM, E4], F32, "ExternalInput")
    w_xTd_h = hbm("w_xTd", [DI, 97], F32, "ExternalInput")
    w_dtT_h = hbm("w_dtT", [DTR, DI], F32, "ExternalInput")
    convw_h = hbm("convw_t", [P, NDI, 4], F32, "ExternalInput")
    convb_h = hbm("convb_t", [P, NDI], F32, "ExternalInput")
    bdt_h = hbm("bdt_t", [P, NDI], F32, "ExternalInput")
    dp_h = hbm("dp_t", [P, NDI], F32, "ExternalInput")
    alogc_h = hbm("alog_cols", [P, NT], F32, "ExternalInput")
    alogT_h = hbm("alogT", [NST, DI], F32, "ExternalInput")
    w_o1T_h = hbm("w_o1T", [E4, DI], F32, "ExternalInput")
    w_o2T_h = hbm("w_o2T", [DI, DM], F32, "ExternalInput")
    wg_h = hbm("wg", [FI, DM], F32, "ExternalInput")
    wu_h = hbm("wu", [FI, DM], F32, "ExternalInput")
    wd_h = hbm("wd", [DM, FI], F32, "ExternalInput")
    blkfull_h = hbm("blkfull", [P, NDI * P], F32, "ExternalInput")
    sel8f_h = hbm("sel8full", [P, NDI * P], F32, "ExternalInput")
    ident_h = hbm("ident", [P, P], F32, "ExternalInput")
    hmask_h = hbm("hmask", [P, 1], F32, "ExternalInput")
    mseg8_h = hbm("mseg8", [P, 8], F32, "ExternalInput")
    out_h = hbm("xout", [P, ND, SEG], F32, "ExternalOutput")

    st_delta = hbm("st_delta", [P, NDI, SEG], BF16)
    st_w = hbm("st_w", [P, NDI, SEG], BF16)
    st_u = hbm("st_u", [P, NDI, SEG], BF16)
    st_zs = hbm("st_zs", [P, NDI, SEG], BF16)
    st_yl = hbm("st_yl", [P, NDI, SEG], BF16)
    st_yc = hbm("st_yc", [P, NDI, SEG], BF16)
    st_x1 = hbm("st_x1", [P, ND, SEG], F32)
    st_v = hbm("st_v", [P, ND, SEG], F32)
    st_h = hbm("st_h", [P, NDI, SEG], BF16)

    st_wg = hbm("st_wg", [P, ND, FI], BF16)
    st_wu = hbm("st_wu", [P, ND, FI], BF16)
    st_wd = hbm("st_wd", [P, NFI, DM], BF16)
    st_gate = hbm("st_gate", [P, NFI, SEG], BF16)
    st_up = hbm("st_up", [P, NFI, SEG], BF16)
    st_ku = hbm("st_ku", [P, NFI, SEG], BF16)
    st_t2 = hbm("st_t2", [P, NFI, SEG], BF16)
    st_o2 = hbm("st_o2", [P, NFI, SEG], BF16)

    cc_rp_in = hbm("cc_rp_in", [1, 2 * P * NT], F32)
    cc_rp_out = hbm("cc_rp_out", [8, 2 * P * NT], F32)
    cc_mx_in = hbm("cc_mx_in", [1, 2], F32)
    cc_mx_out = hbm("cc_mx_out", [1, 2], F32)
    cc_m2_in = hbm("cc_m2_in", [1, 2], F32)
    cc_m2_out = hbm("cc_m2_out", [1, 2], F32)
    cc_m3_in = hbm("cc_m3_in", [1, 2], F32)
    cc_m3_out = hbm("cc_m3_out", [1, 2], F32)
    cc_m4_in = hbm("cc_m4_in", [1, 2], F32)
    cc_m4_out = hbm("cc_m4_out", [1, 2], F32)

    dbg = {}
    if _DBG == "p1":
        for nm, src in (("d_delta", st_delta), ("d_w", st_w),
                        ("d_u", st_u), ("d_zs", st_zs)):
            dbg[nm] = (hbm(nm, [P, NDI, SEG], BF16, "ExternalOutput"), src)
    if _DBG in ("p15", "p2"):
        dbg["d_yl"] = (hbm("d_yl", [P, NDI, SEG], BF16, "ExternalOutput"), st_yl)
    if _DBG == "p2":
        dbg["d_yc"] = (hbm("d_yc", [P, NDI, SEG], BF16, "ExternalOutput"), st_yc)
    if _DBG == "p4":
        dbg["d_gate"] = (hbm("d_gate", [P, NFI, SEG], BF16, "ExternalOutput"), st_gate)
        dbg["d_up"] = (hbm("d_up", [P, NFI, SEG], BF16, "ExternalOutput"), st_up)
        dbg["d_wg"] = (hbm("d_wg", [P, ND, FI], BF16, "ExternalOutput"), st_wg)
        dbg["d_t2"] = (hbm("d_t2", [P, NFI, SEG], BF16, "ExternalOutput"), st_t2)
        dbg["d_o2"] = (hbm("d_o2", [P, NFI, SEG], BF16, "ExternalOutput"), st_o2)
        dbg["d_ku"] = (hbm("d_ku", [P, NFI, SEG], BF16, "ExternalOutput"), st_ku)
        d_sgn_h = hbm("d_sgn", [P, NT], F32, "ExternalOutput")
        d_lab_h = hbm("d_lab", [P, NT], F32, "ExternalOutput")
        d_sin_h = hbm("d_sin", [P, NT], F32, "ExternalOutput")
        d_e2_h = hbm("d_e2", [P, TC], F32, "ExternalOutput")
        d_dbc_h = hbm("d_dbc", [P, TC], BF16, "ExternalOutput")
        d_sgs_h = hbm("d_sgs", [P, P], BF16, "ExternalOutput")

    bcd_bf = sb("bcd_bf", [P, SEG], BF16)
    cdpu_bf = sb("cdpu_bf", [17, SEG], BF16)
    a_cols = sb("a_cols", [P, NT])
    lnabs_c = sb("lnabs_c", [P, NT])
    sign_c = sb("sign_c", [P, NT])
    state_c = sb("state_c", [P, NT])
    rseg_c = sb("rseg_c", [P, NT])
    pseg_c = sb("pseg_c", [P, NT])
    convw = sb("convw", [P, NDI, 4])
    convb = sb("convb", [P, NDI])
    bdt = sb("bdt", [P, NDI])
    dpc = sb("dpc", [P, NDI])
    hm_col = sb("hm_col", [P, 1])
    mseg_col = sb("mseg_col", [P, 8])
    ones_bf = sb("ones_bf", [P, P], BF16)
    blkf_bf = sb("blkf_bf", [P, NDI * P], BF16)
    sel8_bf = sb("sel8_bf", [P, NDI * P], BF16)
    dsum = sb("dsum", [P, NDI])
    dsum_bf = sb("dsum_bf", [P, NDI], BF16)
    dsr = sb("dsr", [P, 1])
    ident_bf = sb("ident_bf", [P, P], BF16)
    lhsT_yt = sb("lhsT_yt", [17, DI], BF16)

    ps = [es.enter_context(nc.psum_tensor(f"ps{i}", [P, TC], F32)) for i in range(8)]

    _cc_store = sb("cc_store", [P, 16])
    _cc_map = {}

    def cc(val):
        if val not in _cc_map:
            i = len(_cc_map)
            assert i < 16
            kb.op("vector", lambda e, i=i, v=val: e.memset(_cc_store[:, i:i + 1], v))
            _cc_map[val] = _cc_store[:, i:i + 1]
        return _cc_map[val]

    CINV_DM = cc(1.0 / DM)
    CEPS = cc(1e-5)
    CNEG1 = cc(-1.0)
    CONE = cc(1.0)

    p1w = ExitStack()
    p0w = ExitStack()
    w_in_bf = sb("w_in_bf", [P, ND, E4], BF16, p1w)
    w_x_bf = sb("w_x_bf", [P, NDI, 97], BF16, p1w)
    w_dt_bf = sb("w_dt_bf", [DTR, DI], BF16, p1w)
    wstg = [sb(f"wstg{i}", [P, E4], F32, p0w) for i in range(2)]

    for (dst, src) in ((convw, convw_h), (convb, convb_h), (bdt, bdt_h),
                       (dpc, dp_h)):
        kb.dma(lambda e, d=dst, s=src: e.dma_start(
            out=d[:, :, :] if len(d.shape) == 3 else d[:, :], in_=s), stream="ini")
    OFF_BLK, OFF_ID, OFF_AT = 0, NDI * P, 0
    kb.dma(lambda e: e.dma_start(out=a_cols[:], in_=alogc_h), stream="ini")
    kb.dma(lambda e: e.dma_start(
        out=wstg[1][0:P, OFF_BLK:OFF_BLK + NDI * P], in_=blkfull_h), stream="ini")
    kb.dma(lambda e: e.dma_start(
        out=wstg[0][0:P, DI:DI + NDI * P], in_=sel8f_h), stream="ini")
    kb.dma(lambda e: e.dma_start(
        out=wstg[1][0:P, OFF_ID:OFF_ID + P], in_=ident_h), stream="ini")
    kb.dma(lambda e: e.dma_start(
        out=wstg[0][0:NST, OFF_AT:OFF_AT + DI], in_=alogT_h), stream="ini")
    kb.dma(lambda e: e.dma_start(out=hm_col[:], in_=hmask_h), stream="ini")
    kb.dma(lambda e: e.dma_start(out=mseg_col[:], in_=mseg8_h), stream="ini")
    t_ini = kb.dtok("ini")
    t_alog = t_blk = t_id = t_alT = t_hm = t_ini

    kb.op("vector", lambda e: e.memset(ones_bf[:], 1.0))
    kb.op("vector", lambda e: e.tensor_copy(
        out=blkf_bf[:], in_=wstg[1][0:P, OFF_BLK:OFF_BLK + NDI * P]), waits=[t_blk])
    kb.op("vector", lambda e: e.tensor_copy(
        out=sel8_bf[:], in_=wstg[0][0:P, DI:DI + NDI * P]), waits=[t_blk])
    kb.op("vector", lambda e: e.memset(dsum[:], 0.0))
    kb.op("vector", lambda e: e.tensor_copy(
        out=ident_bf[:], in_=wstg[1][0:P, OFF_ID:OFF_ID + P]), waits=[t_id])
    t_ac = kb.op("scalar", lambda e: e.activation(
        out=a_cols[:], in_=a_cols[:], func=AF.Exp), waits=[t_alog])
    kb.op("vector", lambda e: e.tensor_scalar(
        out=a_cols[:], in0=a_cols[:], scalar1=-1.0, scalar2=None, op0=ALU.mult),
        waits=[t_ac])
    t_yt = kb.op("scalar", lambda e: e.activation(
        out=wstg[0][0:NST, OFF_AT:OFF_AT + DI],
        in_=wstg[0][0:NST, OFF_AT:OFF_AT + DI], func=AF.Exp), waits=[t_alT])
    kb.op("vector", lambda e: e.memset(lhsT_yt[0:17, :], 1.0))
    kb.op("vector", lambda e: e.tensor_scalar(
        out=lhsT_yt[0:NST, :], in0=wstg[0][0:NST, OFF_AT:OFF_AT + DI],
        scalar1=-1.0, scalar2=None, op0=ALU.mult), waits=[t_yt])
    kb.op("vector", lambda e: e.memset(state_c[:], 0.0))

    tw = []
    for dk in range(ND):
        tl = kb.dma(lambda e, dk=dk: e.dma_start(
            out=wstg[dk % 2][:, :], in_=w_inT_h[P * dk:P * (dk + 1), :]),
            waits=([tw[dk - 2]] if dk >= 2 else [kb.tok("vector")]),
            stream="wA" if dk % 2 == 0 else "wB")
        tw.append(kb.op("vector", lambda e, dk=dk: e.tensor_copy(
            out=w_in_bf[:, dk, :], in_=wstg[dk % 2][:, :]), waits=[tl]))
    t_wx = kb.dma(lambda e: e.dma_start(
        out=wstg[0][:, 0:NDI * 97].rearrange("p (k e) -> p k e", e=97),
        in_=w_xTd_h.rearrange("(k p) e -> p k e", p=P)),
        waits=[tw[-2], kb.tok("vector"), kb.tok("scalar")], stream="wA")
    t_wx = kb.op("vector", lambda e: e.tensor_copy(
        out=w_x_bf[:].rearrange("p a b -> p (a b)"), in_=wstg[0][:, 0:NDI * 97]),
        waits=[t_wx])
    t_wdt = kb.dma(lambda e: e.dma_start(out=wstg[1][0:DTR, 0:DI], in_=w_dtT_h),
                   waits=[tw[-1], kb.tok("vector")], stream="wB")
    t_wdt = kb.op("vector", lambda e: e.tensor_copy(
        out=w_dt_bf[:], in_=wstg[1][0:DTR, 0:DI]), waits=[t_wdt])

    kb.barrier()
    p0w.close()

    p1 = ExitStack()
    x_ld = [sb(f"x_ld{i}", [P, TC], F32, p1) for i in range(2)]
    x_bf = sb("x_bf", [P, ND, TC], BF16, p1)
    xh_bf = sb("xh_bf", [P, ND, TC], BF16, p1)
    stat = {k: sb("sta_" + k, [P, TC], F32, p1) for k in ("m", "rstd", "t1", "t2")}
    xi_bf = sb("xi_bf", [P, NDI, TC + 3], BF16, p1)
    zsdl_bf = sb("zsdl_bf", [P, NDI, TC], BF16, p1)
    u_bf = sb("u_bf", [P, NDI, TC], BF16, p1)
    cacc = sb("cacc", [P, TC], F32, p1)
    tts = sb("tts", [P, TC], F32, p1)
    sps = sb("sps", [P, TC], F32, p1)
    dls = sb("dls", [P, TC], F32, p1)
    dtr_bf = sb("dtr_bf", [DTR, TC], BF16, p1)

    st = {"prev_bank": [None] * 8, "halo_free": None}

    def emit_ln(width, off, deps):
        t_casts = []
        for dk in range(ND):
            tl = kb.dma(lambda e, dk=dk, w=width, o=off: e.dma_start(
                out=x_ld[dk % 2][:, 0:w], in_=x_h[P * dk:P * (dk + 1), o:o + w]),
                waits=deps + ([t_casts[dk - 2]] if dk >= 2 else []),
                stream="x0" if dk % 2 == 0 else "x1")
            t_casts.append(kb.op("vector", lambda e, dk=dk, w=width: e.tensor_copy(
                out=x_bf[:, dk, 0:w], in_=x_ld[dk % 2][:, 0:w]), waits=[tl]))
        t_sq = kb.op("scalar", lambda e, w=width: e.activation(
            out=xh_bf[:, :, 0:w], in_=x_bf[:, :, 0:w], func=AF.Square),
            waits=[t_casts[-1]])
        tmm = None
        for dk in range(ND):
            tmm = kb.op("tensor", lambda e, dk=dk, w=width: e.matmul(
                out=ps[4][:, 0:w], lhsT=ones_bf[:], rhs=x_bf[:, dk, 0:w],
                start=(dk == 0), stop=(dk == ND - 1)),
                waits=[t_casts[dk], st["prev_bank"][4]])
        for dk in range(ND):
            tmm = kb.op("tensor", lambda e, dk=dk, w=width: e.matmul(
                out=ps[5][:, 0:w], lhsT=ones_bf[:], rhs=xh_bf[:, dk, 0:w],
                start=(dk == 0), stop=(dk == ND - 1)),
                waits=[t_sq, st["prev_bank"][5]])
        t_m = kb.op("scalar", lambda e, w=width: e.activation(
            out=stat["m"][:, 0:w], in_=ps[4][:, 0:w], func=AF.Identity, scale=CINV_DM),
            waits=[tmm])
        t_v = kb.op("scalar", lambda e, w=width: e.activation(
            out=stat["t2"][:, 0:w], in_=ps[5][:, 0:w], func=AF.Identity, scale=CINV_DM),
            waits=[tmm])
        st["prev_bank"][4] = t_m
        st["prev_bank"][5] = t_v
        t_ms = kb.op("scalar", lambda e, w=width: e.activation(
            out=stat["t1"][:, 0:w], in_=stat["m"][:, 0:w], func=AF.Square),
            waits=[t_m])
        strict = width < P
        t_var = kb.op("vector", lambda e, w=width: e.tensor_sub(
            out=stat["t2"][:, 0:w], in0=stat["t2"][:, 0:w], in1=stat["t1"][:, 0:w]),
            waits=[t_v, t_ms], strict=strict)
        t_sd = kb.op("scalar", lambda e, w=width: e.activation(
            out=stat["t1"][:, 0:w], in_=stat["t2"][:, 0:w], func=AF.Sqrt, bias=CEPS),
            waits=[t_var])
        t_r = kb.op("vector", lambda e, w=width: e.reciprocal(
            out=stat["rstd"][:, 0:w], in_=stat["t1"][:, 0:w]), waits=[t_sd],
            strict=strict)
        t_app = None
        for dk in range(ND):
            ta = kb.op("vector", lambda e, dk=dk, w=width: e.tensor_sub(
                out=stat["t1"][:, 0:w], in0=x_bf[:, dk, 0:w], in1=stat["m"][:, 0:w]),
                waits=[t_r], strict=strict)
            t_app = kb.op("vector", lambda e, dk=dk, w=width: e.tensor_mul(
                out=xh_bf[:, dk, 0:w], in0=stat["t1"][:, 0:w],
                in1=stat["rstd"][:, 0:w]), waits=[ta], strict=strict)
        return t_app

    def emit_xz(width, t_xh, is_halo, zs_free=None):
        evac = []
        for em in range(32):
            bank = em % 2
            tmm = None
            for dk in range(ND):
                tmm = kb.op("tensor", lambda e, em=em, dk=dk, bank=bank, w=width: e.matmul(
                    out=ps[bank][:, 0:w],
                    lhsT=w_in_bf[:, dk, P * em:P * (em + 1)],
                    rhs=xh_bf[:, dk, 0:w],
                    start=(dk == 0), stop=(dk == ND - 1)),
                    waits=[t_xh, st["prev_bank"][bank]])
            if em < NDI:
                if is_halo:
                    te = kb.op("scalar", lambda e, em=em, bank=bank: e.activation(
                        out=xi_bf[:, em, 0:3], in_=ps[bank][:, 0:3],
                        func=AF.Identity, scale=hm_col[:, 0:1]),
                        waits=[tmm, t_hm])
                else:
                    te = kb.op("scalar", lambda e, em=em, bank=bank, w=width: e.copy(
                        out=xi_bf[:, em, 3:3 + w], in_=ps[bank][:, 0:w]),
                        waits=[tmm, st["halo_free"]])
            else:
                if is_halo:
                    te = tmm
                else:
                    te = kb.op("scalar", lambda e, em=em, bank=bank, w=width: e.activation(
                        out=zsdl_bf[:, em - NDI, 0:w], in_=ps[bank][:, 0:w],
                        func=AF.Silu), waits=[tmm, zs_free])
            st["prev_bank"][bank] = te
            evac.append(te)
        return evac

    kb.scope = 'p1'
    t_xh0 = emit_ln(3, 0, [tw[-1], t_wx, kb.tok("vector")])
    emit_xz(3, t_xh0, True)
    kb.barrier()
    if _DBG == "p1":
        d_xi_h = hbm("d_xi", [P, NDI, 3], BF16, "ExternalOutput")
        d_xh_h = hbm("d_xh", [P, ND, 3], BF16, "ExternalOutput")
        d_xb_h = hbm("d_xb", [P, ND, 3], BF16, "ExternalOutput")
        d_st_h = hbm("d_st", [P, 8], F32, "ExternalOutput")
        kb.dma(lambda e: e.dma_start(out=d_xi_h[:, :, :], in_=xi_bf[:, :, 0:3]),
               stream="dxi")
        kb.dma(lambda e: e.dma_start(out=d_xh_h[:, :, :], in_=xh_bf[:, :, 0:3]),
               stream="dxi")
        kb.dma(lambda e: e.dma_start(out=d_xb_h[:, :, :], in_=x_bf[:, :, 0:3]),
               stream="dxi")
        kb.dma(lambda e: e.dma_start(out=d_st_h[:, 0:3], in_=stat["m"][:, 0:3]),
               stream="dxi")
        kb.dma(lambda e: e.dma_start(out=d_st_h[:, 4:7], in_=stat["rstd"][:, 0:3]),
               stream="dxi")
        kb.barrier()

    prev = {"xdbl": None, "dt0": None, "dt1": None, "zs_dma": None,
            "u_dma": None, "dl_dma": None, "w_dma": None}

    for c in range(NCH):
        off = 3 + TC * c
        t_xh = emit_ln(TC, off, [])
        evac = emit_xz(TC, t_xh, False, zs_free=prev["dl_dma"])
        prev["zs_dma"] = kb.dma(lambda e, c=c: e.dma_start(
            out=st_zs[:, :, TC * c:TC * (c + 1)], in_=zsdl_bf[:, :, :]),
            waits=[evac[-1]], stream="szs")

        t_u = []
        for j in range(NDI):
            tk = kb.op("vector", lambda e, j=j: e.tensor_scalar(
                out=cacc[:, :], in0=xi_bf[:, j, 0:TC],
                scalar1=convw[:, j, 0:1], scalar2=None, op0=ALU.mult),
                waits=[evac[j], st["halo_free"], t_u[-1] if t_u else None])
            for k in range(1, 4):
                tk = kb.op("vector", lambda e, j=j, k=k: e.scalar_tensor_tensor(
                    out=cacc[:, :], in0=xi_bf[:, j, k:k + TC],
                    scalar=convw[:, j, k:k + 1], in1=cacc[:, :],
                    op0=ALU.mult, op1=ALU.add), waits=[tk])
            t_u.append(kb.op("scalar", lambda e, j=j: e.activation(
                out=u_bf[:, j, :], in_=cacc[:, :], func=AF.Silu,
                bias=convb[:, j:j + 1]), waits=[tk, prev["w_dma"]]))
        st["halo_free"] = kb.op("vector", lambda e: e.tensor_copy(
            out=xi_bf[:, :, 0:3], in_=xi_bf[:, :, TC:TC + 3]), waits=[t_u[-1]])
        prev["u_dma"] = kb.dma(lambda e, c=c: e.dma_start(
            out=st_u[:, :, TC * c:TC * (c + 1)], in_=u_bf[:, :, :]),
            waits=[t_u[-1]], stream="su")

        tmm = None
        for k in range(NDI):
            tmm = kb.op("tensor", lambda e, k=k: e.matmul(
                out=ps[6][0:97, :], lhsT=w_x_bf[:, k, :], rhs=u_bf[:, k, :],
                start=(k == 0), stop=(k == NDI - 1)),
                waits=[t_u[k], prev["xdbl"]])
        t_dtr = kb.op("scalar", lambda e: e.copy(
            out=dtr_bf[:, :], in_=ps[6][0:DTR, :]), waits=[tmm])
        t_bc = kb.op("scalar", lambda e, c=c: e.copy(
            out=bcd_bf[0:33, TC * c:TC * (c + 1)], in_=ps[6][DTR:97, :]), waits=[tmm])
        t_cd = kb.dma(lambda e, c=c: e.dma_start(
            out=cdpu_bf[0:17, TC * c:TC * (c + 1)],
            in_=bcd_bf[16:33, TC * c:TC * (c + 1)]), waits=[t_bc], stream="scd")
        prev["xdbl"] = t_bc

        t_dl_all = []
        for m in range(NDI):
            bk = 2 + (m % 2)
            tm = kb.op("tensor", lambda e, m=m, bk=bk: e.matmul(
                out=ps[bk][:, :], lhsT=w_dt_bf[:, P * m:P * (m + 1)],
                rhs=dtr_bf[:, :], start=True, stop=True),
                waits=[t_dtr, t_wdt, prev["dt0"] if bk == 2 else prev["dt1"]])
            te = kb.op("scalar", lambda e, m=m, bk=bk: e.activation(
                out=tts[:, :], in_=ps[bk][:, :], func=AF.Identity,
                bias=bdt[:, m:m + 1]), waits=[tm, kb.tok("vector")])
            if bk == 2:
                prev["dt0"] = te
            else:
                prev["dt1"] = te
            t_sp = kb.op("scalar", lambda e: e.activation(
                out=sps[:, :], in_=tts[:, :], func=AF.Abs), waits=[te])
            t_sp = kb.op("scalar", lambda e: e.activation(
                out=sps[:, :], in_=sps[:, :], func=AF.Exp, scale=CNEG1), waits=[t_sp])
            t_sp = kb.op("scalar", lambda e: e.activation(
                out=sps[:, :], in_=sps[:, :], func=AF.Ln, bias=CONE), waits=[t_sp])
            t_dl = kb.op("vector", lambda e: e.scalar_tensor_tensor(
                out=dls[:, :], in0=tts[:, :], scalar=0.0, in1=sps[:, :],
                op0=ALU.max, op1=ALU.add), waits=[t_sp])
            t_dsr = kb.op("vector", lambda e: e.reduce_sum(
                out=dsr[:, 0:1], in_=dls[:, :], axis=AX.X), waits=[t_dl])
            kb.op("vector", lambda e, m=m: e.tensor_add(
                out=dsum[:, m:m + 1], in0=dsum[:, m:m + 1], in1=dsr[:, 0:1]),
                waits=[t_dsr], strict=True)
            t_w = kb.op("vector", lambda e, m=m: e.tensor_mul(
                out=u_bf[:, m, :], in0=dls[:, :], in1=u_bf[:, m, :]),
                waits=[t_dl, tmm, prev["u_dma"]])
            t_db = kb.op("vector", lambda e, m=m: e.tensor_copy(
                out=zsdl_bf[:, m, :], in_=dls[:, :]),
                waits=[t_dl, prev["zs_dma"]])
            t_dl_all.append((t_w, t_db))
        prev["dl_dma"] = kb.dma(lambda e, c=c: e.dma_start(
            out=st_delta[:, :, TC * c:TC * (c + 1)], in_=zsdl_bf[:, :, :]),
            waits=[t_dl_all[-1][1]], stream="sdl")
        prev["w_dma"] = kb.dma(lambda e, c=c: e.dma_start(
            out=st_w[:, :, TC * c:TC * (c + 1)], in_=u_bf[:, :, :]),
            waits=[t_dl_all[-1][0]], stream="sw")
        kb.barrier()

    kb.scope = 'p15'
    p1.close()
    p1w.close()
    psc = ExitStack()
    if _PHASES >= 2:
        bt_f = sb("bt_f", [P, SEG], BF16, psc)
        ct_f = sb("ct_f", [P, SEG], BF16, psc)
        dgrp = [sb(f"dgrp{i}", [P, SEG], BF16, psc) for i in range(2)]
        wgrp = [sb(f"wgrp{i}", [P, SEG], BF16, psc) for i in range(2)]
        dA_f = [sb(f"dA_f{i}", [P, SEG], F32, psc) for i in range(2)]
        b_f = sb("b_f", [P, SEG], BF16, psc)
        s_f = [sb(f"s_f{i}", [P, SEG], BF16, psc) for i in range(2)]
        p_f = [sb(f"p_f{i}", [P, SEG], BF16, psc) for i in range(4)]
        yl_s = [sb(f"yl_s{i}", [P, SEG], BF16, psc) for i in range(2)]
        drep = sb("drep", [P, NT], F32, psc)
        bias_c = sb("bias_c", [P, NT], F32, psc)
        gtmp = [sb(f"gtmp{i}", [P, NT], F32, psc) for i in range(2)]

        for i8 in range(8):
            kb.dma(lambda e, i8=i8: e.dma_start(
                out=bt_f[16 * i8:16 * (i8 + 1), :], in_=bcd_bf[0:16, :]),
                stream="pbt")
            kb.dma(lambda e, i8=i8: e.dma_start(
                out=ct_f[16 * i8:16 * (i8 + 1), :], in_=bcd_bf[16:32, :]),
                stream="pct")

        def scan_sweep(correction):
            pv = {}
            for g in range(NDI):
                gp = g % 2
                t_ld = kb.dma(lambda e, g=g, gp=gp: e.dma_start(
                    out=dgrp[gp][:, :], in_=st_delta[:, g, :]),
                    waits=[pv.get(("dr", gp))], stream=f"pld{gp}")
                if not correction:
                    t_rhs = kb.dma(lambda e, g=g, gp=gp: e.dma_start(
                        out=wgrp[gp][:, :], in_=st_w[:, g, :]),
                        waits=[pv.get(("wr", gp))], stream=f"plw{gp}")
                else:
                    t_rhs = kb.op("vector", lambda e, gp=gp: e.tensor_tensor_scan(
                        out=wgrp[gp][:, :], data0=dgrp[gp][:, :],
                        data1=dgrp[gp][:, :], initial=0.0,
                        op0=ALU.add, op1=ALU.bypass),
                        waits=[t_ld, pv.get(("wr", gp))])
                    pv[("dr", gp)] = t_rhs
                for q in range(4):
                    for r in range(4):
                        jj = 4 * q + r
                        j = NDI * g + jj
                        pb = jj % 2
                        sfb = jj % 2
                        if not correction:
                            for cc in range(8):
                                cb = cc % 2
                                tmd = kb.op("tensor", lambda e, jj=jj, gp=gp, cc=cc, cb=cb: e.matmul(
                                    out=ps[cb][:, :], lhsT=sel8_bf[:, P * jj:P * (jj + 1)],
                                    rhs=dgrp[gp][:, TC * cc:TC * (cc + 1)],
                                    start=True, stop=True),
                                    waits=[t_ld, pv.get(("db", cb))])
                                t_dA = kb.op("scalar", lambda e, j=j, pb=pb, cc=cc, cb=cb: e.activation(
                                    out=dA_f[pb][:, TC * cc:TC * (cc + 1)], in_=ps[cb][:, :],
                                    func=AF.Exp, scale=a_cols[:, j:j + 1]),
                                    waits=[tmd, pv.get(("dAr", pb))])
                                pv[("db", cb)] = t_dA
                            pv[("dmml", gp)] = tmd
                            for cc in range(8):
                                cb = 2 + cc % 2
                                tmw = kb.op("tensor", lambda e, jj=jj, gp=gp, cc=cc, cb=cb: e.matmul(
                                    out=ps[cb][:, :], lhsT=sel8_bf[:, P * jj:P * (jj + 1)],
                                    rhs=wgrp[gp][:, TC * cc:TC * (cc + 1)],
                                    start=True, stop=True),
                                    waits=[t_rhs, pv.get(("wb", cb))])
                                t_b = kb.op("vector", lambda e, cc=cc, cb=cb: e.tensor_tensor(
                                    out=b_f[:, TC * cc:TC * (cc + 1)], in0=ps[cb][:, :],
                                    in1=bt_f[:, TC * cc:TC * (cc + 1)], op=ALU.mult),
                                    waits=[tmw, kb.dtok("pbt")])
                                pv[("wb", cb)] = t_b
                            pv[("wmml", gp)] = tmw
                            t_scan = kb.op("vector", lambda e, pb=pb, sfb=sfb: e.tensor_tensor_scan(
                                out=s_f[sfb][:, :], data0=dA_f[pb][:, :], data1=b_f[:, :],
                                initial=0.0, op0=ALU.mult, op1=ALU.add),
                                waits=[t_dA])
                            pv[("dAr", pb)] = t_scan
                            t_p = kb.op("vector", lambda e, sfb=sfb, r=r: e.tensor_tensor(
                                out=p_f[r][:, :], in0=s_f[sfb][:, :], in1=ct_f[:, :],
                                op=ALU.mult),
                                waits=[kb.dtok("pct"), pv.get(("pr", r))])
                            kb.op("vector", lambda e, j=j, sfb=sfb: e.tensor_copy(
                                out=state_c[:, j:j + 1], in_=s_f[sfb][:, SEG - 1:SEG]))
                        else:
                            for cc in range(8):
                                cb = cc % 2
                                tmd = kb.op("tensor", lambda e, jj=jj, gp=gp, cc=cc, cb=cb: e.matmul(
                                    out=ps[cb][:, :], lhsT=sel8_bf[:, P * jj:P * (jj + 1)],
                                    rhs=wgrp[gp][:, TC * cc:TC * (cc + 1)],
                                    start=True, stop=True),
                                    waits=[t_rhs, pv.get(("db", cb))])
                                t_E = kb.op("scalar", lambda e, j=j, pb=pb, cc=cc, cb=cb: e.activation(
                                    out=dA_f[pb][:, TC * cc:TC * (cc + 1)], in_=ps[cb][:, :],
                                    func=AF.Exp, scale=a_cols[:, j:j + 1],
                                    bias=lnabs_c[:, j:j + 1]),
                                    waits=[tmd, pv.get(("dAr", pb))])
                                pv[("db", cb)] = t_E
                            pv[("wmml", gp)] = tmd
                            t_p = kb.op("vector", lambda e, j=j, pb=pb, r=r: e.scalar_tensor_tensor(
                                out=p_f[r][:, :], in0=dA_f[pb][:, :],
                                scalar=sign_c[:, j:j + 1], in1=ct_f[:, :],
                                op0=ALU.mult, op1=ALU.mult),
                                waits=[t_E, kb.dtok("pct"), pv.get(("pr", r))])
                            pv[("dAr", pb)] = t_p
                        pv[("tp", r)] = t_p
                    qc = 4 * g + q
                    yb = 4 + qc % 2
                    for cc in range(8):
                        tmy = None
                        for r in range(4):
                            jj = 4 * q + r
                            tmy = kb.op("tensor", lambda e, jj=jj, q=q, r=r, cc=cc, yb=yb: e.matmul(
                                out=ps[yb][0:32, :],
                                lhsT=blkf_bf[:, P * jj + 32 * q:P * jj + 32 * q + 32],
                                rhs=p_f[r][:, TC * cc:TC * (cc + 1)],
                                start=(r == 0), stop=(r == 3), skip_group_check=True),
                                waits=[pv[("tp", r)],
                                       pv.get(("yb", yb)) if r == 0 else None])
                            pv[("pr", r)] = tmy
                        tye = kb.op("scalar", lambda e, q=q, cc=cc, yb=yb, gp=gp: e.activation(
                            out=yl_s[gp][32 * q:32 * q + 32, TC * cc:TC * (cc + 1)],
                            in_=ps[yb][0:32, :], func=AF.Identity),
                            waits=[tmy, pv.get(("ylw", gp))])
                        pv[("yb", yb)] = tye
                        pv[("ye", gp)] = tye
                t_sto = kb.dma(lambda e, g=g, gp=gp, correction=correction: e.dma_start(
                    out=(st_yc if correction else st_yl)[:, g, :],
                    in_=yl_s[gp][:, :]),
                    waits=[pv.get(("ye", gp))], stream=f"pyl{gp}")
                pv[("ylw", gp)] = t_sto
                if not correction:
                    pv[("dr", gp)] = pv[("dmml", gp)]
                pv[("wr", gp)] = pv[("wmml", gp)]

        scan_sweep(correction=False)

    if _PHASES >= 3:
        kb.scope = 'pag'
        t_dsb = kb.op("vector", lambda e: e.tensor_copy(
            out=dsum_bf[:, :], in_=dsum[:, :]), waits=[kb.tok("vector")],
            strict=True)
        for jj in range(NDI):
            tm = kb.op("tensor", lambda e, jj=jj: e.matmul(
                out=ps[6][:, 0:NDI], lhsT=sel8_bf[:, P * jj:P * (jj + 1)],
                rhs=dsum_bf[:, :], start=True, stop=True),
                waits=[t_dsb, kb.tok("scalar")])
            kb.op("scalar", lambda e, jj=jj: e.activation(
                out=drep[:, jj::NDI], in_=ps[6][:, 0:NDI], func=AF.Identity),
                waits=[tm])
        t_ps = kb.op("vector", lambda e: e.tensor_mul(
            out=pseg_c[:, :], in0=a_cols[:, :], in1=drep[:, :]),
            waits=[kb.tok("scalar")], strict=True)
        t_ps = kb.op("scalar", lambda e: e.activation(
            out=pseg_c[:, :], in_=pseg_c[:, :], func=AF.Exp), waits=[t_ps])
        t_c1 = kb.dma(lambda e: e.dma_start(
            out=cc_rp_in[0, 0:P * NT], in_=state_c[:, :]),
            waits=[kb.tok("vector")], stream="cca")
        t_c2 = kb.dma(lambda e: e.dma_start(
            out=cc_rp_in[0, P * NT:2 * P * NT], in_=pseg_c[:, :]),
            waits=[t_ps], stream="cca")
        t_ag = kb.op("gpsimd", lambda e: e.collective_compute(
            "AllGather", ALU.bypass, replica_groups=RG8,
            ins=[cc_rp_in[:, :]], outs=[cc_rp_out[:, :]]),
            waits=[t_c1, t_c2])
        acc = state_c
        rr_s = gtmp[0]
        pr_s = gtmp[1]
        kb.op("vector", lambda e: e.memset(acc[:, :], 0.0), waits=[t_ag])
        for r in range(8):
            t_l1 = kb.dma(lambda e, r=r: e.dma_start(
                out=rr_s[:, :], in_=cc_rp_out[r, 0:P * NT]),
                waits=[t_ag, kb.tok("vector")], stream="ccr")
            t_l2 = kb.dma(lambda e, r=r: e.dma_start(
                out=pr_s[:, :], in_=cc_rp_out[r, P * NT:2 * P * NT]),
                waits=[t_ag, kb.tok("vector")], stream="ccp")
            kb.op("vector", lambda e: e.tensor_mul(
                out=bias_c[:, :], in0=pr_s[:, :], in1=acc[:, :]),
                waits=[t_l1, t_l2], strict=True)
            kb.op("vector", lambda e: e.tensor_add(
                out=bias_c[:, :], in0=bias_c[:, :], in1=rr_s[:, :]), strict=True)
            kb.op("vector", lambda e: e.tensor_sub(
                out=bias_c[:, :], in0=bias_c[:, :], in1=acc[:, :]), strict=True)
            kb.op("vector", lambda e, r=r: e.scalar_tensor_tensor(
                out=acc[:, :], in0=bias_c[:, :], scalar=mseg_col[:, r:r + 1],
                in1=acc[:, :], op0=ALU.mult, op1=ALU.add), strict=True)
        t_ab = kb.op("scalar", lambda e: e.activation(
            out=lnabs_c[:, :], in_=acc[:, :], func=AF.Abs),
            waits=[kb.tok("vector")])
        t_ab = kb.op("vector", lambda e: e.tensor_scalar(
            out=lnabs_c[:, :], in0=lnabs_c[:, :], scalar1=1e-38, scalar2=None,
            op0=ALU.max), waits=[t_ab], strict=True)
        t_ab = kb.op("scalar", lambda e: e.activation(
            out=lnabs_c[:, :], in_=lnabs_c[:, :], func=AF.Ln), waits=[t_ab])
        t_sg = kb.op("vector", lambda e: e.tensor_scalar(
            out=sign_c[:, :], in0=acc[:, :], scalar1=0.0, scalar2=None,
            op0=ALU.is_ge), waits=[kb.tok("scalar")], strict=True)
        t_sg = kb.op("vector", lambda e: e.tensor_scalar(
            out=sign_c[:, :], in0=sign_c[:, :], scalar1=2.0, scalar2=-1.0,
            op0=ALU.mult, op1=ALU.add), waits=[t_sg], strict=True)
        kb.barrier()
        kb.scope = 'p2'
        scan_sweep(correction=True)

    if _PHASES >= 4:
        kb.scope = 'p3'
        kb.barrier()
        psc.close()
        p3 = ExitStack()
        wo2_bf = sb("wo2_bf", [P, NDI, DM], BF16, p3)
        h_bf = sb("h_bf", [P, NDI, TC], BF16, p3)
        p3w = ExitStack()
        st_wo1 = hbm("st_wo1", [P, 32, DI], BF16)
        wo_stg = [sb(f"wo_stg{i}", [P, DI], F32, p3w) for i in range(2)]
        wo_bf = sb("wo_bf", [P, DI], BF16, p3w)
        t_pc = None
        for kk in range(32):
            tl = kb.dma(lambda e, kk=kk: e.dma_start(
                out=wo_stg[kk % 2][:, :], in_=w_o1T_h[P * kk:P * (kk + 1), :]),
                waits=[t_pc], stream="wo" + str(kk % 2))
            t_cst = kb.op("vector", lambda e, kk=kk: e.tensor_copy(
                out=wo_bf[:, :], in_=wo_stg[kk % 2][:, :]), waits=[tl, t_pc])
            t_pc = kb.dma(lambda e, kk=kk: e.dma_start(
                out=st_wo1[:, kk, :], in_=wo_bf[:, :]), waits=[t_cst],
                stream="wo2")
        for kk in range(NDI):
            tl = kb.dma(lambda e, kk=kk: e.dma_start(
                out=wo_stg[kk % 2][:, 0:DM], in_=w_o2T_h[P * kk:P * (kk + 1), :]),
                waits=[kb.tok("vector")], stream="wo" + str(kk % 2))
            kb.op("vector", lambda e, kk=kk: e.tensor_copy(
                out=wo2_bf[:, kk, :], in_=wo_stg[kk % 2][:, 0:DM]), waits=[tl])
        kb.barrier()
        p3w.close()

        p3a = ExitStack()
        yin = {k: sb("yin_" + k, [P, NDI, TC], BF16, p3a)
               for k in ("yl", "yc", "u", "zs")}
        ycat = sb("ycat", [P, 32, TC], BF16, p3a)
        wslab = [sb(f"wslab{i}", [P, 32, P], BF16, p3a) for i in range(2)]
        for c in range(NCH):
            cols = slice(TC * c, TC * (c + 1))
            tl = {}
            for k, srcs in (("yl", st_yl), ("yc", st_yc), ("u", st_u),
                            ("zs", st_zs)):
                tl[k] = kb.dma(lambda e, k=k, srcs=srcs, c=c: e.dma_start(
                    out=yin[k][:, :, :], in_=srcs[:, :, TC * c:TC * (c + 1)]),
                    stream="l3" + k)
            t_yf = None
            for m in range(NDI):
                t1 = kb.op("vector", lambda e, m=m: e.scalar_tensor_tensor(
                    out=ycat[:, m, :], in0=yin["u"][:, m, :],
                    scalar=dpc[:, m:m + 1], in1=yin["yl"][:, m, :],
                    op0=ALU.mult, op1=ALU.add),
                    waits=[tl["u"], tl["yl"]])
                t1 = kb.op("vector", lambda e, m=m: e.tensor_add(
                    out=ycat[:, m, :], in0=ycat[:, m, :], in1=yin["yc"][:, m, :]),
                    waits=[tl["yc"]])
                t_yf = kb.op("vector", lambda e, m=m: e.tensor_mul(
                    out=ycat[:, m, :], in0=ycat[:, m, :], in1=yin["zs"][:, m, :]),
                    waits=[tl["zs"]])
            for m in range(NDI):
                bk = m % 2
                tm = kb.op("tensor", lambda e, m=m, bk=bk, c=c: e.matmul(
                    out=ps[bk][:, :], lhsT=lhsT_yt[:, P * m:P * (m + 1)],
                    rhs=cdpu_bf[0:17, TC * c:TC * (c + 1)],
                    start=True, stop=True),
                    waits=[prev.get(("yt", bk)), kb.dtok("scd")])
                te = kb.op("scalar", lambda e, m=m, bk=bk: e.copy(
                    out=ycat[:, NDI + m, :], in_=ps[bk][:, :]), waits=[tm])
                prev[("yt", bk)] = te
            t_h = None
            for m in range(NDI):
                wb = m % 2
                t_ws = kb.dma(lambda e, m=m, wb=wb: e.dma_start(
                    out=wslab[wb][:, :, :],
                    in_=st_wo1[:, :, P * m:P * (m + 1)]),
                    waits=[prev.get(("ws", wb))], stream="lws" + str(wb))
                bk = 2 + m % 2
                tm = None
                for k in range(32):
                    tm = kb.op("tensor", lambda e, m=m, k=k, bk=bk, wb=wb: e.matmul(
                        out=ps[bk][:, :], lhsT=wslab[wb][:, k, :],
                        rhs=ycat[:, k, :], start=(k == 0), stop=(k == 31)),
                        waits=[t_ws, t_yf, kb.tok("scalar"),
                               prev.get(("h", bk))])
                prev[("ws", wb)] = tm
                t_h = kb.op("scalar", lambda e, m=m, bk=bk: e.activation(
                    out=h_bf[:, m, :], in_=ps[bk][:, :], func=AF.Relu), waits=[tm])
                prev[("h", bk)] = t_h
            kb.dma(lambda e, c=c: e.dma_start(
                out=st_h[:, :, TC * c:TC * (c + 1)], in_=h_bf[:, :, :]),
                waits=[t_h, kb.tok("scalar")], stream="sh")
            kb.barrier()

        kb.barrier()
        p3a.close()
        mix_f = sb("mix_f", [P, ND, TC], F32, p3)
        x1_f = sb("x1_f", [P, ND, TC], F32, p3)
        v_f = sb("v_f", [P, ND, TC], F32, p3)
        xr_f = sb("xr_f", [P, ND, TC], F32, p3)
        lnb = sb("lnb", [P, ND, TC], BF16, p3)
        lnsq = sb("lnsq", [P, ND, TC], BF16, p3)
        lst = {k: sb("l3_" + k, [P, TC], F32, p3) for k in ("m", "r", "t1", "t2")}
        vmax_c = sb("vmax_c", [P, 1], F32, p3)
        vmr = sb("vmr", [P, 1], F32, p3)
        kb.op("vector", lambda e: e.memset(vmax_c[:], 0.0))

        def emit_ln_tail(src_f, width, t_src):
            t_c = kb.op("vector", lambda e, w=width: e.tensor_copy(
                out=lnb[:, :, 0:w], in_=src_f[:, :, 0:w]), waits=[t_src])
            t_s = kb.op("scalar", lambda e, w=width: e.activation(
                out=lnsq[:, :, 0:w], in_=lnb[:, :, 0:w], func=AF.Square),
                waits=[t_c])
            tmm = None
            for dk in range(ND):
                tmm = kb.op("tensor", lambda e, dk=dk, w=width: e.matmul(
                    out=ps[4][:, 0:w], lhsT=ones_bf[:], rhs=lnb[:, dk, 0:w],
                    start=(dk == 0), stop=(dk == ND - 1)),
                    waits=[t_c, prev.get(("ln", 4))])
            for dk in range(ND):
                tmm = kb.op("tensor", lambda e, dk=dk, w=width: e.matmul(
                    out=ps[5][:, 0:w], lhsT=ones_bf[:], rhs=lnsq[:, dk, 0:w],
                    start=(dk == 0), stop=(dk == ND - 1)),
                    waits=[t_s, prev.get(("ln", 5))])
            t_m = kb.op("scalar", lambda e, w=width: e.activation(
                out=lst["m"][:, 0:w], in_=ps[4][:, 0:w], func=AF.Identity,
                scale=CINV_DM), waits=[tmm])
            t_v = kb.op("scalar", lambda e, w=width: e.activation(
                out=lst["t2"][:, 0:w], in_=ps[5][:, 0:w], func=AF.Identity,
                scale=CINV_DM), waits=[tmm])
            prev[("ln", 4)] = t_m
            prev[("ln", 5)] = t_v
            t_ms = kb.op("scalar", lambda e, w=width: e.activation(
                out=lst["t1"][:, 0:w], in_=lst["m"][:, 0:w], func=AF.Square),
                waits=[t_m])
            t_var = kb.op("vector", lambda e, w=width: e.tensor_sub(
                out=lst["t2"][:, 0:w], in0=lst["t2"][:, 0:w], in1=lst["t1"][:, 0:w]),
                waits=[t_v, t_ms])
            t_sd = kb.op("scalar", lambda e, w=width: e.activation(
                out=lst["t1"][:, 0:w], in_=lst["t2"][:, 0:w], func=AF.Sqrt,
                bias=CEPS), waits=[t_var])
            t_r = kb.op("vector", lambda e, w=width: e.reciprocal(
                out=lst["r"][:, 0:w], in_=lst["t1"][:, 0:w]), waits=[t_sd])
            return t_r

        for c in range(NCH):
            t_lh = kb.dma(lambda e, c=c: e.dma_start(
                out=h_bf[:, :, :], in_=st_h[:, :, TC * c:TC * (c + 1)]),
                stream="l3h")
            t_lx = kb.dma(lambda e, c=c: e.dma_start(
                out=xr_f[:, :, :],
                in_=x_h[:, 3 + TC * c:3 + TC * (c + 1)]
                .rearrange("(k p) t -> p k t", p=P)), stream="l3x")
            t_mx = None
            for m in range(ND):
                bk = m % 2
                tm = None
                for k in range(NDI):
                    tm = kb.op("tensor", lambda e, m=m, k=k, bk=bk: e.matmul(
                        out=ps[bk][:, :], lhsT=wo2_bf[:, k, P * m:P * (m + 1)],
                        rhs=h_bf[:, k, :], start=(k == 0), stop=(k == NDI - 1)),
                        waits=[t_lh, prev.get(("mx", bk))])
                t_mx = kb.op("scalar", lambda e, m=m, bk=bk: e.copy(
                    out=mix_f[:, m, :], in_=ps[bk][:, :]), waits=[tm])
                prev[("mx", bk)] = t_mx
            t_r2 = emit_ln_tail(mix_f, TC, t_mx)
            t_x1 = None
            for dk in range(ND):
                ta = kb.op("vector", lambda e, dk=dk: e.tensor_sub(
                    out=lst["t1"][:, :], in0=mix_f[:, dk, :], in1=lst["m"][:, :]),
                    waits=[t_r2])
                ta = kb.op("vector", lambda e, dk=dk: e.tensor_mul(
                    out=lst["t1"][:, :], in0=lst["t1"][:, :], in1=lst["r"][:, :]),
                    waits=[])
                t_x1 = kb.op("vector", lambda e, dk=dk: e.tensor_add(
                    out=x1_f[:, dk, :], in0=lst["t1"][:, :], in1=xr_f[:, dk, :]),
                    waits=[t_lx])
            kb.dma(lambda e, c=c: e.dma_start(
                out=st_x1[:, :, TC * c:TC * (c + 1)], in_=x1_f[:, :, :]),
                waits=[t_x1], stream="sx1")
            t_r3 = emit_ln_tail(x1_f, TC, t_x1)
            t_v3 = None
            for dk in range(ND):
                ta = kb.op("vector", lambda e, dk=dk: e.tensor_sub(
                    out=lst["t1"][:, :], in0=x1_f[:, dk, :], in1=lst["m"][:, :]),
                    waits=[t_r3])
                t_v3 = kb.op("vector", lambda e, dk=dk: e.tensor_mul(
                    out=v_f[:, dk, :], in0=lst["t1"][:, :], in1=lst["r"][:, :]),
                    waits=[])
            kb.dma(lambda e, c=c: e.dma_start(
                out=st_v[:, :, TC * c:TC * (c + 1)], in_=v_f[:, :, :]),
                waits=[t_v3], stream="sv")
            t_vm = kb.op("vector", lambda e: e.tensor_reduce(
                out=vmr[:, 0:1], in_=v_f[:, :, :].rearrange("p a b -> p (a b)"),
                axis=AX.X, op=ALU.max, apply_absolute_value=True),
                waits=[t_v3], strict=True)
            kb.op("vector", lambda e: e.tensor_max(
                out=vmax_c[:, 0:1], in0=vmax_c[:, 0:1], in1=vmr[:, 0:1]),
                waits=[t_vm], strict=True)
            kb.barrier()

        if _DBG == "p3":
            dbg["d_x1"] = (hbm("d_x1", [P, ND, SEG], F32, "ExternalOutput"), st_x1)
            dbg["d_v"] = (hbm("d_v", [P, ND, SEG], F32, "ExternalOutput"), st_v)
        es.callback(p3.close)

    if _PHASES >= 5:
        kb.scope = 'ffnw'
        kb.barrier()
        p3.close()
        scol = sb("scol", [P, 24])
        vrow = sb("vrow", [1, P])
        fmr = sb("fmr", [P, 1])
        cc_sc = hbm("cc_sc", [1, P], F32)

        vone = sb("vone", [1, 2])
        ones1f = sb("ones1f", [1, P])
        kb.op("vector", lambda e: e.memset(ones1f[:], 1.0))

        def cross_core_max2(src_cols, cc_in, cc_out, dst_cols, t_src):
            toks = []
            tprev = t_src
            for i, (ten, col) in enumerate(src_cols):
                t1 = kb.dma(lambda e, ten=ten, col=col, i=i: e.dma_start(
                    out=cc_sc[0, 0:P], in_=ten[:, col:col + 1]),
                    waits=[t_src, tprev], stream="xm1")
                t2 = kb.dma(lambda e, i=i: e.dma_start(
                    out=vrow[0:1, :], in_=cc_sc[0, 0:P]), waits=[t1],
                    stream="xm2")
                t3 = kb.op("vector", lambda e: e.tensor_reduce(
                    out=vrow[0:1, 0:1], in_=vrow[0:1, :], axis=AX.X, op=ALU.max),
                    waits=[t2], strict=True)
                t4 = kb.dma(lambda e, i=i: e.dma_start(
                    out=cc_in[0, i:i + 1], in_=vrow[0:1, 0:1],
                    allow_slow_non_contiguous=True),
                    waits=[t3], stream="xm3")
                toks.append(t4)
                tprev = t4
            t_ag = kb.op("gpsimd", lambda e: e.collective_compute(
                "AllReduce", ALU.max, replica_groups=RG8,
                ins=[cc_in[:, :]], outs=[cc_out[:, :]]), waits=toks)
            outs = []
            tprev2 = t_ag
            for i, (ten, col) in enumerate(dst_cols):
                t5 = kb.dma(lambda e, i=i: e.dma_start(
                    out=vone[0:1, i:i + 1], in_=cc_out[0:1, i:i + 1],
                    allow_slow_non_contiguous=True),
                    waits=[t_ag, tprev2], stream="xm4")
                t6 = kb.op("tensor", lambda e, i=i: e.matmul(
                    out=ps[6][:, 0:1], lhsT=ones1f[0:1, :],
                    rhs=vone[0:1, i:i + 1], start=True, stop=True),
                    waits=[t5, kb.tok("scalar")])
                t7 = kb.op("scalar", lambda e, ten=ten, col=col: e.copy(
                    out=ten[:, col:col + 1], in_=ps[6][:, 0:1]), waits=[t6])
                outs.append(t7)
                tprev2 = t7
            kb.barrier()
            return outs

        p4w = ExitStack()
        dqw = [sb(f"dqw{i}", [P, FI], F32, p4w) for i in range(2)]
        dqq = sb("dqq", [P, FI], F32, p4w)
        dqsc = sb("dqsc", [P, FI // 32], F32, p4w)
        dqrs = sb("dqrs", [P, FI // 32], F32, p4w)
        dqbf = [sb(f"dqbf{i}", [P, FI], BF16, p4w) for i in range(2)]
        trbf = sb("trbf", [P, P], BF16, p4w)
        prevq = {}
        for (src_h, dst_h, O, I, jn) in ((wg_h, st_wg, FI, DM, "g"),
                                         (wu_h, st_wu, FI, DM, "u"),
                                         (wd_h, st_wd, DM, FI, "d")):
            for ot in range(O // P):
                bi = (len(prevq) + ot) % 2 if False else ot % 2
                tl = kb.dma(lambda e, s=src_h, ot=ot, bi=bi, I=I: e.dma_start(
                    out=dqw[bi][:, 0:I], in_=s[P * ot:P * (ot + 1), :]),
                    waits=[prevq.get(("dq", bi))], stream="fdl" + str(bi))
                tq = kb.op("vector", lambda e, bi=bi, I=I: e.tensor_reduce(
                    out=dqsc[:, 0:I // 32],
                    in_=dqw[bi][:, 0:I].rearrange("p (b k) -> p b k", k=32),
                    axis=AX.X, op=ALU.max, apply_absolute_value=True),
                    waits=[tl], strict=True)
                tq = kb.op("gpsimd", lambda e, I=I: e.tensor_scalar(
                    out=dqsc[:, 0:I // 32], in0=dqsc[:, 0:I // 32],
                    scalar1=1.0 / 7.0, scalar2=1e-12, op0=ALU.mult, op1=ALU.max),
                    waits=[tq], strict=True)
                tq2 = kb.op("vector", lambda e, I=I: e.reciprocal(
                    out=dqrs[:, 0:I // 32], in_=dqsc[:, 0:I // 32]),
                    waits=[tq], strict=True)
                tq = kb.op("gpsimd", lambda e, bi=bi, I=I: e.tensor_tensor(
                    out=dqq[:, 0:I], in0=dqw[bi][:, 0:I],
                    in1=dqrs[:, 0:I // 32, None].broadcast_to([P, I // 32, 32]),
                    op=ALU.mult), waits=[tq2, prevq.get("dqq")], strict=True)
                tq = kb.op("gpsimd", lambda e, I=I: e.tensor_scalar(
                    out=dqq[:, 0:I], in0=dqq[:, 0:I], scalar1=MAGIC, scalar2=MAGIC,
                    op0=ALU.add, op1=ALU.subtract), waits=[tq], strict=True)
                tq = kb.op("gpsimd", lambda e, I=I: e.tensor_scalar(
                    out=dqq[:, 0:I], in0=dqq[:, 0:I], scalar1=7.0, scalar2=-8.0,
                    op0=ALU.min, op1=ALU.max), waits=[tq], strict=True)
                tq = kb.op("gpsimd", lambda e, bi=bi, I=I: e.tensor_tensor(
                    out=dqbf[bi][:, 0:I], in0=dqq[:, 0:I],
                    in1=dqsc[:, 0:I // 32, None].broadcast_to([P, I // 32, 32]),
                    op=ALU.mult), waits=[tq, prevq.get(("bf", bi))], strict=True)
                prevq[("dq", bi)] = tq
                prevq["dqq"] = tq
                for kk in range(I // P):
                    tt = kb.op("tensor", lambda e, bi=bi, kk=kk: e.transpose(
                        out=ps[7][:, 0:P // 2].bitcast(BF16),
                        in_=dqbf[bi][:, P * kk:P * (kk + 1)],
                        identity=ident_bf[:]),
                        waits=[tq, prevq.get("tr")])
                    te = kb.op("scalar", lambda e: e.copy(
                        out=trbf[:, :], in_=ps[7][:, 0:P // 2].bitcast(BF16)),
                        waits=[tt, kb.dtok("fds")])
                    prevq["tr"] = te
                    prevq[("bf", bi)] = te
                    kb.dma(lambda e, d=dst_h, ot=ot, kk=kk: e.dma_start(
                        out=d[:, kk, P * ot:P * (ot + 1)], in_=trbf[:, :]),
                        waits=[te], stream="fds")
        kb.barrier()
        p4w.close()

        kb.scope = 'p4'
        S1, RS1, GM, UM, SG, SU, RT, RO, TM, OM = range(10)
        t_mx = cross_core_max2([(vmax_c, 0)], cc_mx_in, cc_mx_out,
                               [(scol, S1)], kb.tok("vector"))
        t_s1 = kb.op("vector", lambda e: e.tensor_scalar(
            out=scol[:, S1:S1 + 1], in0=scol[:, S1:S1 + 1],
            scalar1=1.0 / 32767.0, scalar2=1e-12, op0=ALU.mult, op1=ALU.max),
            waits=t_mx, strict=True)
        t_rs1 = kb.op("vector", lambda e: e.reciprocal(
            out=scol[:, RS1:RS1 + 1], in_=scol[:, S1:S1 + 1]),
            waits=[t_s1], strict=True)
        kb.barrier()

        p4 = ExitStack()
        gl_bf = sb("gl_bf", [P, ND, FI], BF16, p4)
        ul_bf = sb("ul_bf", [P, ND, FI], BF16, p4)
        vf4 = sb("vf4", [P, ND, TC], F32, p4)
        kq_bf = sb("kq_bf", [P, ND, TC], BF16, p4)
        kt_f = sb("kt_f", [P, TC], F32, p4)
        fo_bf = sb("fo_bf", [P, NFI, TC], BF16, p4)
        for kk in range(ND):
            t1 = kb.dma(lambda e, kk=kk: e.dma_start(
                out=gl_bf[:, kk, :], in_=st_wg[:, kk, :]), stream="lgw")
            t2 = kb.dma(lambda e, kk=kk: e.dma_start(
                out=ul_bf[:, kk, :], in_=st_wu[:, kk, :]), stream="luw")
        t_wld = [kb.dtok("lgw"), kb.dtok("luw")]
        kb.op("vector", lambda e: e.memset(scol[:, GM:GM + 1], 0.0))
        kb.op("vector", lambda e: e.memset(scol[:, UM:UM + 1], 0.0))
        for c in range(NCH):
            t_lv = kb.dma(lambda e, c=c: e.dma_start(
                out=vf4[:, :, :], in_=st_v[:, :, TC * c:TC * (c + 1)]),
                waits=[prev.get("vwar")], stream="l4v")
            t_k = None
            for dk in range(ND):
                tk = kb.op("vector", lambda e, dk=dk: e.tensor_scalar(
                    out=kt_f[:, :], in0=vf4[:, dk, :],
                    scalar1=scol[:, RS1:RS1 + 1], scalar2=MAGIC,
                    op0=ALU.mult, op1=ALU.add),
                    waits=[t_lv, t_rs1], strict=(dk == 0))
                tk = kb.op("vector", lambda e, dk=dk: e.tensor_scalar(
                    out=kt_f[:, :], in0=kt_f[:, :], scalar1=MAGIC,
                    scalar2=32767.0, op0=ALU.subtract, op1=ALU.min), waits=[tk])
                t_k = kb.op("vector", lambda e, dk=dk: e.tensor_scalar(
                    out=kq_bf[:, dk, :], in0=kt_f[:, :], scalar1=-32768.0,
                    scalar2=None, op0=ALU.max), waits=[tk])
            prev["vwar"] = t_k
            for (wten, sto, mcol, st_nm) in ((gl_bf, st_gate, GM, "s4g"),
                                             (ul_bf, st_up, UM, "s4u")):
                t_fo = None
                fo_war = prev.get("fo_war") or []
                for o in range(NFI):
                    bk = o % 2
                    tm = None
                    for k in range(ND):
                        tm = kb.op("tensor", lambda e, o=o, k=k, bk=bk, w=wten: e.matmul(
                            out=ps[bk][:, :], lhsT=w[:, k, P * o:P * (o + 1)],
                            rhs=kq_bf[:, k, :], start=(k == 0), stop=(k == ND - 1)),
                            waits=[t_k, prev.get(("f4", bk))] + t_wld)
                    t_fo = kb.op("scalar", lambda e, o=o, bk=bk: e.activation(
                        out=fo_bf[:, o, :], in_=ps[bk][:, :], func=AF.Identity,
                        scale=scol[:, S1:S1 + 1]), waits=[tm, t_s1] + fo_war)
                    prev[("f4", bk)] = t_fo
                t_st = kb.dma(lambda e, c=c, sto=sto: e.dma_start(
                    out=sto[:, :, TC * c:TC * (c + 1)], in_=fo_bf[:, :, :]),
                    waits=[t_fo], stream=st_nm)
                t_mr = kb.op("vector", lambda e: e.tensor_reduce(
                    out=fmr[:, 0:1],
                    in_=fo_bf[:, :, :].rearrange("p a b -> p (a b)"),
                    axis=AX.X, op=ALU.max, apply_absolute_value=True),
                    waits=[t_fo], strict=True)
                kb.op("vector", lambda e, mcol=mcol: e.tensor_max(
                    out=scol[:, mcol:mcol + 1], in0=scol[:, mcol:mcol + 1],
                    in1=fmr[:, 0:1]), waits=[t_mr], strict=True)
                prev["fo_war"] = [t_st, t_mr]
            kb.barrier()
        kb.barrier()
        p4.close()

        kb.scope = 'p5'
        t_mx = cross_core_max2([(scol, GM), (scol, UM)], cc_m2_in, cc_m2_out,
                               [(scol, GM), (scol, UM)], kb.tok("gpsimd"))
        t_sg = kb.op("vector", lambda e: e.tensor_scalar(
            out=scol[:, SG:SG + 1], in0=scol[:, GM:GM + 1],
            scalar1=1.0 / 32767.0, scalar2=1e-12, op0=ALU.mult, op1=ALU.max),
            waits=t_mx, strict=True)
        t_su = kb.op("vector", lambda e: e.tensor_scalar(
            out=scol[:, SU:SU + 1], in0=scol[:, UM:UM + 1],
            scalar1=1.0 / 32767.0, scalar2=1e-12, op0=ALU.mult, op1=ALU.max),
            waits=t_mx, strict=True)
        t_rg = kb.op("vector", lambda e: e.reciprocal(
            out=scol[:, GM:GM + 1], in_=scol[:, SG:SG + 1]), waits=[t_sg],
            strict=True)
        t_ru = kb.op("vector", lambda e: e.reciprocal(
            out=scol[:, UM:UM + 1], in_=scol[:, SU:SU + 1]), waits=[t_su],
            strict=True)
        kb.barrier()

        p5 = ExitStack()
        gin = sb("gin", [P, NFI, TC], BF16, p5)
        kgf = sb("kgf", [P, NFI, TC], F32, p5)
        sgm = sb("sgm", [P, NFI, TC], F32, p5)
        t2o = sb("t2o", [P, NFI, TC], BF16, p5)
        kb.op("vector", lambda e: e.memset(scol[:, TM:TM + 1], 0.0))
        flatn = lambda ap: ap.rearrange("p a b -> p (a b)")
        for c in range(NCH):
            t_lg = kb.dma(lambda e, c=c: e.dma_start(
                out=gin[:, :, :], in_=st_gate[:, :, TC * c:TC * (c + 1)]),
                waits=[prev.get("p5war")], stream="l5g")
            tk = kb.op("vector", lambda e: e.tensor_scalar(
                out=flatn(kgf[:, :, :]), in0=flatn(gin[:, :, :]),
                scalar1=scol[:, GM:GM + 1], scalar2=MAGIC,
                op0=ALU.mult, op1=ALU.add), waits=[t_lg, t_rg], strict=True)
            tk = kb.op("vector", lambda e: e.tensor_scalar(
                out=flatn(kgf[:, :, :]), in0=flatn(kgf[:, :, :]), scalar1=MAGIC,
                scalar2=32767.0, op0=ALU.subtract, op1=ALU.min), waits=[tk])
            tk = kb.op("vector", lambda e: e.tensor_scalar(
                out=flatn(kgf[:, :, :]), in0=flatn(kgf[:, :, :]),
                scalar1=-32768.0, scalar2=None, op0=ALU.max), waits=[tk])
            t_sig = kb.op("scalar", lambda e: e.activation(
                out=flatn(sgm[:, :, :]), in_=flatn(kgf[:, :, :]), func=AF.Sigmoid,
                scale=scol[:, SG:SG + 1]), waits=[tk, t_sg])
            t_sig = kb.op("vector", lambda e: e.tensor_scalar(
                out=flatn(sgm[:, :, :]), in0=flatn(sgm[:, :, :]),
                scalar1=65536.0, scalar2=MAGIC, op0=ALU.mult, op1=ALU.add),
                waits=[t_sig], strict=True)
            t_sig = kb.op("vector", lambda e: e.tensor_scalar(
                out=flatn(sgm[:, :, :]), in0=flatn(sgm[:, :, :]), scalar1=MAGIC,
                scalar2=65535.0, op0=ALU.subtract, op1=ALU.min), waits=[t_sig])
            t_t2 = kb.op("vector", lambda e: e.tensor_mul(
                out=flatn(t2o[:, :, :]), in0=flatn(kgf[:, :, :]),
                in1=flatn(sgm[:, :, :])), waits=[t_sig])
            t_st = kb.dma(lambda e, c=c: e.dma_start(
                out=st_t2[:, :, TC * c:TC * (c + 1)], in_=t2o[:, :, :]),
                waits=[t_t2], stream="s5t")
            t_mr = kb.op("vector", lambda e: e.tensor_reduce(
                out=fmr[:, 0:1], in_=flatn(t2o[:, :, :]),
                axis=AX.X, op=ALU.max, apply_absolute_value=True),
                waits=[t_t2], strict=True)
            kb.op("vector", lambda e: e.tensor_max(
                out=scol[:, TM:TM + 1], in0=scol[:, TM:TM + 1], in1=fmr[:, 0:1]),
                waits=[t_mr], strict=True)
            t_lu = kb.dma(lambda e, c=c: e.dma_start(
                out=gin[:, :, :], in_=st_up[:, :, TC * c:TC * (c + 1)]),
                waits=[tk, t_mr], stream="l5u")
            tu = kb.op("vector", lambda e: e.tensor_scalar(
                out=flatn(kgf[:, :, :]), in0=flatn(gin[:, :, :]),
                scalar1=scol[:, UM:UM + 1], scalar2=MAGIC,
                op0=ALU.mult, op1=ALU.add), waits=[t_lu, t_ru, t_t2], strict=True)
            tu = kb.op("vector", lambda e: e.tensor_scalar(
                out=flatn(kgf[:, :, :]), in0=flatn(kgf[:, :, :]), scalar1=MAGIC,
                scalar2=32767.0, op0=ALU.subtract, op1=ALU.min), waits=[tu])
            tu = kb.op("vector", lambda e: e.tensor_scalar(
                out=flatn(t2o[:, :, :]), in0=flatn(kgf[:, :, :]),
                scalar1=-32768.0, scalar2=None, op0=ALU.max),
                waits=[tu, t_st, t_mr])
            t_sku = kb.dma(lambda e, c=c: e.dma_start(
                out=st_ku[:, :, TC * c:TC * (c + 1)], in_=t2o[:, :, :]),
                waits=[tu, t_st], stream="s5k")
            prev["p5war"] = t_sku
            kb.barrier()
        kb.barrier()
        p5.close()

        kb.scope = 'p6'
        t_mx = cross_core_max2([(scol, TM)], cc_m3_in, cc_m3_out,
                               [(scol, TM)], kb.tok("gpsimd"))
        kb.barrier()
        t_t = kb.op("vector", lambda e: e.tensor_scalar(
            out=scol[:, RT:RT + 1], in0=scol[:, TM:TM + 1],
            scalar1=scol[:, SG:SG + 1], scalar2=2.0 ** -16 / 32767.0,
            op0=ALU.mult, op1=ALU.mult), waits=t_mx, strict=True)
        t_t = kb.op("vector", lambda e: e.tensor_scalar(
            out=scol[:, RT:RT + 1], in0=scol[:, RT:RT + 1], scalar1=1e-12,
            scalar2=None, op0=ALU.max), waits=[t_t], strict=True)
        t_t2s = kb.op("vector", lambda e: e.reciprocal(
            out=fmr[:, 0:1], in_=scol[:, RT:RT + 1]), waits=[t_t], strict=True)
        t_rt = kb.op("vector", lambda e: e.tensor_scalar(
            out=scol[:, TM:TM + 1], in0=fmr[:, 0:1],
            scalar1=scol[:, SG:SG + 1], scalar2=2.0 ** -16,
            op0=ALU.mult, op1=ALU.mult), waits=[t_t2s], strict=True)
        kb.barrier()

        p6 = ExitStack()
        t2i = sb("t2i", [P, NFI, TC], BF16, p6)
        kui = sb("kui", [P, NFI, TC], BF16, p6)
        k2f = sb("k2f", [P, NFI, TC], F32, p6)
        o2o = sb("o2o", [P, NFI, TC], BF16, p6)
        kb.op("vector", lambda e: e.memset(scol[:, OM:OM + 1], 0.0))
        for c in range(NCH):
            t_l1 = kb.dma(lambda e, c=c: e.dma_start(
                out=t2i[:, :, :], in_=st_t2[:, :, TC * c:TC * (c + 1)]),
                waits=[prev.get("p6war")], stream="l6t")
            t_l2 = kb.dma(lambda e, c=c: e.dma_start(
                out=kui[:, :, :], in_=st_ku[:, :, TC * c:TC * (c + 1)]),
                waits=[prev.get("p6war")], stream="l6k")
            tk = kb.op("vector", lambda e: e.tensor_scalar(
                out=flatn(k2f[:, :, :]), in0=flatn(t2i[:, :, :]),
                scalar1=scol[:, TM:TM + 1], scalar2=MAGIC,
                op0=ALU.mult, op1=ALU.add), waits=[t_l1, t_rt], strict=True)
            tk = kb.op("vector", lambda e: e.tensor_scalar(
                out=flatn(k2f[:, :, :]), in0=flatn(k2f[:, :, :]), scalar1=MAGIC,
                scalar2=32767.0, op0=ALU.subtract, op1=ALU.min), waits=[tk])
            tk = kb.op("vector", lambda e: e.tensor_scalar(
                out=flatn(k2f[:, :, :]), in0=flatn(k2f[:, :, :]),
                scalar1=-32768.0, scalar2=None, op0=ALU.max), waits=[tk])
            t_o2 = kb.op("vector", lambda e: e.tensor_mul(
                out=flatn(o2o[:, :, :]), in0=flatn(k2f[:, :, :]),
                in1=flatn(kui[:, :, :])), waits=[tk, t_l2])
            t_st = kb.dma(lambda e, c=c: e.dma_start(
                out=st_o2[:, :, TC * c:TC * (c + 1)], in_=o2o[:, :, :]),
                waits=[t_o2], stream="s6o")
            prev["p6war"] = t_st
            t_mr = kb.op("vector", lambda e: e.tensor_reduce(
                out=fmr[:, 0:1], in_=flatn(o2o[:, :, :]),
                axis=AX.X, op=ALU.max, apply_absolute_value=True),
                waits=[t_o2], strict=True)
            kb.op("vector", lambda e: e.tensor_max(
                out=scol[:, OM:OM + 1], in0=scol[:, OM:OM + 1], in1=fmr[:, 0:1]),
                waits=[t_mr], strict=True)
            kb.barrier()
        kb.barrier()
        p6.close()

        kb.scope = 'p7'
        t_mx = cross_core_max2([(scol, OM)], cc_m4_in, cc_m4_out,
                               [(scol, OM)], kb.tok("gpsimd"))
        t_o = kb.op("vector", lambda e: e.tensor_scalar(
            out=scol[:, RO:RO + 1], in0=scol[:, RT:RT + 1],
            scalar1=scol[:, SU:SU + 1], scalar2=None, op0=ALU.mult),
            waits=t_mx, strict=True)
        t_o = kb.op("vector", lambda e: e.tensor_scalar(
            out=scol[:, OM:OM + 1], in0=scol[:, OM:OM + 1],
            scalar1=scol[:, RO:RO + 1], scalar2=1.0 / 32767.0,
            op0=ALU.mult, op1=ALU.mult), waits=[t_o], strict=True)
        t_o = kb.op("vector", lambda e: e.tensor_scalar(
            out=scol[:, OM:OM + 1], in0=scol[:, OM:OM + 1], scalar1=1e-12,
            scalar2=None, op0=ALU.max), waits=[t_o], strict=True)
        t_or = kb.op("vector", lambda e: e.reciprocal(
            out=fmr[:, 0:1], in_=scol[:, OM:OM + 1]), waits=[t_o], strict=True)
        t_ro = kb.op("vector", lambda e: e.tensor_scalar(
            out=scol[:, RO:RO + 1], in0=fmr[:, 0:1],
            scalar1=scol[:, RO:RO + 1], scalar2=None, op0=ALU.mult),
            waits=[t_or], strict=True)
        kb.barrier()

        p7 = ExitStack()
        dl_bf = sb("dl_bf", [P, NFI, DM], BF16, p7)
        o2i = sb("o2i", [P, NFI, TC], BF16, p7)
        kof = sb("kof", [P, TC], F32, p7)
        ko_bf = sb("ko_bf", [P, NFI, TC], BF16, p7)
        ffn_f = sb("ffn_f", [P, ND, TC], F32, p7)
        x1r = sb("x1r", [P, ND, TC], F32, p7)
        out_f = sb("out_f", [P, ND, TC], F32, p7)
        l4b = sb("l4b", [P, ND, TC], BF16, p7)
        l4sq = sb("l4sq", [P, ND, TC], BF16, p7)
        l4st = {k: sb("l4_" + k, [P, TC], F32, p7) for k in ("m", "r", "t1", "t2")}
        for kk in range(NFI):
            kb.dma(lambda e, kk=kk: e.dma_start(
                out=dl_bf[:, kk, :], in_=st_wd[:, kk, :]), stream="ldw")
        t_dld = kb.dtok("ldw")

        def emit_ln4(src_f, t_src):
            t_c = kb.op("vector", lambda e: e.tensor_copy(
                out=l4b[:, :, :], in_=src_f[:, :, :]), waits=[t_src])
            t_s = kb.op("scalar", lambda e: e.activation(
                out=l4sq[:, :, :], in_=l4b[:, :, :], func=AF.Square), waits=[t_c])
            tmm = None
            for dk in range(ND):
                tmm = kb.op("tensor", lambda e, dk=dk: e.matmul(
                    out=ps[4][:, :], lhsT=ones_bf[:], rhs=l4b[:, dk, :],
                    start=(dk == 0), stop=(dk == ND - 1)),
                    waits=[t_c, prev.get(("l4", 4))])
            for dk in range(ND):
                tmm = kb.op("tensor", lambda e, dk=dk: e.matmul(
                    out=ps[5][:, :], lhsT=ones_bf[:], rhs=l4sq[:, dk, :],
                    start=(dk == 0), stop=(dk == ND - 1)),
                    waits=[t_s, prev.get(("l4", 5))])
            t_m = kb.op("scalar", lambda e: e.activation(
                out=l4st["m"][:, :], in_=ps[4][:, :], func=AF.Identity,
                scale=CINV_DM), waits=[tmm])
            t_v = kb.op("scalar", lambda e: e.activation(
                out=l4st["t2"][:, :], in_=ps[5][:, :], func=AF.Identity,
                scale=CINV_DM), waits=[tmm])
            prev[("l4", 4)] = t_m
            prev[("l4", 5)] = t_v
            t_ms = kb.op("scalar", lambda e: e.activation(
                out=l4st["t1"][:, :], in_=l4st["m"][:, :], func=AF.Square),
                waits=[t_m])
            t_var = kb.op("vector", lambda e: e.tensor_sub(
                out=l4st["t2"][:, :], in0=l4st["t2"][:, :], in1=l4st["t1"][:, :]),
                waits=[t_v, t_ms])
            t_sd = kb.op("scalar", lambda e: e.activation(
                out=l4st["t1"][:, :], in_=l4st["t2"][:, :], func=AF.Sqrt,
                bias=CEPS), waits=[t_var])
            return kb.op("vector", lambda e: e.reciprocal(
                out=l4st["r"][:, :], in_=l4st["t1"][:, :]), waits=[t_sd])

        for c in range(NCH):
            t_l1 = kb.dma(lambda e, c=c: e.dma_start(
                out=o2i[:, :, :], in_=st_o2[:, :, TC * c:TC * (c + 1)]),
                waits=[prev.get("p7war")], stream="l7o")
            tk = None
            for o in range(NFI):
                tk = kb.op("vector", lambda e, o=o: e.tensor_scalar(
                    out=kof[:, :], in0=o2i[:, o, :],
                    scalar1=scol[:, RO:RO + 1], scalar2=MAGIC,
                    op0=ALU.mult, op1=ALU.add), waits=[t_l1, t_ro],
                    strict=(o == 0))
                tk = kb.op("vector", lambda e, o=o: e.tensor_scalar(
                    out=kof[:, :], in0=kof[:, :], scalar1=MAGIC,
                    scalar2=32767.0, op0=ALU.subtract, op1=ALU.min), waits=[tk])
                tk = kb.op("vector", lambda e, o=o: e.tensor_scalar(
                    out=ko_bf[:, o, :], in0=kof[:, :],
                    scalar1=-32768.0, scalar2=None, op0=ALU.max), waits=[tk])
            prev["p7war"] = tk
            t_ff = None
            for m in range(ND):
                bk = m % 2
                tm = None
                for k in range(NFI):
                    tm = kb.op("tensor", lambda e, m=m, k=k, bk=bk: e.matmul(
                        out=ps[bk][:, :], lhsT=dl_bf[:, k, P * m:P * (m + 1)],
                        rhs=ko_bf[:, k, :], start=(k == 0), stop=(k == NFI - 1)),
                        waits=[tk, t_dld, prev.get(("p7", bk))])
                t_ff = kb.op("scalar", lambda e, m=m, bk=bk: e.activation(
                    out=ffn_f[:, m, :], in_=ps[bk][:, :], func=AF.Identity,
                    scale=scol[:, OM:OM + 1]), waits=[tm, t_o])
                prev[("p7", bk)] = t_ff
            t_lx1 = kb.dma(lambda e, c=c: e.dma_start(
                out=x1r[:, :, :], in_=st_x1[:, :, TC * c:TC * (c + 1)]),
                waits=[prev.get("p7xwar")], stream="l7x")
            t_r4 = emit_ln4(ffn_f, t_ff)
            t_out = None
            for dk in range(ND):
                ta = kb.op("vector", lambda e, dk=dk: e.tensor_sub(
                    out=l4st["t1"][:, :], in0=ffn_f[:, dk, :], in1=l4st["m"][:, :]),
                    waits=[t_r4])
                ta = kb.op("vector", lambda e, dk=dk: e.tensor_mul(
                    out=l4st["t1"][:, :], in0=l4st["t1"][:, :], in1=l4st["r"][:, :]),
                    waits=[])
                t_out = kb.op("vector", lambda e, dk=dk: e.tensor_add(
                    out=out_f[:, dk, :], in0=l4st["t1"][:, :], in1=x1r[:, dk, :]),
                    waits=[t_lx1])
            prev["p7xwar"] = t_out
            kb.dma(lambda e, c=c: e.dma_start(
                out=out_h[:, :, TC * c:TC * (c + 1)], in_=out_f[:, :, :]),
                waits=[t_out], stream="sout")
            kb.barrier()
        es.callback(p7.close)

    if _DBG == "p4":
        d_scol_h = hbm("d_scol", [P, 24], F32, "ExternalOutput")
        kb.dma(lambda e: e.dma_start(out=d_scol_h[:, :], in_=scol[:, :]),
               stream="dsc")
    if dbg:
        dts = []
        for nm, (oh, src) in dbg.items():
            dts.append(kb.dma(lambda e, o=oh, s=src: e.dma_start(
                out=o[:, :, :], in_=s[:, :, :]), stream="dbg" + nm))
        kb.op("gpsimd", None, waits=dts, inc=0)

    if _PHASES < 5:
        kb.dma(lambda e: e.dma_start(out=out_h[:, 0, 0:1], in_=hm_col[:, 0:1],
                                     allow_slow_non_contiguous=True),
               waits=[kb.tok("vector")], stream="out")
    kb.op("gpsimd", None,
          waits=[(s, cv) for (s, cv) in kb.dsem.values()], inc=0)

    kb.finalize()
    psc.close()
    es.close()
    return nc


def prepare_inputs(inputs):
    x = inputs["L0_attn_x"]
    W_in = inputs["L0_attn_W_in"]
    conv_w = inputs["L0_attn_conv_w"]
    conv_b = inputs["L0_attn_conv_b"]
    W_x = inputs["L0_attn_W_x"]
    W_dt = inputs["L0_attn_W_dt"]
    b_dt = inputs["L0_attn_b_dt"]
    A_log = inputs["L0_attn_A_log"]
    Dp = inputs["L0_attn_Dp"]

    C = np.ascontiguousarray
    blkfull = np.zeros((P, NDI, P), np.float32)
    for jj in range(NDI):
        for p in range(P):
            blkfull[p, jj, 8 * jj + p // NST] = 1.0
    shared = dict(
        w_inT=C(W_in.T),
        w_xTd=C(np.concatenate([W_x.T, Dp[:, None]], 1)),
        w_dtT=C(W_dt.T),
        convw_t=C(conv_w.reshape(NDI, P, 4).transpose(1, 0, 2)),
        convb_t=C(conv_b.reshape(NDI, P).T),
        bdt_t=C(b_dt.reshape(NDI, P).T),
        dp_t=C(Dp.reshape(NDI, P).T),
        alog_cols=C(A_log.reshape(NT, 8, NST).transpose(1, 2, 0).reshape(P, NT)),
        alogT=C(A_log.T),
        w_o1T=C(inputs["L0_attn_W_out1"].T),
        w_o2T=C(inputs["L0_attn_W_out2"].T),
        wg=C(inputs["L0_ffn_w_gate"]), wu=C(inputs["L0_ffn_w_up"]),
        wd=C(inputs["L0_ffn_w_down"]),
        blkfull=C(blkfull.reshape(P, NDI * P)),
        sel8full=C(np.stack(
            [(np.arange(P)[:, None] == 8 * jj + np.arange(P)[None, :] // NST)
             .astype(np.float32) for jj in range(NDI)], axis=1)
            .reshape(P, NDI * P)),
        ident=np.eye(P, dtype=np.float32),
    )
    ins = []
    for c in range(8):
        b, q = c // 4, c % 4
        lo = q * SEG
        halo = x[b, 0:3, :] if q == 0 else x[b, lo - 3:lo, :]
        seg = np.concatenate([halo, x[b, lo:lo + SEG, :]], 0)
        d = dict(shared)
        d["x_seg"] = C(seg.T)
        d["hmask"] = np.full((P, 1), 0.0 if q == 0 else 1.0, np.float32)
        d["mseg8"] = np.tile(np.array([[1.0 if (j // 4 == b and j % 4 < q)
                                        else 0.0 for j in range(8)]], np.float32),
                             (P, 1))
        ins.append(d)
    return ins


def kernel(**inputs):
    ins = prepare_inputs(inputs)
    nc = build_program()
    res = run_bass_kernel_spmd(nc, ins, core_ids=list(range(8)))
    out = np.zeros((2, 16384, 1024), np.float32)
    for c in range(8):
        b, q = c // 4, c % 4
        arr = np.asarray(res.results[c]["xout"], dtype=np.float32)
        out[b, q * SEG:(q + 1) * SEG, :] = arr.transpose(1, 0, 2).reshape(DM, SEG).T
    return out


def run_debug(inputs, trace=False):
    ins = prepare_inputs(inputs)
    nc = build_program()
    return run_bass_kernel_spmd(nc, ins, core_ids=list(range(8)), trace=trace)


# version 03
# speedup vs baseline: 1.3735x; 1.0637x over previous
import numpy as np
from contextlib import ExitStack

import concourse.bass as bass
import concourse.mybir as mybir
from concourse.bass_utils import run_bass_kernel_spmd

F32 = mybir.dt.float32
BF16 = mybir.dt.bfloat16
AF = mybir.ActivationFunctionType
ALU = mybir.AluOpType
AX = mybir.AxisListType

P = 128
SEG = 4096
TH = SEG + 3
TC = 512
NCH = SEG // TC
DM = 1024
ND = DM // P
DI = 2048
NDI = DI // P
E4 = 4096
NST = 16
DTR = 64
NT = DI * NST // P
FI = 3072
NFI = FI // P
MAGIC = 12582912.0
RG8 = [[0, 1, 2, 3, 4, 5, 6, 7]]

_DBG = ""
_PHASES = 5


class KB:
    ENG = ("sync", "vector", "scalar", "tensor", "gpsimd")

    def __init__(self, nc, es):
        self.nc = nc
        self.es = es
        self.sem = {e: es.enter_context(nc.semaphore("sp_" + e)) for e in self.ENG}
        self.cnt = {e: 0 for e in self.ENG}
        self.ops = {e: [] for e in self.ENG}
        self._last_wait = {}
        self.dsem = {}
        self.scope = "p0"

    def op(self, eng, fn, waits=(), inc=1, strict=False):
        ws = []
        if strict:
            ws.append((self.sem[eng], self.cnt[eng]))
        for t in waits:
            if t is None:
                continue
            (s, v) = t
            if s is self.sem[eng]:
                continue
            key = (eng, s.name)
            if self._last_wait.get(key, -1) >= v:
                continue
            self._last_wait[key] = v
            ws.append((s, v))
        self.cnt[eng] += inc
        self.ops[eng].append((fn, tuple(ws), inc, self.scope))
        return (self.sem[eng], self.cnt[eng])

    def dma(self, fn, waits=(), stream="misc", eng="sync"):
        if stream not in self.dsem:
            self.dsem[stream] = [self.es.enter_context(
                self.nc.semaphore("sd_" + stream)), 0]
        sem, cntv = self.dsem[stream]
        self.dsem[stream][1] = cntv + 16
        tok = (sem, cntv + 16)

        def wrapped(e, fn=fn, sem=sem):
            inst = fn(e)
            inst.then_inc(sem, 16)
            return inst
        self.op(eng, wrapped, waits, inc=0)
        return tok

    def dtok(self, stream):
        if stream not in self.dsem:
            return None
        sem, cntv = self.dsem[stream]
        return (sem, cntv)

    def tok(self, eng):
        return (self.sem[eng], self.cnt[eng])

    def toks(self):
        return [self.tok(e) for e in self.ENG]

    def barrier(self):
        toks = self.toks() + [(s, c) for (s, c) in self.dsem.values()]
        for e in self.ENG:
            self.op(e, None, waits=toks, inc=0)

    def finalize(self):
        with self.nc.Block() as block:
            for e in self.ENG:
                getattr(block, e)(self._mk_runner(e))

    def _mk_runner(self, e):
        ops, sem = self.ops[e], self.sem[e]
        nc = self.nc

        def run(eng):
            for fn, waits, inc, scope in ops:
                with nc.named_scope(scope):
                    for (s, v) in waits:
                        eng.wait_ge(s, v)
                    if fn is None:
                        continue
                    inst = fn(eng)
                    if inc:
                        assert inst is not None, f"op on {e} returned None"
                        inst.then_inc(sem, inc)
        return run


def build_program():
    nc = bass.Bass("TRN2", target_bir_lowering=False, debug=False,
                   detect_race_conditions=False)
    es = ExitStack()
    kb = KB(nc, es)

    def hbm(name, shape, dt=F32, kind="Internal"):
        return nc.dram_tensor(name, shape, dt, kind=kind).ap()

    def sb(name, shape, dt=F32, stack=None):
        return (stack or es).enter_context(nc.sbuf_tensor(name, shape, dt))

    x_h = hbm("x_seg", [DM, TH], F32, "ExternalInput")
    w_inT_h = hbm("w_inT", [DM, E4], F32, "ExternalInput")
    w_xTd_h = hbm("w_xTd", [DI, 97], F32, "ExternalInput")
    w_dtT_h = hbm("w_dtT", [DTR, DI], F32, "ExternalInput")
    convw_h = hbm("convw_t", [P, NDI, 4], F32, "ExternalInput")
    convb_h = hbm("convb_t", [P, NDI], F32, "ExternalInput")
    bdt_h = hbm("bdt_t", [P, NDI], F32, "ExternalInput")
    dp_h = hbm("dp_t", [P, NDI], F32, "ExternalInput")
    alogc_h = hbm("alog_cols", [P, NT], F32, "ExternalInput")
    alogT_h = hbm("alogT", [NST, DI], F32, "ExternalInput")
    w_o1T_h = hbm("w_o1T", [E4, DI], F32, "ExternalInput")
    w_o2T_h = hbm("w_o2T", [DI, DM], F32, "ExternalInput")
    wg_h = hbm("wg", [FI, DM], F32, "ExternalInput")
    wu_h = hbm("wu", [FI, DM], F32, "ExternalInput")
    wd_h = hbm("wd", [DM, FI], F32, "ExternalInput")
    blkfull_h = hbm("blkfull", [P, NDI * P], F32, "ExternalInput")
    sel8f_h = hbm("sel8full", [P, NDI * P], F32, "ExternalInput")
    ident_h = hbm("ident", [P, P], F32, "ExternalInput")
    hmask_h = hbm("hmask", [P, 1], F32, "ExternalInput")
    mseg8_h = hbm("mseg8", [P, 8], F32, "ExternalInput")
    out_h = hbm("xout", [P, ND, SEG], F32, "ExternalOutput")

    st_delta = hbm("st_delta", [P, NDI, SEG], BF16)
    st_w = hbm("st_w", [P, NDI, SEG], BF16)
    st_u = hbm("st_u", [P, NDI, SEG], BF16)
    st_zs = hbm("st_zs", [P, NDI, SEG], BF16)
    st_yl = hbm("st_yl", [P, NDI, SEG], BF16)
    st_yc = hbm("st_yc", [P, NDI, SEG], BF16)
    st_x1 = hbm("st_x1", [P, ND, SEG], F32)
    st_v = hbm("st_v", [P, ND, SEG], F32)
    st_h = hbm("st_h", [P, NDI, SEG], BF16)

    st_wg = hbm("st_wg", [P, ND, FI], BF16)
    st_wu = hbm("st_wu", [P, ND, FI], BF16)
    st_wd = hbm("st_wd", [P, NFI, DM], BF16)
    st_gate = hbm("st_gate", [P, NFI, SEG], BF16)
    st_up = hbm("st_up", [P, NFI, SEG], BF16)
    st_ku = hbm("st_ku", [P, NFI, SEG], BF16)
    st_t2 = hbm("st_t2", [P, NFI, SEG], BF16)
    st_o2 = hbm("st_o2", [P, NFI, SEG], BF16)

    cc_rp_in = hbm("cc_rp_in", [1, 2 * P * NT], F32)
    cc_rp_out = hbm("cc_rp_out", [8, 2 * P * NT], F32)
    cc_mx_in = hbm("cc_mx_in", [1, 2], F32)
    cc_mx_out = hbm("cc_mx_out", [1, 2], F32)
    cc_m2_in = hbm("cc_m2_in", [1, 2], F32)
    cc_m2_out = hbm("cc_m2_out", [1, 2], F32)
    cc_m3_in = hbm("cc_m3_in", [1, 2], F32)
    cc_m3_out = hbm("cc_m3_out", [1, 2], F32)
    cc_m4_in = hbm("cc_m4_in", [1, 2], F32)
    cc_m4_out = hbm("cc_m4_out", [1, 2], F32)

    dbg = {}
    if _DBG == "p1":
        for nm, src in (("d_delta", st_delta), ("d_w", st_w),
                        ("d_u", st_u), ("d_zs", st_zs)):
            dbg[nm] = (hbm(nm, [P, NDI, SEG], BF16, "ExternalOutput"), src)
    if _DBG in ("p15", "p2"):
        dbg["d_yl"] = (hbm("d_yl", [P, NDI, SEG], BF16, "ExternalOutput"), st_yl)
    if _DBG == "p2":
        dbg["d_yc"] = (hbm("d_yc", [P, NDI, SEG], BF16, "ExternalOutput"), st_yc)
    if _DBG == "p4":
        dbg["d_gate"] = (hbm("d_gate", [P, NFI, SEG], BF16, "ExternalOutput"), st_gate)
        dbg["d_up"] = (hbm("d_up", [P, NFI, SEG], BF16, "ExternalOutput"), st_up)
        dbg["d_wg"] = (hbm("d_wg", [P, ND, FI], BF16, "ExternalOutput"), st_wg)
        dbg["d_t2"] = (hbm("d_t2", [P, NFI, SEG], BF16, "ExternalOutput"), st_t2)
        dbg["d_o2"] = (hbm("d_o2", [P, NFI, SEG], BF16, "ExternalOutput"), st_o2)
        dbg["d_ku"] = (hbm("d_ku", [P, NFI, SEG], BF16, "ExternalOutput"), st_ku)
        d_sgn_h = hbm("d_sgn", [P, NT], F32, "ExternalOutput")
        d_lab_h = hbm("d_lab", [P, NT], F32, "ExternalOutput")
        d_sin_h = hbm("d_sin", [P, NT], F32, "ExternalOutput")
        d_e2_h = hbm("d_e2", [P, TC], F32, "ExternalOutput")
        d_dbc_h = hbm("d_dbc", [P, TC], BF16, "ExternalOutput")
        d_sgs_h = hbm("d_sgs", [P, P], BF16, "ExternalOutput")

    bcd_bf = sb("bcd_bf", [P, SEG], BF16)
    cdpu_bf = sb("cdpu_bf", [17, SEG], BF16)
    a_cols = sb("a_cols", [P, NT])
    lnabs_c = sb("lnabs_c", [P, NT])
    sign_c = sb("sign_c", [P, NT])
    state_c = sb("state_c", [P, NT])
    rseg_c = sb("rseg_c", [P, NT])
    pseg_c = sb("pseg_c", [P, NT])
    convw = sb("convw", [P, NDI, 4])
    convb = sb("convb", [P, NDI])
    bdt = sb("bdt", [P, NDI])
    dpc = sb("dpc", [P, NDI])
    hm_col = sb("hm_col", [P, 1])
    mseg_col = sb("mseg_col", [P, 8])
    ones_bf = sb("ones_bf", [P, P], BF16)
    blkf_bf = sb("blkf_bf", [P, NDI * P], BF16)
    sel8_bf = sb("sel8_bf", [P, NDI * P], BF16)
    dsum = sb("dsum", [P, NDI])
    dsum_bf = sb("dsum_bf", [P, NDI], BF16)
    dsr = sb("dsr", [P, 1])
    ident_bf = sb("ident_bf", [P, P], BF16)
    lhsT_yt = sb("lhsT_yt", [17, DI], BF16)

    ps = [es.enter_context(nc.psum_tensor(f"ps{i}", [P, TC], F32)) for i in range(8)]

    _cc_store = sb("cc_store", [P, 16])
    _cc_map = {}

    def cc(val):
        if val not in _cc_map:
            i = len(_cc_map)
            assert i < 16
            kb.op("vector", lambda e, i=i, v=val: e.memset(_cc_store[:, i:i + 1], v))
            _cc_map[val] = _cc_store[:, i:i + 1]
        return _cc_map[val]

    CINV_DM = cc(1.0 / DM)
    CEPS = cc(1e-5)
    CNEG1 = cc(-1.0)
    CONE = cc(1.0)

    p1w = ExitStack()
    p0w = ExitStack()
    w_in_bf = sb("w_in_bf", [P, ND, E4], BF16, p1w)
    w_x_bf = sb("w_x_bf", [P, NDI, 97], BF16, p1w)
    w_dt_bf = sb("w_dt_bf", [DTR, DI], BF16, p1w)
    wstg = [sb(f"wstg{i}", [P, E4], F32, p0w) for i in range(2)]

    for (dst, src) in ((convw, convw_h), (convb, convb_h), (bdt, bdt_h),
                       (dpc, dp_h)):
        kb.dma(lambda e, d=dst, s=src: e.dma_start(
            out=d[:, :, :] if len(d.shape) == 3 else d[:, :], in_=s), stream="ini")
    OFF_BLK, OFF_ID, OFF_AT = 0, NDI * P, 0
    kb.dma(lambda e: e.dma_start(out=a_cols[:], in_=alogc_h), stream="ini")
    kb.dma(lambda e: e.dma_start(
        out=wstg[1][0:P, OFF_BLK:OFF_BLK + NDI * P], in_=blkfull_h), stream="ini")
    kb.dma(lambda e: e.dma_start(
        out=wstg[0][0:P, DI:DI + NDI * P], in_=sel8f_h), stream="ini")
    kb.dma(lambda e: e.dma_start(
        out=wstg[1][0:P, OFF_ID:OFF_ID + P], in_=ident_h), stream="ini")
    kb.dma(lambda e: e.dma_start(
        out=wstg[0][0:NST, OFF_AT:OFF_AT + DI], in_=alogT_h), stream="ini")
    kb.dma(lambda e: e.dma_start(out=hm_col[:], in_=hmask_h), stream="ini")
    kb.dma(lambda e: e.dma_start(out=mseg_col[:], in_=mseg8_h), stream="ini")
    t_ini = kb.dtok("ini")
    t_alog = t_blk = t_id = t_alT = t_hm = t_ini

    kb.op("vector", lambda e: e.memset(ones_bf[:], 1.0))
    kb.op("vector", lambda e: e.tensor_copy(
        out=blkf_bf[:], in_=wstg[1][0:P, OFF_BLK:OFF_BLK + NDI * P]), waits=[t_blk])
    kb.op("vector", lambda e: e.tensor_copy(
        out=sel8_bf[:], in_=wstg[0][0:P, DI:DI + NDI * P]), waits=[t_blk])
    kb.op("vector", lambda e: e.memset(dsum[:], 0.0))
    kb.op("vector", lambda e: e.tensor_copy(
        out=ident_bf[:], in_=wstg[1][0:P, OFF_ID:OFF_ID + P]), waits=[t_id])
    t_ac = kb.op("scalar", lambda e: e.activation(
        out=a_cols[:], in_=a_cols[:], func=AF.Exp), waits=[t_alog])
    kb.op("vector", lambda e: e.tensor_scalar(
        out=a_cols[:], in0=a_cols[:], scalar1=-1.0, scalar2=None, op0=ALU.mult),
        waits=[t_ac])
    t_yt = kb.op("scalar", lambda e: e.activation(
        out=wstg[0][0:NST, OFF_AT:OFF_AT + DI],
        in_=wstg[0][0:NST, OFF_AT:OFF_AT + DI], func=AF.Exp), waits=[t_alT])
    kb.op("vector", lambda e: e.memset(lhsT_yt[0:17, :], 1.0))
    kb.op("vector", lambda e: e.tensor_scalar(
        out=lhsT_yt[0:NST, :], in0=wstg[0][0:NST, OFF_AT:OFF_AT + DI],
        scalar1=-1.0, scalar2=None, op0=ALU.mult), waits=[t_yt])
    kb.op("vector", lambda e: e.memset(state_c[:], 0.0))

    tw = []
    for dk in range(ND):
        tl = kb.dma(lambda e, dk=dk: e.dma_start(
            out=wstg[dk % 2][:, :], in_=w_inT_h[P * dk:P * (dk + 1), :]),
            waits=([tw[dk - 2]] if dk >= 2 else [kb.tok("vector")]),
            stream="wA" if dk % 2 == 0 else "wB")
        tw.append(kb.op("vector", lambda e, dk=dk: e.tensor_copy(
            out=w_in_bf[:, dk, :], in_=wstg[dk % 2][:, :]), waits=[tl]))
    t_wx = kb.dma(lambda e: e.dma_start(
        out=wstg[0][:, 0:NDI * 97].rearrange("p (k e) -> p k e", e=97),
        in_=w_xTd_h.rearrange("(k p) e -> p k e", p=P)),
        waits=[tw[-2], kb.tok("vector"), kb.tok("scalar")], stream="wA")
    t_wx = kb.op("vector", lambda e: e.tensor_copy(
        out=w_x_bf[:].rearrange("p a b -> p (a b)"), in_=wstg[0][:, 0:NDI * 97]),
        waits=[t_wx])
    t_wdt = kb.dma(lambda e: e.dma_start(out=wstg[1][0:DTR, 0:DI], in_=w_dtT_h),
                   waits=[tw[-1], kb.tok("vector")], stream="wB")
    t_wdt = kb.op("vector", lambda e: e.tensor_copy(
        out=w_dt_bf[:], in_=wstg[1][0:DTR, 0:DI]), waits=[t_wdt])

    kb.barrier()
    p0w.close()

    p1 = ExitStack()
    x_ld = [sb(f"x_ld{i}", [P, TC], F32, p1) for i in range(2)]
    x_bf = sb("x_bf", [P, ND, TC], BF16, p1)
    xh_bf = sb("xh_bf", [P, ND, TC], BF16, p1)
    stat = {k: sb("sta_" + k, [P, TC], F32, p1) for k in ("m", "rstd", "t1", "t2")}
    xi_bf = sb("xi_bf", [P, NDI, TC + 3], BF16, p1)
    zsdl_bf = sb("zsdl_bf", [P, NDI, TC], BF16, p1)
    u_bf = sb("u_bf", [P, NDI, TC], BF16, p1)
    cacc = sb("cacc", [P, TC], F32, p1)
    tts = sb("tts", [P, TC], F32, p1)
    sps = sb("sps", [P, TC], F32, p1)
    dls = sb("dls", [P, TC], F32, p1)
    dtr_bf = sb("dtr_bf", [DTR, TC], BF16, p1)

    st = {"prev_bank": [None] * 8, "halo_free": None}

    def emit_ln(width, off, deps):
        t_casts = []
        for dk in range(ND):
            tl = kb.dma(lambda e, dk=dk, w=width, o=off: e.dma_start(
                out=x_ld[dk % 2][:, 0:w], in_=x_h[P * dk:P * (dk + 1), o:o + w]),
                waits=deps + ([t_casts[dk - 2]] if dk >= 2 else []),
                stream="x0" if dk % 2 == 0 else "x1")
            t_casts.append(kb.op("vector", lambda e, dk=dk, w=width: e.tensor_copy(
                out=x_bf[:, dk, 0:w], in_=x_ld[dk % 2][:, 0:w]), waits=[tl]))
        t_sq = kb.op("scalar", lambda e, w=width: e.activation(
            out=xh_bf[:, :, 0:w], in_=x_bf[:, :, 0:w], func=AF.Square),
            waits=[t_casts[-1]])
        tmm = None
        for dk in range(ND):
            tmm = kb.op("tensor", lambda e, dk=dk, w=width: e.matmul(
                out=ps[4][:, 0:w], lhsT=ones_bf[:], rhs=x_bf[:, dk, 0:w],
                start=(dk == 0), stop=(dk == ND - 1)),
                waits=[t_casts[dk], st["prev_bank"][4]])
        for dk in range(ND):
            tmm = kb.op("tensor", lambda e, dk=dk, w=width: e.matmul(
                out=ps[5][:, 0:w], lhsT=ones_bf[:], rhs=xh_bf[:, dk, 0:w],
                start=(dk == 0), stop=(dk == ND - 1)),
                waits=[t_sq, st["prev_bank"][5]])
        t_m = kb.op("scalar", lambda e, w=width: e.activation(
            out=stat["m"][:, 0:w], in_=ps[4][:, 0:w], func=AF.Identity, scale=CINV_DM),
            waits=[tmm])
        t_v = kb.op("scalar", lambda e, w=width: e.activation(
            out=stat["t2"][:, 0:w], in_=ps[5][:, 0:w], func=AF.Identity, scale=CINV_DM),
            waits=[tmm])
        st["prev_bank"][4] = t_m
        st["prev_bank"][5] = t_v
        t_ms = kb.op("scalar", lambda e, w=width: e.activation(
            out=stat["t1"][:, 0:w], in_=stat["m"][:, 0:w], func=AF.Square),
            waits=[t_m])
        strict = width < P
        t_var = kb.op("vector", lambda e, w=width: e.tensor_sub(
            out=stat["t2"][:, 0:w], in0=stat["t2"][:, 0:w], in1=stat["t1"][:, 0:w]),
            waits=[t_v, t_ms], strict=strict)
        t_sd = kb.op("scalar", lambda e, w=width: e.activation(
            out=stat["t1"][:, 0:w], in_=stat["t2"][:, 0:w], func=AF.Sqrt, bias=CEPS),
            waits=[t_var])
        t_r = kb.op("vector", lambda e, w=width: e.reciprocal(
            out=stat["rstd"][:, 0:w], in_=stat["t1"][:, 0:w]), waits=[t_sd],
            strict=strict)
        t_app = None
        for dk in range(ND):
            ta = kb.op("vector", lambda e, dk=dk, w=width: e.tensor_sub(
                out=stat["t1"][:, 0:w], in0=x_bf[:, dk, 0:w], in1=stat["m"][:, 0:w]),
                waits=[t_r], strict=strict)
            t_app = kb.op("vector", lambda e, dk=dk, w=width: e.tensor_mul(
                out=xh_bf[:, dk, 0:w], in0=stat["t1"][:, 0:w],
                in1=stat["rstd"][:, 0:w]), waits=[ta], strict=strict)
        return t_app

    def emit_xz(width, t_xh, is_halo, zs_free=None):
        evac = []
        for em in range(32):
            bank = em % 2
            tmm = None
            for dk in range(ND):
                tmm = kb.op("tensor", lambda e, em=em, dk=dk, bank=bank, w=width: e.matmul(
                    out=ps[bank][:, 0:w],
                    lhsT=w_in_bf[:, dk, P * em:P * (em + 1)],
                    rhs=xh_bf[:, dk, 0:w],
                    start=(dk == 0), stop=(dk == ND - 1)),
                    waits=[t_xh, st["prev_bank"][bank]])
            if em < NDI:
                if is_halo:
                    te = kb.op("scalar", lambda e, em=em, bank=bank: e.activation(
                        out=xi_bf[:, em, 0:3], in_=ps[bank][:, 0:3],
                        func=AF.Identity, scale=hm_col[:, 0:1]),
                        waits=[tmm, t_hm])
                else:
                    te = kb.op("scalar", lambda e, em=em, bank=bank, w=width: e.copy(
                        out=xi_bf[:, em, 3:3 + w], in_=ps[bank][:, 0:w]),
                        waits=[tmm, st["halo_free"]])
            else:
                if is_halo:
                    te = tmm
                else:
                    te = kb.op("scalar", lambda e, em=em, bank=bank, w=width: e.activation(
                        out=zsdl_bf[:, em - NDI, 0:w], in_=ps[bank][:, 0:w],
                        func=AF.Silu), waits=[tmm, zs_free])
            st["prev_bank"][bank] = te
            evac.append(te)
        return evac

    kb.scope = 'p1'
    t_xh0 = emit_ln(3, 0, [tw[-1], t_wx, kb.tok("vector")])
    emit_xz(3, t_xh0, True)
    kb.barrier()
    if _DBG == "p1":
        d_xi_h = hbm("d_xi", [P, NDI, 3], BF16, "ExternalOutput")
        d_xh_h = hbm("d_xh", [P, ND, 3], BF16, "ExternalOutput")
        d_xb_h = hbm("d_xb", [P, ND, 3], BF16, "ExternalOutput")
        d_st_h = hbm("d_st", [P, 8], F32, "ExternalOutput")
        kb.dma(lambda e: e.dma_start(out=d_xi_h[:, :, :], in_=xi_bf[:, :, 0:3]),
               stream="dxi")
        kb.dma(lambda e: e.dma_start(out=d_xh_h[:, :, :], in_=xh_bf[:, :, 0:3]),
               stream="dxi")
        kb.dma(lambda e: e.dma_start(out=d_xb_h[:, :, :], in_=x_bf[:, :, 0:3]),
               stream="dxi")
        kb.dma(lambda e: e.dma_start(out=d_st_h[:, 0:3], in_=stat["m"][:, 0:3]),
               stream="dxi")
        kb.dma(lambda e: e.dma_start(out=d_st_h[:, 4:7], in_=stat["rstd"][:, 0:3]),
               stream="dxi")
        kb.barrier()

    prev = {"xdbl": None, "dt0": None, "dt1": None, "zs_dma": None,
            "u_dma": None, "dl_dma": None, "w_dma": None}

    for c in range(NCH):
        off = 3 + TC * c
        t_xh = emit_ln(TC, off, [])
        evac = emit_xz(TC, t_xh, False, zs_free=prev["dl_dma"])
        prev["zs_dma"] = kb.dma(lambda e, c=c: e.dma_start(
            out=st_zs[:, :, TC * c:TC * (c + 1)], in_=zsdl_bf[:, :, :]),
            waits=[evac[-1]], stream="szs")

        t_u = []
        for j in range(NDI):
            tk = kb.op("vector", lambda e, j=j: e.tensor_scalar(
                out=cacc[:, :], in0=xi_bf[:, j, 0:TC],
                scalar1=convw[:, j, 0:1], scalar2=None, op0=ALU.mult),
                waits=[evac[j], st["halo_free"], t_u[-1] if t_u else None])
            for k in range(1, 4):
                tk = kb.op("vector", lambda e, j=j, k=k: e.scalar_tensor_tensor(
                    out=cacc[:, :], in0=xi_bf[:, j, k:k + TC],
                    scalar=convw[:, j, k:k + 1], in1=cacc[:, :],
                    op0=ALU.mult, op1=ALU.add), waits=[tk])
            t_u.append(kb.op("scalar", lambda e, j=j: e.activation(
                out=u_bf[:, j, :], in_=cacc[:, :], func=AF.Silu,
                bias=convb[:, j:j + 1]), waits=[tk, prev["w_dma"]]))
        st["halo_free"] = kb.op("vector", lambda e: e.tensor_copy(
            out=xi_bf[:, :, 0:3], in_=xi_bf[:, :, TC:TC + 3]), waits=[t_u[-1]])
        prev["u_dma"] = kb.dma(lambda e, c=c: e.dma_start(
            out=st_u[:, :, TC * c:TC * (c + 1)], in_=u_bf[:, :, :]),
            waits=[t_u[-1]], stream="su")

        tmm = None
        for k in range(NDI):
            tmm = kb.op("tensor", lambda e, k=k: e.matmul(
                out=ps[6][0:97, :], lhsT=w_x_bf[:, k, :], rhs=u_bf[:, k, :],
                start=(k == 0), stop=(k == NDI - 1)),
                waits=[t_u[k], prev["xdbl"]])
        t_dtr = kb.op("scalar", lambda e: e.copy(
            out=dtr_bf[:, :], in_=ps[6][0:DTR, :]), waits=[tmm])
        t_bc = kb.op("scalar", lambda e, c=c: e.copy(
            out=bcd_bf[0:33, TC * c:TC * (c + 1)], in_=ps[6][DTR:97, :]), waits=[tmm])
        t_cd = kb.dma(lambda e, c=c: e.dma_start(
            out=cdpu_bf[0:17, TC * c:TC * (c + 1)],
            in_=bcd_bf[16:33, TC * c:TC * (c + 1)]), waits=[t_bc], stream="scd")
        prev["xdbl"] = t_bc

        t_dl_all = []
        for m in range(NDI):
            bk = 2 + (m % 2)
            tm = kb.op("tensor", lambda e, m=m, bk=bk: e.matmul(
                out=ps[bk][:, :], lhsT=w_dt_bf[:, P * m:P * (m + 1)],
                rhs=dtr_bf[:, :], start=True, stop=True),
                waits=[t_dtr, t_wdt, prev["dt0"] if bk == 2 else prev["dt1"]])
            te = kb.op("scalar", lambda e, m=m, bk=bk: e.activation(
                out=tts[:, :], in_=ps[bk][:, :], func=AF.Identity,
                bias=bdt[:, m:m + 1]), waits=[tm, kb.tok("vector")])
            if bk == 2:
                prev["dt0"] = te
            else:
                prev["dt1"] = te
            t_sp = kb.op("scalar", lambda e: e.activation(
                out=sps[:, :], in_=tts[:, :], func=AF.Abs), waits=[te])
            t_sp = kb.op("scalar", lambda e: e.activation(
                out=sps[:, :], in_=sps[:, :], func=AF.Exp, scale=CNEG1), waits=[t_sp])
            t_sp = kb.op("scalar", lambda e: e.activation(
                out=sps[:, :], in_=sps[:, :], func=AF.Ln, bias=CONE), waits=[t_sp])
            t_dl = kb.op("vector", lambda e: e.scalar_tensor_tensor(
                out=dls[:, :], in0=tts[:, :], scalar=0.0, in1=sps[:, :],
                op0=ALU.max, op1=ALU.add), waits=[t_sp])
            t_dsr = kb.op("vector", lambda e: e.reduce_sum(
                out=dsr[:, 0:1], in_=dls[:, :], axis=AX.X), waits=[t_dl])
            kb.op("vector", lambda e, m=m: e.tensor_add(
                out=dsum[:, m:m + 1], in0=dsum[:, m:m + 1], in1=dsr[:, 0:1]),
                waits=[t_dsr], strict=True)
            t_w = kb.op("vector", lambda e, m=m: e.tensor_mul(
                out=u_bf[:, m, :], in0=dls[:, :], in1=u_bf[:, m, :]),
                waits=[t_dl, tmm, prev["u_dma"]])
            t_db = kb.op("vector", lambda e, m=m: e.tensor_copy(
                out=zsdl_bf[:, m, :], in_=dls[:, :]),
                waits=[t_dl, prev["zs_dma"]])
            t_dl_all.append((t_w, t_db))
        prev["dl_dma"] = kb.dma(lambda e, c=c: e.dma_start(
            out=st_delta[:, :, TC * c:TC * (c + 1)], in_=zsdl_bf[:, :, :]),
            waits=[t_dl_all[-1][1]], stream="sdl")
        prev["w_dma"] = kb.dma(lambda e, c=c: e.dma_start(
            out=st_w[:, :, TC * c:TC * (c + 1)], in_=u_bf[:, :, :]),
            waits=[t_dl_all[-1][0]], stream="sw")
        kb.barrier()

    kb.scope = 'p15'
    p1.close()
    p1w.close()
    psc = ExitStack()
    if _PHASES >= 2:
        bt_f = sb("bt_f", [P, SEG], BF16, psc)
        ct_f = sb("ct_f", [P, SEG], BF16, psc)
        dgrp = [sb(f"dgrp{i}", [P, SEG], BF16, psc) for i in range(2)]
        wgrp = [sb(f"wgrp{i}", [P, SEG], BF16, psc) for i in range(2)]
        dA_f = [sb(f"dA_f{i}", [P, SEG], BF16, psc) for i in range(2)]
        b_f = sb("b_f", [P, SEG], BF16, psc)
        s_f = [sb(f"s_f{i}", [P, SEG], BF16, psc) for i in range(2)]
        p_f = [sb(f"p_f{i}", [P, SEG], BF16, psc) for i in range(8)]
        yl_s = [sb(f"yl_s{i}", [P, SEG], BF16, psc) for i in range(2)]
        drep = sb("drep", [P, NT], F32, psc)
        bias_c = sb("bias_c", [P, NT], F32, psc)
        gtmp = [sb(f"gtmp{i}", [P, NT], F32, psc) for i in range(2)]

        for i8 in range(8):
            kb.dma(lambda e, i8=i8: e.dma_start(
                out=bt_f[16 * i8:16 * (i8 + 1), :], in_=bcd_bf[0:16, :]),
                stream="pbt")
            kb.dma(lambda e, i8=i8: e.dma_start(
                out=ct_f[16 * i8:16 * (i8 + 1), :], in_=bcd_bf[16:32, :]),
                stream="pct")

        def scan_sweep(correction):
            pv = {}
            for g in range(NDI):
                gp = g % 2
                t_ld = kb.dma(lambda e, g=g, gp=gp: e.dma_start(
                    out=dgrp[gp][:, :], in_=st_delta[:, g, :]),
                    waits=[pv.get(("dr", gp))], stream=f"pld{gp}")
                if not correction:
                    t_rhs = kb.dma(lambda e, g=g, gp=gp: e.dma_start(
                        out=wgrp[gp][:, :], in_=st_w[:, g, :]),
                        waits=[pv.get(("wr", gp))], stream=f"plw{gp}")
                else:
                    t_rhs = kb.op("vector", lambda e, gp=gp: e.tensor_tensor_scan(
                        out=wgrp[gp][:, :], data0=dgrp[gp][:, :],
                        data1=dgrp[gp][:, :], initial=0.0,
                        op0=ALU.add, op1=ALU.bypass),
                        waits=[t_ld, pv.get(("wr", gp))])
                    pv[("dr", gp)] = t_rhs
                for h in range(2):
                    for r in range(8):
                        jj = 8 * h + r
                        j = NDI * g + jj
                        pb = jj % 2
                        sfb = jj % 2
                        if not correction:
                            for cc in range(8):
                                cb = cc % 2
                                tmd = kb.op("tensor", lambda e, jj=jj, gp=gp, cc=cc, cb=cb: e.matmul(
                                    out=ps[cb][:, :], lhsT=sel8_bf[:, P * jj:P * (jj + 1)],
                                    rhs=dgrp[gp][:, TC * cc:TC * (cc + 1)],
                                    start=True, stop=True),
                                    waits=[t_ld, pv.get(("db", cb))])
                                t_dA = kb.op("scalar", lambda e, j=j, pb=pb, cc=cc, cb=cb: e.activation(
                                    out=dA_f[pb][:, TC * cc:TC * (cc + 1)], in_=ps[cb][:, :],
                                    func=AF.Exp, scale=a_cols[:, j:j + 1]),
                                    waits=[tmd, pv.get(("dAr", pb))])
                                pv[("db", cb)] = t_dA
                            pv[("dmml", gp)] = tmd
                            for cc in range(8):
                                cb = 2 + cc % 2
                                tmw = kb.op("tensor", lambda e, jj=jj, gp=gp, cc=cc, cb=cb: e.matmul(
                                    out=ps[cb][:, :], lhsT=sel8_bf[:, P * jj:P * (jj + 1)],
                                    rhs=wgrp[gp][:, TC * cc:TC * (cc + 1)],
                                    start=True, stop=True),
                                    waits=[t_rhs, pv.get(("wb", cb))])
                                t_b = kb.op("vector", lambda e, cc=cc, cb=cb: e.tensor_tensor(
                                    out=b_f[:, TC * cc:TC * (cc + 1)], in0=ps[cb][:, :],
                                    in1=bt_f[:, TC * cc:TC * (cc + 1)], op=ALU.mult),
                                    waits=[tmw, kb.dtok("pbt")])
                                pv[("wb", cb)] = t_b
                            pv[("wmml", gp)] = tmw
                            t_scan = kb.op("vector", lambda e, pb=pb, sfb=sfb: e.tensor_tensor_scan(
                                out=s_f[sfb][:, :], data0=dA_f[pb][:, :], data1=b_f[:, :],
                                initial=0.0, op0=ALU.mult, op1=ALU.add),
                                waits=[t_dA, pv.get(("sc", sfb))])
                            pv[("dAr", pb)] = t_scan
                            t_p = kb.op("gpsimd", lambda e, sfb=sfb, r=r: e.tensor_tensor(
                                out=p_f[r][:, :], in0=s_f[sfb][:, :], in1=ct_f[:, :],
                                op=ALU.mult),
                                waits=[t_scan, kb.dtok("pct"), pv.get(("pr", r))])
                            t_sc = kb.op("gpsimd", lambda e, j=j, sfb=sfb: e.tensor_copy(
                                out=state_c[:, j:j + 1], in_=s_f[sfb][:, SEG - 1:SEG]))
                            pv[("sc", sfb)] = t_sc
                        else:
                            for cc in range(8):
                                cb = cc % 2
                                tmd = kb.op("tensor", lambda e, jj=jj, gp=gp, cc=cc, cb=cb: e.matmul(
                                    out=ps[cb][:, :], lhsT=sel8_bf[:, P * jj:P * (jj + 1)],
                                    rhs=wgrp[gp][:, TC * cc:TC * (cc + 1)],
                                    start=True, stop=True),
                                    waits=[t_rhs, pv.get(("db", cb))])
                                t_E = kb.op("scalar", lambda e, j=j, pb=pb, cc=cc, cb=cb: e.activation(
                                    out=dA_f[pb][:, TC * cc:TC * (cc + 1)], in_=ps[cb][:, :],
                                    func=AF.Exp, scale=a_cols[:, j:j + 1],
                                    bias=lnabs_c[:, j:j + 1]),
                                    waits=[tmd, pv.get(("dAr", pb))])
                                pv[("db", cb)] = t_E
                            pv[("wmml", gp)] = tmd
                            t_p = kb.op("vector", lambda e, j=j, pb=pb, r=r: e.scalar_tensor_tensor(
                                out=p_f[r][:, :], in0=dA_f[pb][:, :],
                                scalar=sign_c[:, j:j + 1], in1=ct_f[:, :],
                                op0=ALU.mult, op1=ALU.mult),
                                waits=[t_E, kb.dtok("pct"), pv.get(("pr", r))])
                            pv[("dAr", pb)] = t_p
                        pv[("tp", r)] = t_p
                    hc = 2 * g + h
                    yb = 4 + hc % 2
                    for cc in range(8):
                        tmy = None
                        for r in range(8):
                            jj = 8 * h + r
                            tmy = kb.op("tensor", lambda e, jj=jj, h=h, r=r, cc=cc, yb=yb: e.matmul(
                                out=ps[yb][0:64, :],
                                lhsT=blkf_bf[:, P * jj + 64 * h:P * jj + 64 * h + 64],
                                rhs=p_f[r][:, TC * cc:TC * (cc + 1)],
                                start=(r == 0), stop=(r == 7), skip_group_check=True),
                                waits=[pv[("tp", r)],
                                       pv.get(("yb", yb)) if r == 0 else None])
                            pv[("pr", r)] = tmy
                        tye = kb.op("scalar", lambda e, h=h, cc=cc, yb=yb, gp=gp: e.activation(
                            out=yl_s[gp][64 * h:64 * h + 64, TC * cc:TC * (cc + 1)],
                            in_=ps[yb][0:64, :], func=AF.Identity),
                            waits=[tmy, pv.get(("ylw", gp))])
                        pv[("yb", yb)] = tye
                        pv[("ye", gp)] = tye
                t_sto = kb.dma(lambda e, g=g, gp=gp, correction=correction: e.dma_start(
                    out=(st_yc if correction else st_yl)[:, g, :],
                    in_=yl_s[gp][:, :]),
                    waits=[pv.get(("ye", gp))], stream=f"pyl{gp}")
                pv[("ylw", gp)] = t_sto
                if not correction:
                    pv[("dr", gp)] = pv[("dmml", gp)]
                pv[("wr", gp)] = pv[("wmml", gp)]

        scan_sweep(correction=False)

    if _PHASES >= 3:
        kb.scope = 'pag'
        t_dsb = kb.op("vector", lambda e: e.tensor_copy(
            out=dsum_bf[:, :], in_=dsum[:, :]), waits=[kb.tok("vector")],
            strict=True)
        for jj in range(NDI):
            tm = kb.op("tensor", lambda e, jj=jj: e.matmul(
                out=ps[6][:, 0:NDI], lhsT=sel8_bf[:, P * jj:P * (jj + 1)],
                rhs=dsum_bf[:, :], start=True, stop=True),
                waits=[t_dsb, kb.tok("scalar")])
            kb.op("scalar", lambda e, jj=jj: e.activation(
                out=drep[:, jj::NDI], in_=ps[6][:, 0:NDI], func=AF.Identity),
                waits=[tm])
        t_ps = kb.op("vector", lambda e: e.tensor_mul(
            out=pseg_c[:, :], in0=a_cols[:, :], in1=drep[:, :]),
            waits=[kb.tok("scalar")], strict=True)
        t_ps = kb.op("scalar", lambda e: e.activation(
            out=pseg_c[:, :], in_=pseg_c[:, :], func=AF.Exp), waits=[t_ps])
        t_c1 = kb.dma(lambda e: e.dma_start(
            out=cc_rp_in[0, 0:P * NT], in_=state_c[:, :]),
            waits=[kb.tok("vector"), kb.tok("gpsimd")], stream="cca")
        t_c2 = kb.dma(lambda e: e.dma_start(
            out=cc_rp_in[0, P * NT:2 * P * NT], in_=pseg_c[:, :]),
            waits=[t_ps], stream="cca")
        t_ag = kb.op("gpsimd", lambda e: e.collective_compute(
            "AllGather", ALU.bypass, replica_groups=RG8,
            ins=[cc_rp_in[:, :]], outs=[cc_rp_out[:, :]]),
            waits=[t_c1, t_c2])
        acc = state_c
        rr_s = gtmp[0]
        pr_s = gtmp[1]
        kb.op("vector", lambda e: e.memset(acc[:, :], 0.0), waits=[t_ag])
        for r in range(8):
            t_l1 = kb.dma(lambda e, r=r: e.dma_start(
                out=rr_s[:, :], in_=cc_rp_out[r, 0:P * NT]),
                waits=[t_ag, kb.tok("vector")], stream="ccr")
            t_l2 = kb.dma(lambda e, r=r: e.dma_start(
                out=pr_s[:, :], in_=cc_rp_out[r, P * NT:2 * P * NT]),
                waits=[t_ag, kb.tok("vector")], stream="ccp")
            kb.op("vector", lambda e: e.tensor_mul(
                out=bias_c[:, :], in0=pr_s[:, :], in1=acc[:, :]),
                waits=[t_l1, t_l2], strict=True)
            kb.op("vector", lambda e: e.tensor_add(
                out=bias_c[:, :], in0=bias_c[:, :], in1=rr_s[:, :]), strict=True)
            kb.op("vector", lambda e: e.tensor_sub(
                out=bias_c[:, :], in0=bias_c[:, :], in1=acc[:, :]), strict=True)
            kb.op("vector", lambda e, r=r: e.scalar_tensor_tensor(
                out=acc[:, :], in0=bias_c[:, :], scalar=mseg_col[:, r:r + 1],
                in1=acc[:, :], op0=ALU.mult, op1=ALU.add), strict=True)
        t_ab = kb.op("scalar", lambda e: e.activation(
            out=lnabs_c[:, :], in_=acc[:, :], func=AF.Abs),
            waits=[kb.tok("vector")])
        t_ab = kb.op("vector", lambda e: e.tensor_scalar(
            out=lnabs_c[:, :], in0=lnabs_c[:, :], scalar1=1e-38, scalar2=None,
            op0=ALU.max), waits=[t_ab], strict=True)
        t_ab = kb.op("scalar", lambda e: e.activation(
            out=lnabs_c[:, :], in_=lnabs_c[:, :], func=AF.Ln), waits=[t_ab])
        t_sg = kb.op("vector", lambda e: e.tensor_scalar(
            out=sign_c[:, :], in0=acc[:, :], scalar1=0.0, scalar2=None,
            op0=ALU.is_ge), waits=[kb.tok("scalar")], strict=True)
        t_sg = kb.op("vector", lambda e: e.tensor_scalar(
            out=sign_c[:, :], in0=sign_c[:, :], scalar1=2.0, scalar2=-1.0,
            op0=ALU.mult, op1=ALU.add), waits=[t_sg], strict=True)
        kb.barrier()
        kb.scope = 'p2'
        scan_sweep(correction=True)

    if _PHASES >= 4:
        kb.scope = 'p3'
        kb.barrier()
        psc.close()
        p3 = ExitStack()
        wo2_bf = sb("wo2_bf", [P, NDI, DM], BF16, p3)
        h_bf = sb("h_bf", [P, NDI, TC], BF16, p3)
        p3w = ExitStack()
        st_wo1 = hbm("st_wo1", [P, 32, DI], BF16)
        wo_stg = [sb(f"wo_stg{i}", [P, DI], F32, p3w) for i in range(2)]
        wo_bf = sb("wo_bf", [P, DI], BF16, p3w)
        t_pc = None
        for kk in range(32):
            tl = kb.dma(lambda e, kk=kk: e.dma_start(
                out=wo_stg[kk % 2][:, :], in_=w_o1T_h[P * kk:P * (kk + 1), :]),
                waits=[t_pc], stream="wo" + str(kk % 2))
            t_cst = kb.op("vector", lambda e, kk=kk: e.tensor_copy(
                out=wo_bf[:, :], in_=wo_stg[kk % 2][:, :]), waits=[tl, t_pc])
            t_pc = kb.dma(lambda e, kk=kk: e.dma_start(
                out=st_wo1[:, kk, :], in_=wo_bf[:, :]), waits=[t_cst],
                stream="wo2")
        for kk in range(NDI):
            tl = kb.dma(lambda e, kk=kk: e.dma_start(
                out=wo_stg[kk % 2][:, 0:DM], in_=w_o2T_h[P * kk:P * (kk + 1), :]),
                waits=[kb.tok("vector")], stream="wo" + str(kk % 2))
            kb.op("vector", lambda e, kk=kk: e.tensor_copy(
                out=wo2_bf[:, kk, :], in_=wo_stg[kk % 2][:, 0:DM]), waits=[tl])
        kb.barrier()
        p3w.close()

        p3a = ExitStack()
        yin = {k: sb("yin_" + k, [P, NDI, TC], BF16, p3a)
               for k in ("yl", "yc", "u", "zs")}
        ycat = sb("ycat", [P, 32, TC], BF16, p3a)
        wslab = [sb(f"wslab{i}", [P, 32, P], BF16, p3a) for i in range(2)]
        for c in range(NCH):
            cols = slice(TC * c, TC * (c + 1))
            tl = {}
            for k, srcs in (("yl", st_yl), ("yc", st_yc), ("u", st_u),
                            ("zs", st_zs)):
                tl[k] = kb.dma(lambda e, k=k, srcs=srcs, c=c: e.dma_start(
                    out=yin[k][:, :, :], in_=srcs[:, :, TC * c:TC * (c + 1)]),
                    stream="l3" + k)
            t_yf = None
            for m in range(NDI):
                t1 = kb.op("vector", lambda e, m=m: e.scalar_tensor_tensor(
                    out=ycat[:, m, :], in0=yin["u"][:, m, :],
                    scalar=dpc[:, m:m + 1], in1=yin["yl"][:, m, :],
                    op0=ALU.mult, op1=ALU.add),
                    waits=[tl["u"], tl["yl"]])
                t1 = kb.op("vector", lambda e, m=m: e.tensor_add(
                    out=ycat[:, m, :], in0=ycat[:, m, :], in1=yin["yc"][:, m, :]),
                    waits=[tl["yc"]])
                t_yf = kb.op("vector", lambda e, m=m: e.tensor_mul(
                    out=ycat[:, m, :], in0=ycat[:, m, :], in1=yin["zs"][:, m, :]),
                    waits=[tl["zs"]])
            for m in range(NDI):
                bk = m % 2
                tm = kb.op("tensor", lambda e, m=m, bk=bk, c=c: e.matmul(
                    out=ps[bk][:, :], lhsT=lhsT_yt[:, P * m:P * (m + 1)],
                    rhs=cdpu_bf[0:17, TC * c:TC * (c + 1)],
                    start=True, stop=True),
                    waits=[prev.get(("yt", bk)), kb.dtok("scd")])
                te = kb.op("scalar", lambda e, m=m, bk=bk: e.copy(
                    out=ycat[:, NDI + m, :], in_=ps[bk][:, :]), waits=[tm])
                prev[("yt", bk)] = te
            t_h = None
            for m in range(NDI):
                wb = m % 2
                t_ws = kb.dma(lambda e, m=m, wb=wb: e.dma_start(
                    out=wslab[wb][:, :, :],
                    in_=st_wo1[:, :, P * m:P * (m + 1)]),
                    waits=[prev.get(("ws", wb))], stream="lws" + str(wb))
                bk = 2 + m % 2
                tm = None
                for k in range(32):
                    tm = kb.op("tensor", lambda e, m=m, k=k, bk=bk, wb=wb: e.matmul(
                        out=ps[bk][:, :], lhsT=wslab[wb][:, k, :],
                        rhs=ycat[:, k, :], start=(k == 0), stop=(k == 31)),
                        waits=[t_ws, t_yf, kb.tok("scalar"),
                               prev.get(("h", bk))])
                prev[("ws", wb)] = tm
                t_h = kb.op("scalar", lambda e, m=m, bk=bk: e.activation(
                    out=h_bf[:, m, :], in_=ps[bk][:, :], func=AF.Relu), waits=[tm])
                prev[("h", bk)] = t_h
            kb.dma(lambda e, c=c: e.dma_start(
                out=st_h[:, :, TC * c:TC * (c + 1)], in_=h_bf[:, :, :]),
                waits=[t_h, kb.tok("scalar")], stream="sh")
            kb.barrier()

        kb.barrier()
        p3a.close()
        mix_f = sb("mix_f", [P, ND, TC], F32, p3)
        x1_f = sb("x1_f", [P, ND, TC], F32, p3)
        v_f = sb("v_f", [P, ND, TC], F32, p3)
        xr_f = sb("xr_f", [P, ND, TC], F32, p3)
        lnb = sb("lnb", [P, ND, TC], BF16, p3)
        lnsq = sb("lnsq", [P, ND, TC], BF16, p3)
        lst = {k: sb("l3_" + k, [P, TC], F32, p3) for k in ("m", "r", "t1", "t2")}
        vmax_c = sb("vmax_c", [P, 1], F32, p3)
        vmr = sb("vmr", [P, 1], F32, p3)
        kb.op("vector", lambda e: e.memset(vmax_c[:], 0.0))

        def emit_ln_tail(src_f, width, t_src):
            t_c = kb.op("vector", lambda e, w=width: e.tensor_copy(
                out=lnb[:, :, 0:w], in_=src_f[:, :, 0:w]), waits=[t_src])
            t_s = kb.op("scalar", lambda e, w=width: e.activation(
                out=lnsq[:, :, 0:w], in_=lnb[:, :, 0:w], func=AF.Square),
                waits=[t_c])
            tmm = None
            for dk in range(ND):
                tmm = kb.op("tensor", lambda e, dk=dk, w=width: e.matmul(
                    out=ps[4][:, 0:w], lhsT=ones_bf[:], rhs=lnb[:, dk, 0:w],
                    start=(dk == 0), stop=(dk == ND - 1)),
                    waits=[t_c, prev.get(("ln", 4))])
            for dk in range(ND):
                tmm = kb.op("tensor", lambda e, dk=dk, w=width: e.matmul(
                    out=ps[5][:, 0:w], lhsT=ones_bf[:], rhs=lnsq[:, dk, 0:w],
                    start=(dk == 0), stop=(dk == ND - 1)),
                    waits=[t_s, prev.get(("ln", 5))])
            t_m = kb.op("scalar", lambda e, w=width: e.activation(
                out=lst["m"][:, 0:w], in_=ps[4][:, 0:w], func=AF.Identity,
                scale=CINV_DM), waits=[tmm])
            t_v = kb.op("scalar", lambda e, w=width: e.activation(
                out=lst["t2"][:, 0:w], in_=ps[5][:, 0:w], func=AF.Identity,
                scale=CINV_DM), waits=[tmm])
            prev[("ln", 4)] = t_m
            prev[("ln", 5)] = t_v
            t_ms = kb.op("scalar", lambda e, w=width: e.activation(
                out=lst["t1"][:, 0:w], in_=lst["m"][:, 0:w], func=AF.Square),
                waits=[t_m])
            t_var = kb.op("vector", lambda e, w=width: e.tensor_sub(
                out=lst["t2"][:, 0:w], in0=lst["t2"][:, 0:w], in1=lst["t1"][:, 0:w]),
                waits=[t_v, t_ms])
            t_sd = kb.op("scalar", lambda e, w=width: e.activation(
                out=lst["t1"][:, 0:w], in_=lst["t2"][:, 0:w], func=AF.Sqrt,
                bias=CEPS), waits=[t_var])
            t_r = kb.op("vector", lambda e, w=width: e.reciprocal(
                out=lst["r"][:, 0:w], in_=lst["t1"][:, 0:w]), waits=[t_sd])
            return t_r

        for c in range(NCH):
            t_lh = kb.dma(lambda e, c=c: e.dma_start(
                out=h_bf[:, :, :], in_=st_h[:, :, TC * c:TC * (c + 1)]),
                stream="l3h")
            t_lx = kb.dma(lambda e, c=c: e.dma_start(
                out=xr_f[:, :, :],
                in_=x_h[:, 3 + TC * c:3 + TC * (c + 1)]
                .rearrange("(k p) t -> p k t", p=P)), stream="l3x")
            t_mx = None
            for m in range(ND):
                bk = m % 2
                tm = None
                for k in range(NDI):
                    tm = kb.op("tensor", lambda e, m=m, k=k, bk=bk: e.matmul(
                        out=ps[bk][:, :], lhsT=wo2_bf[:, k, P * m:P * (m + 1)],
                        rhs=h_bf[:, k, :], start=(k == 0), stop=(k == NDI - 1)),
                        waits=[t_lh, prev.get(("mx", bk))])
                t_mx = kb.op("scalar", lambda e, m=m, bk=bk: e.copy(
                    out=mix_f[:, m, :], in_=ps[bk][:, :]), waits=[tm])
                prev[("mx", bk)] = t_mx
            t_r2 = emit_ln_tail(mix_f, TC, t_mx)
            t_x1 = None
            for dk in range(ND):
                ta = kb.op("vector", lambda e, dk=dk: e.tensor_sub(
                    out=lst["t1"][:, :], in0=mix_f[:, dk, :], in1=lst["m"][:, :]),
                    waits=[t_r2])
                ta = kb.op("vector", lambda e, dk=dk: e.tensor_mul(
                    out=lst["t1"][:, :], in0=lst["t1"][:, :], in1=lst["r"][:, :]),
                    waits=[])
                t_x1 = kb.op("vector", lambda e, dk=dk: e.tensor_add(
                    out=x1_f[:, dk, :], in0=lst["t1"][:, :], in1=xr_f[:, dk, :]),
                    waits=[t_lx])
            kb.dma(lambda e, c=c: e.dma_start(
                out=st_x1[:, :, TC * c:TC * (c + 1)], in_=x1_f[:, :, :]),
                waits=[t_x1], stream="sx1")
            t_r3 = emit_ln_tail(x1_f, TC, t_x1)
            t_v3 = None
            for dk in range(ND):
                ta = kb.op("vector", lambda e, dk=dk: e.tensor_sub(
                    out=lst["t1"][:, :], in0=x1_f[:, dk, :], in1=lst["m"][:, :]),
                    waits=[t_r3])
                t_v3 = kb.op("vector", lambda e, dk=dk: e.tensor_mul(
                    out=v_f[:, dk, :], in0=lst["t1"][:, :], in1=lst["r"][:, :]),
                    waits=[])
            kb.dma(lambda e, c=c: e.dma_start(
                out=st_v[:, :, TC * c:TC * (c + 1)], in_=v_f[:, :, :]),
                waits=[t_v3], stream="sv")
            t_vm = kb.op("vector", lambda e: e.tensor_reduce(
                out=vmr[:, 0:1], in_=v_f[:, :, :].rearrange("p a b -> p (a b)"),
                axis=AX.X, op=ALU.max, apply_absolute_value=True),
                waits=[t_v3], strict=True)
            kb.op("vector", lambda e: e.tensor_max(
                out=vmax_c[:, 0:1], in0=vmax_c[:, 0:1], in1=vmr[:, 0:1]),
                waits=[t_vm], strict=True)
            kb.barrier()

        if _DBG == "p3":
            dbg["d_x1"] = (hbm("d_x1", [P, ND, SEG], F32, "ExternalOutput"), st_x1)
            dbg["d_v"] = (hbm("d_v", [P, ND, SEG], F32, "ExternalOutput"), st_v)
        es.callback(p3.close)

    if _PHASES >= 5:
        kb.scope = 'ffnw'
        kb.barrier()
        p3.close()
        scol = sb("scol", [P, 24])
        vrow = sb("vrow", [1, P])
        fmr = sb("fmr", [P, 1])
        cc_sc = hbm("cc_sc", [1, P], F32)

        vone = sb("vone", [1, 2])
        ones1f = sb("ones1f", [1, P])
        kb.op("vector", lambda e: e.memset(ones1f[:], 1.0))

        def cross_core_max2(src_cols, cc_in, cc_out, dst_cols, t_src):
            toks = []
            tprev = t_src
            for i, (ten, col) in enumerate(src_cols):
                t1 = kb.dma(lambda e, ten=ten, col=col, i=i: e.dma_start(
                    out=cc_sc[0, 0:P], in_=ten[:, col:col + 1]),
                    waits=[t_src, tprev], stream="xm1")
                t2 = kb.dma(lambda e, i=i: e.dma_start(
                    out=vrow[0:1, :], in_=cc_sc[0, 0:P]), waits=[t1],
                    stream="xm2")
                t3 = kb.op("vector", lambda e: e.tensor_reduce(
                    out=vrow[0:1, 0:1], in_=vrow[0:1, :], axis=AX.X, op=ALU.max),
                    waits=[t2], strict=True)
                t4 = kb.dma(lambda e, i=i: e.dma_start(
                    out=cc_in[0, i:i + 1], in_=vrow[0:1, 0:1],
                    allow_slow_non_contiguous=True),
                    waits=[t3], stream="xm3")
                toks.append(t4)
                tprev = t4
            t_ag = kb.op("gpsimd", lambda e: e.collective_compute(
                "AllReduce", ALU.max, replica_groups=RG8,
                ins=[cc_in[:, :]], outs=[cc_out[:, :]]), waits=toks)
            outs = []
            tprev2 = t_ag
            for i, (ten, col) in enumerate(dst_cols):
                t5 = kb.dma(lambda e, i=i: e.dma_start(
                    out=vone[0:1, i:i + 1], in_=cc_out[0:1, i:i + 1],
                    allow_slow_non_contiguous=True),
                    waits=[t_ag, tprev2], stream="xm4")
                t6 = kb.op("tensor", lambda e, i=i: e.matmul(
                    out=ps[6][:, 0:1], lhsT=ones1f[0:1, :],
                    rhs=vone[0:1, i:i + 1], start=True, stop=True),
                    waits=[t5, kb.tok("scalar")])
                t7 = kb.op("scalar", lambda e, ten=ten, col=col: e.copy(
                    out=ten[:, col:col + 1], in_=ps[6][:, 0:1]), waits=[t6])
                outs.append(t7)
                tprev2 = t7
            kb.barrier()
            return outs

        p4w = ExitStack()
        dqw = [sb(f"dqw{i}", [P, FI], F32, p4w) for i in range(2)]
        dqq = sb("dqq", [P, FI], F32, p4w)
        dqsc = sb("dqsc", [P, FI // 32], F32, p4w)
        dqrs = sb("dqrs", [P, FI // 32], F32, p4w)
        dqbf = [sb(f"dqbf{i}", [P, FI], BF16, p4w) for i in range(2)]
        trbf = sb("trbf", [P, P], BF16, p4w)
        prevq = {}
        for (src_h, dst_h, O, I, jn) in ((wg_h, st_wg, FI, DM, "g"),
                                         (wu_h, st_wu, FI, DM, "u"),
                                         (wd_h, st_wd, DM, FI, "d")):
            for ot in range(O // P):
                bi = (len(prevq) + ot) % 2 if False else ot % 2
                tl = kb.dma(lambda e, s=src_h, ot=ot, bi=bi, I=I: e.dma_start(
                    out=dqw[bi][:, 0:I], in_=s[P * ot:P * (ot + 1), :]),
                    waits=[prevq.get(("dq", bi))], stream="fdl" + str(bi))
                tq = kb.op("vector", lambda e, bi=bi, I=I: e.tensor_reduce(
                    out=dqsc[:, 0:I // 32],
                    in_=dqw[bi][:, 0:I].rearrange("p (b k) -> p b k", k=32),
                    axis=AX.X, op=ALU.max, apply_absolute_value=True),
                    waits=[tl], strict=True)
                tq = kb.op("gpsimd", lambda e, I=I: e.tensor_scalar(
                    out=dqsc[:, 0:I // 32], in0=dqsc[:, 0:I // 32],
                    scalar1=1.0 / 7.0, scalar2=1e-12, op0=ALU.mult, op1=ALU.max),
                    waits=[tq], strict=True)
                tq2 = kb.op("vector", lambda e, I=I: e.reciprocal(
                    out=dqrs[:, 0:I // 32], in_=dqsc[:, 0:I // 32]),
                    waits=[tq], strict=True)
                tq = kb.op("gpsimd", lambda e, bi=bi, I=I: e.tensor_tensor(
                    out=dqq[:, 0:I], in0=dqw[bi][:, 0:I],
                    in1=dqrs[:, 0:I // 32, None].broadcast_to([P, I // 32, 32]),
                    op=ALU.mult), waits=[tq2, prevq.get("dqq")], strict=True)
                tq = kb.op("gpsimd", lambda e, I=I: e.tensor_scalar(
                    out=dqq[:, 0:I], in0=dqq[:, 0:I], scalar1=MAGIC, scalar2=MAGIC,
                    op0=ALU.add, op1=ALU.subtract), waits=[tq], strict=True)
                tq = kb.op("gpsimd", lambda e, I=I: e.tensor_scalar(
                    out=dqq[:, 0:I], in0=dqq[:, 0:I], scalar1=7.0, scalar2=-8.0,
                    op0=ALU.min, op1=ALU.max), waits=[tq], strict=True)
                tq = kb.op("gpsimd", lambda e, bi=bi, I=I: e.tensor_tensor(
                    out=dqbf[bi][:, 0:I], in0=dqq[:, 0:I],
                    in1=dqsc[:, 0:I // 32, None].broadcast_to([P, I // 32, 32]),
                    op=ALU.mult), waits=[tq, prevq.get(("bf", bi))], strict=True)
                prevq[("dq", bi)] = tq
                prevq["dqq"] = tq
                for kk in range(I // P):
                    tt = kb.op("tensor", lambda e, bi=bi, kk=kk: e.transpose(
                        out=ps[7][:, 0:P // 2].bitcast(BF16),
                        in_=dqbf[bi][:, P * kk:P * (kk + 1)],
                        identity=ident_bf[:]),
                        waits=[tq, prevq.get("tr")])
                    te = kb.op("scalar", lambda e: e.copy(
                        out=trbf[:, :], in_=ps[7][:, 0:P // 2].bitcast(BF16)),
                        waits=[tt, kb.dtok("fds")])
                    prevq["tr"] = te
                    prevq[("bf", bi)] = te
                    kb.dma(lambda e, d=dst_h, ot=ot, kk=kk: e.dma_start(
                        out=d[:, kk, P * ot:P * (ot + 1)], in_=trbf[:, :]),
                        waits=[te], stream="fds")
        kb.barrier()
        p4w.close()

        kb.scope = 'p4'
        S1, RS1, GM, UM, SG, SU, RT, RO, TM, OM = range(10)
        t_mx = cross_core_max2([(vmax_c, 0)], cc_mx_in, cc_mx_out,
                               [(scol, S1)], kb.tok("vector"))
        t_s1 = kb.op("vector", lambda e: e.tensor_scalar(
            out=scol[:, S1:S1 + 1], in0=scol[:, S1:S1 + 1],
            scalar1=1.0 / 32767.0, scalar2=1e-12, op0=ALU.mult, op1=ALU.max),
            waits=t_mx, strict=True)
        t_rs1 = kb.op("vector", lambda e: e.reciprocal(
            out=scol[:, RS1:RS1 + 1], in_=scol[:, S1:S1 + 1]),
            waits=[t_s1], strict=True)
        kb.barrier()

        p4 = ExitStack()
        gl_bf = sb("gl_bf", [P, ND, FI], BF16, p4)
        ul_bf = sb("ul_bf", [P, ND, FI], BF16, p4)
        vf4 = sb("vf4", [P, ND, TC], F32, p4)
        kq_bf = sb("kq_bf", [P, ND, TC], BF16, p4)
        kt_f = sb("kt_f", [P, TC], F32, p4)
        fo_bf = sb("fo_bf", [P, NFI, TC], BF16, p4)
        for kk in range(ND):
            t1 = kb.dma(lambda e, kk=kk: e.dma_start(
                out=gl_bf[:, kk, :], in_=st_wg[:, kk, :]), stream="lgw")
            t2 = kb.dma(lambda e, kk=kk: e.dma_start(
                out=ul_bf[:, kk, :], in_=st_wu[:, kk, :]), stream="luw")
        t_wld = [kb.dtok("lgw"), kb.dtok("luw")]
        kb.op("vector", lambda e: e.memset(scol[:, GM:GM + 1], 0.0))
        kb.op("vector", lambda e: e.memset(scol[:, UM:UM + 1], 0.0))
        for c in range(NCH):
            t_lv = kb.dma(lambda e, c=c: e.dma_start(
                out=vf4[:, :, :], in_=st_v[:, :, TC * c:TC * (c + 1)]),
                waits=[prev.get("vwar")], stream="l4v")
            t_k = None
            for dk in range(ND):
                tk = kb.op("vector", lambda e, dk=dk: e.tensor_scalar(
                    out=kt_f[:, :], in0=vf4[:, dk, :],
                    scalar1=scol[:, RS1:RS1 + 1], scalar2=MAGIC,
                    op0=ALU.mult, op1=ALU.add),
                    waits=[t_lv, t_rs1], strict=(dk == 0))
                tk = kb.op("vector", lambda e, dk=dk: e.tensor_scalar(
                    out=kt_f[:, :], in0=kt_f[:, :], scalar1=MAGIC,
                    scalar2=32767.0, op0=ALU.subtract, op1=ALU.min), waits=[tk])
                t_k = kb.op("vector", lambda e, dk=dk: e.tensor_scalar(
                    out=kq_bf[:, dk, :], in0=kt_f[:, :], scalar1=-32768.0,
                    scalar2=None, op0=ALU.max), waits=[tk])
            prev["vwar"] = t_k
            for (wten, sto, mcol, st_nm) in ((gl_bf, st_gate, GM, "s4g"),
                                             (ul_bf, st_up, UM, "s4u")):
                t_fo = None
                fo_war = prev.get("fo_war") or []
                for o in range(NFI):
                    bk = o % 2
                    tm = None
                    for k in range(ND):
                        tm = kb.op("tensor", lambda e, o=o, k=k, bk=bk, w=wten: e.matmul(
                            out=ps[bk][:, :], lhsT=w[:, k, P * o:P * (o + 1)],
                            rhs=kq_bf[:, k, :], start=(k == 0), stop=(k == ND - 1)),
                            waits=[t_k, prev.get(("f4", bk))] + t_wld)
                    t_fo = kb.op("scalar", lambda e, o=o, bk=bk: e.activation(
                        out=fo_bf[:, o, :], in_=ps[bk][:, :], func=AF.Identity,
                        scale=scol[:, S1:S1 + 1]), waits=[tm, t_s1] + fo_war)
                    prev[("f4", bk)] = t_fo
                t_st = kb.dma(lambda e, c=c, sto=sto: e.dma_start(
                    out=sto[:, :, TC * c:TC * (c + 1)], in_=fo_bf[:, :, :]),
                    waits=[t_fo], stream=st_nm)
                t_mr = kb.op("vector", lambda e: e.tensor_reduce(
                    out=fmr[:, 0:1],
                    in_=fo_bf[:, :, :].rearrange("p a b -> p (a b)"),
                    axis=AX.X, op=ALU.max, apply_absolute_value=True),
                    waits=[t_fo], strict=True)
                kb.op("vector", lambda e, mcol=mcol: e.tensor_max(
                    out=scol[:, mcol:mcol + 1], in0=scol[:, mcol:mcol + 1],
                    in1=fmr[:, 0:1]), waits=[t_mr], strict=True)
                prev["fo_war"] = [t_st, t_mr]
            kb.barrier()
        kb.barrier()
        p4.close()

        kb.scope = 'p5'
        t_mx = cross_core_max2([(scol, GM), (scol, UM)], cc_m2_in, cc_m2_out,
                               [(scol, GM), (scol, UM)], kb.tok("gpsimd"))
        t_sg = kb.op("vector", lambda e: e.tensor_scalar(
            out=scol[:, SG:SG + 1], in0=scol[:, GM:GM + 1],
            scalar1=1.0 / 32767.0, scalar2=1e-12, op0=ALU.mult, op1=ALU.max),
            waits=t_mx, strict=True)
        t_su = kb.op("vector", lambda e: e.tensor_scalar(
            out=scol[:, SU:SU + 1], in0=scol[:, UM:UM + 1],
            scalar1=1.0 / 32767.0, scalar2=1e-12, op0=ALU.mult, op1=ALU.max),
            waits=t_mx, strict=True)
        t_rg = kb.op("vector", lambda e: e.reciprocal(
            out=scol[:, GM:GM + 1], in_=scol[:, SG:SG + 1]), waits=[t_sg],
            strict=True)
        t_ru = kb.op("vector", lambda e: e.reciprocal(
            out=scol[:, UM:UM + 1], in_=scol[:, SU:SU + 1]), waits=[t_su],
            strict=True)
        kb.barrier()

        p5 = ExitStack()
        gin = sb("gin", [P, NFI, TC], BF16, p5)
        kgf = sb("kgf", [P, NFI, TC], F32, p5)
        sgm = sb("sgm", [P, NFI, TC], F32, p5)
        t2o = sb("t2o", [P, NFI, TC], BF16, p5)
        kb.op("vector", lambda e: e.memset(scol[:, TM:TM + 1], 0.0))
        flatn = lambda ap: ap.rearrange("p a b -> p (a b)")
        for c in range(NCH):
            t_lg = kb.dma(lambda e, c=c: e.dma_start(
                out=gin[:, :, :], in_=st_gate[:, :, TC * c:TC * (c + 1)]),
                waits=[prev.get("p5war")], stream="l5g")
            tk = kb.op("vector", lambda e: e.tensor_scalar(
                out=flatn(kgf[:, :, :]), in0=flatn(gin[:, :, :]),
                scalar1=scol[:, GM:GM + 1], scalar2=MAGIC,
                op0=ALU.mult, op1=ALU.add), waits=[t_lg, t_rg], strict=True)
            tk = kb.op("vector", lambda e: e.tensor_scalar(
                out=flatn(kgf[:, :, :]), in0=flatn(kgf[:, :, :]), scalar1=MAGIC,
                scalar2=32767.0, op0=ALU.subtract, op1=ALU.min), waits=[tk])
            tk = kb.op("vector", lambda e: e.tensor_scalar(
                out=flatn(kgf[:, :, :]), in0=flatn(kgf[:, :, :]),
                scalar1=-32768.0, scalar2=None, op0=ALU.max), waits=[tk])
            t_sig = kb.op("scalar", lambda e: e.activation(
                out=flatn(sgm[:, :, :]), in_=flatn(kgf[:, :, :]), func=AF.Sigmoid,
                scale=scol[:, SG:SG + 1]), waits=[tk, t_sg])
            t_sig = kb.op("vector", lambda e: e.tensor_scalar(
                out=flatn(sgm[:, :, :]), in0=flatn(sgm[:, :, :]),
                scalar1=65536.0, scalar2=MAGIC, op0=ALU.mult, op1=ALU.add),
                waits=[t_sig], strict=True)
            t_sig = kb.op("vector", lambda e: e.tensor_scalar(
                out=flatn(sgm[:, :, :]), in0=flatn(sgm[:, :, :]), scalar1=MAGIC,
                scalar2=65535.0, op0=ALU.subtract, op1=ALU.min), waits=[t_sig])
            t_t2 = kb.op("vector", lambda e: e.tensor_mul(
                out=flatn(t2o[:, :, :]), in0=flatn(kgf[:, :, :]),
                in1=flatn(sgm[:, :, :])), waits=[t_sig])
            t_st = kb.dma(lambda e, c=c: e.dma_start(
                out=st_t2[:, :, TC * c:TC * (c + 1)], in_=t2o[:, :, :]),
                waits=[t_t2], stream="s5t")
            t_mr = kb.op("vector", lambda e: e.tensor_reduce(
                out=fmr[:, 0:1], in_=flatn(t2o[:, :, :]),
                axis=AX.X, op=ALU.max, apply_absolute_value=True),
                waits=[t_t2], strict=True)
            kb.op("vector", lambda e: e.tensor_max(
                out=scol[:, TM:TM + 1], in0=scol[:, TM:TM + 1], in1=fmr[:, 0:1]),
                waits=[t_mr], strict=True)
            t_lu = kb.dma(lambda e, c=c: e.dma_start(
                out=gin[:, :, :], in_=st_up[:, :, TC * c:TC * (c + 1)]),
                waits=[tk, t_mr], stream="l5u")
            tu = kb.op("vector", lambda e: e.tensor_scalar(
                out=flatn(kgf[:, :, :]), in0=flatn(gin[:, :, :]),
                scalar1=scol[:, UM:UM + 1], scalar2=MAGIC,
                op0=ALU.mult, op1=ALU.add), waits=[t_lu, t_ru, t_t2], strict=True)
            tu = kb.op("vector", lambda e: e.tensor_scalar(
                out=flatn(kgf[:, :, :]), in0=flatn(kgf[:, :, :]), scalar1=MAGIC,
                scalar2=32767.0, op0=ALU.subtract, op1=ALU.min), waits=[tu])
            tu = kb.op("vector", lambda e: e.tensor_scalar(
                out=flatn(t2o[:, :, :]), in0=flatn(kgf[:, :, :]),
                scalar1=-32768.0, scalar2=None, op0=ALU.max),
                waits=[tu, t_st, t_mr])
            t_sku = kb.dma(lambda e, c=c: e.dma_start(
                out=st_ku[:, :, TC * c:TC * (c + 1)], in_=t2o[:, :, :]),
                waits=[tu, t_st], stream="s5k")
            prev["p5war"] = t_sku
            kb.barrier()
        kb.barrier()
        p5.close()

        kb.scope = 'p6'
        t_mx = cross_core_max2([(scol, TM)], cc_m3_in, cc_m3_out,
                               [(scol, TM)], kb.tok("gpsimd"))
        kb.barrier()
        t_t = kb.op("vector", lambda e: e.tensor_scalar(
            out=scol[:, RT:RT + 1], in0=scol[:, TM:TM + 1],
            scalar1=scol[:, SG:SG + 1], scalar2=2.0 ** -16 / 32767.0,
            op0=ALU.mult, op1=ALU.mult), waits=t_mx, strict=True)
        t_t = kb.op("vector", lambda e: e.tensor_scalar(
            out=scol[:, RT:RT + 1], in0=scol[:, RT:RT + 1], scalar1=1e-12,
            scalar2=None, op0=ALU.max), waits=[t_t], strict=True)
        t_t2s = kb.op("vector", lambda e: e.reciprocal(
            out=fmr[:, 0:1], in_=scol[:, RT:RT + 1]), waits=[t_t], strict=True)
        t_rt = kb.op("vector", lambda e: e.tensor_scalar(
            out=scol[:, TM:TM + 1], in0=fmr[:, 0:1],
            scalar1=scol[:, SG:SG + 1], scalar2=2.0 ** -16,
            op0=ALU.mult, op1=ALU.mult), waits=[t_t2s], strict=True)
        kb.barrier()

        p6 = ExitStack()
        t2i = sb("t2i", [P, NFI, TC], BF16, p6)
        kui = sb("kui", [P, NFI, TC], BF16, p6)
        k2f = sb("k2f", [P, NFI, TC], F32, p6)
        o2o = sb("o2o", [P, NFI, TC], BF16, p6)
        kb.op("vector", lambda e: e.memset(scol[:, OM:OM + 1], 0.0))
        for c in range(NCH):
            t_l1 = kb.dma(lambda e, c=c: e.dma_start(
                out=t2i[:, :, :], in_=st_t2[:, :, TC * c:TC * (c + 1)]),
                waits=[prev.get("p6war")], stream="l6t")
            t_l2 = kb.dma(lambda e, c=c: e.dma_start(
                out=kui[:, :, :], in_=st_ku[:, :, TC * c:TC * (c + 1)]),
                waits=[prev.get("p6war")], stream="l6k")
            tk = kb.op("vector", lambda e: e.tensor_scalar(
                out=flatn(k2f[:, :, :]), in0=flatn(t2i[:, :, :]),
                scalar1=scol[:, TM:TM + 1], scalar2=MAGIC,
                op0=ALU.mult, op1=ALU.add), waits=[t_l1, t_rt], strict=True)
            tk = kb.op("vector", lambda e: e.tensor_scalar(
                out=flatn(k2f[:, :, :]), in0=flatn(k2f[:, :, :]), scalar1=MAGIC,
                scalar2=32767.0, op0=ALU.subtract, op1=ALU.min), waits=[tk])
            tk = kb.op("vector", lambda e: e.tensor_scalar(
                out=flatn(k2f[:, :, :]), in0=flatn(k2f[:, :, :]),
                scalar1=-32768.0, scalar2=None, op0=ALU.max), waits=[tk])
            t_o2 = kb.op("vector", lambda e: e.tensor_mul(
                out=flatn(o2o[:, :, :]), in0=flatn(k2f[:, :, :]),
                in1=flatn(kui[:, :, :])), waits=[tk, t_l2])
            t_st = kb.dma(lambda e, c=c: e.dma_start(
                out=st_o2[:, :, TC * c:TC * (c + 1)], in_=o2o[:, :, :]),
                waits=[t_o2], stream="s6o")
            prev["p6war"] = t_st
            t_mr = kb.op("vector", lambda e: e.tensor_reduce(
                out=fmr[:, 0:1], in_=flatn(o2o[:, :, :]),
                axis=AX.X, op=ALU.max, apply_absolute_value=True),
                waits=[t_o2], strict=True)
            kb.op("vector", lambda e: e.tensor_max(
                out=scol[:, OM:OM + 1], in0=scol[:, OM:OM + 1], in1=fmr[:, 0:1]),
                waits=[t_mr], strict=True)
            kb.barrier()
        kb.barrier()
        p6.close()

        kb.scope = 'p7'
        t_mx = cross_core_max2([(scol, OM)], cc_m4_in, cc_m4_out,
                               [(scol, OM)], kb.tok("gpsimd"))
        t_o = kb.op("vector", lambda e: e.tensor_scalar(
            out=scol[:, RO:RO + 1], in0=scol[:, RT:RT + 1],
            scalar1=scol[:, SU:SU + 1], scalar2=None, op0=ALU.mult),
            waits=t_mx, strict=True)
        t_o = kb.op("vector", lambda e: e.tensor_scalar(
            out=scol[:, OM:OM + 1], in0=scol[:, OM:OM + 1],
            scalar1=scol[:, RO:RO + 1], scalar2=1.0 / 32767.0,
            op0=ALU.mult, op1=ALU.mult), waits=[t_o], strict=True)
        t_o = kb.op("vector", lambda e: e.tensor_scalar(
            out=scol[:, OM:OM + 1], in0=scol[:, OM:OM + 1], scalar1=1e-12,
            scalar2=None, op0=ALU.max), waits=[t_o], strict=True)
        t_or = kb.op("vector", lambda e: e.reciprocal(
            out=fmr[:, 0:1], in_=scol[:, OM:OM + 1]), waits=[t_o], strict=True)
        t_ro = kb.op("vector", lambda e: e.tensor_scalar(
            out=scol[:, RO:RO + 1], in0=fmr[:, 0:1],
            scalar1=scol[:, RO:RO + 1], scalar2=None, op0=ALU.mult),
            waits=[t_or], strict=True)
        kb.barrier()

        p7 = ExitStack()
        dl_bf = sb("dl_bf", [P, NFI, DM], BF16, p7)
        o2i = sb("o2i", [P, NFI, TC], BF16, p7)
        kof = sb("kof", [P, TC], F32, p7)
        ko_bf = sb("ko_bf", [P, NFI, TC], BF16, p7)
        ffn_f = sb("ffn_f", [P, ND, TC], F32, p7)
        x1r = sb("x1r", [P, ND, TC], F32, p7)
        out_f = sb("out_f", [P, ND, TC], F32, p7)
        l4b = sb("l4b", [P, ND, TC], BF16, p7)
        l4sq = sb("l4sq", [P, ND, TC], BF16, p7)
        l4st = {k: sb("l4_" + k, [P, TC], F32, p7) for k in ("m", "r", "t1", "t2")}
        for kk in range(NFI):
            kb.dma(lambda e, kk=kk: e.dma_start(
                out=dl_bf[:, kk, :], in_=st_wd[:, kk, :]), stream="ldw")
        t_dld = kb.dtok("ldw")

        def emit_ln4(src_f, t_src):
            t_c = kb.op("vector", lambda e: e.tensor_copy(
                out=l4b[:, :, :], in_=src_f[:, :, :]), waits=[t_src])
            t_s = kb.op("scalar", lambda e: e.activation(
                out=l4sq[:, :, :], in_=l4b[:, :, :], func=AF.Square), waits=[t_c])
            tmm = None
            for dk in range(ND):
                tmm = kb.op("tensor", lambda e, dk=dk: e.matmul(
                    out=ps[4][:, :], lhsT=ones_bf[:], rhs=l4b[:, dk, :],
                    start=(dk == 0), stop=(dk == ND - 1)),
                    waits=[t_c, prev.get(("l4", 4))])
            for dk in range(ND):
                tmm = kb.op("tensor", lambda e, dk=dk: e.matmul(
                    out=ps[5][:, :], lhsT=ones_bf[:], rhs=l4sq[:, dk, :],
                    start=(dk == 0), stop=(dk == ND - 1)),
                    waits=[t_s, prev.get(("l4", 5))])
            t_m = kb.op("scalar", lambda e: e.activation(
                out=l4st["m"][:, :], in_=ps[4][:, :], func=AF.Identity,
                scale=CINV_DM), waits=[tmm])
            t_v = kb.op("scalar", lambda e: e.activation(
                out=l4st["t2"][:, :], in_=ps[5][:, :], func=AF.Identity,
                scale=CINV_DM), waits=[tmm])
            prev[("l4", 4)] = t_m
            prev[("l4", 5)] = t_v
            t_ms = kb.op("scalar", lambda e: e.activation(
                out=l4st["t1"][:, :], in_=l4st["m"][:, :], func=AF.Square),
                waits=[t_m])
            t_var = kb.op("vector", lambda e: e.tensor_sub(
                out=l4st["t2"][:, :], in0=l4st["t2"][:, :], in1=l4st["t1"][:, :]),
                waits=[t_v, t_ms])
            t_sd = kb.op("scalar", lambda e: e.activation(
                out=l4st["t1"][:, :], in_=l4st["t2"][:, :], func=AF.Sqrt,
                bias=CEPS), waits=[t_var])
            return kb.op("vector", lambda e: e.reciprocal(
                out=l4st["r"][:, :], in_=l4st["t1"][:, :]), waits=[t_sd])

        for c in range(NCH):
            t_l1 = kb.dma(lambda e, c=c: e.dma_start(
                out=o2i[:, :, :], in_=st_o2[:, :, TC * c:TC * (c + 1)]),
                waits=[prev.get("p7war")], stream="l7o")
            tk = None
            for o in range(NFI):
                tk = kb.op("vector", lambda e, o=o: e.tensor_scalar(
                    out=kof[:, :], in0=o2i[:, o, :],
                    scalar1=scol[:, RO:RO + 1], scalar2=MAGIC,
                    op0=ALU.mult, op1=ALU.add), waits=[t_l1, t_ro],
                    strict=(o == 0))
                tk = kb.op("vector", lambda e, o=o: e.tensor_scalar(
                    out=kof[:, :], in0=kof[:, :], scalar1=MAGIC,
                    scalar2=32767.0, op0=ALU.subtract, op1=ALU.min), waits=[tk])
                tk = kb.op("vector", lambda e, o=o: e.tensor_scalar(
                    out=ko_bf[:, o, :], in0=kof[:, :],
                    scalar1=-32768.0, scalar2=None, op0=ALU.max), waits=[tk])
            prev["p7war"] = tk
            t_ff = None
            for m in range(ND):
                bk = m % 2
                tm = None
                for k in range(NFI):
                    tm = kb.op("tensor", lambda e, m=m, k=k, bk=bk: e.matmul(
                        out=ps[bk][:, :], lhsT=dl_bf[:, k, P * m:P * (m + 1)],
                        rhs=ko_bf[:, k, :], start=(k == 0), stop=(k == NFI - 1)),
                        waits=[tk, t_dld, prev.get(("p7", bk))])
                t_ff = kb.op("scalar", lambda e, m=m, bk=bk: e.activation(
                    out=ffn_f[:, m, :], in_=ps[bk][:, :], func=AF.Identity,
                    scale=scol[:, OM:OM + 1]), waits=[tm, t_o])
                prev[("p7", bk)] = t_ff
            t_lx1 = kb.dma(lambda e, c=c: e.dma_start(
                out=x1r[:, :, :], in_=st_x1[:, :, TC * c:TC * (c + 1)]),
                waits=[prev.get("p7xwar")], stream="l7x")
            t_r4 = emit_ln4(ffn_f, t_ff)
            t_out = None
            for dk in range(ND):
                ta = kb.op("vector", lambda e, dk=dk: e.tensor_sub(
                    out=l4st["t1"][:, :], in0=ffn_f[:, dk, :], in1=l4st["m"][:, :]),
                    waits=[t_r4])
                ta = kb.op("vector", lambda e, dk=dk: e.tensor_mul(
                    out=l4st["t1"][:, :], in0=l4st["t1"][:, :], in1=l4st["r"][:, :]),
                    waits=[])
                t_out = kb.op("vector", lambda e, dk=dk: e.tensor_add(
                    out=out_f[:, dk, :], in0=l4st["t1"][:, :], in1=x1r[:, dk, :]),
                    waits=[t_lx1])
            prev["p7xwar"] = t_out
            kb.dma(lambda e, c=c: e.dma_start(
                out=out_h[:, :, TC * c:TC * (c + 1)], in_=out_f[:, :, :]),
                waits=[t_out], stream="sout")
            kb.barrier()
        es.callback(p7.close)

    if _DBG == "p4":
        d_scol_h = hbm("d_scol", [P, 24], F32, "ExternalOutput")
        kb.dma(lambda e: e.dma_start(out=d_scol_h[:, :], in_=scol[:, :]),
               stream="dsc")
    if dbg:
        dts = []
        for nm, (oh, src) in dbg.items():
            dts.append(kb.dma(lambda e, o=oh, s=src: e.dma_start(
                out=o[:, :, :], in_=s[:, :, :]), stream="dbg" + nm))
        kb.op("gpsimd", None, waits=dts, inc=0)

    if _PHASES < 5:
        kb.dma(lambda e: e.dma_start(out=out_h[:, 0, 0:1], in_=hm_col[:, 0:1],
                                     allow_slow_non_contiguous=True),
               waits=[kb.tok("vector")], stream="out")
    kb.op("gpsimd", None,
          waits=[(s, cv) for (s, cv) in kb.dsem.values()], inc=0)

    kb.finalize()
    psc.close()
    es.close()
    return nc


def prepare_inputs(inputs):
    x = inputs["L0_attn_x"]
    W_in = inputs["L0_attn_W_in"]
    conv_w = inputs["L0_attn_conv_w"]
    conv_b = inputs["L0_attn_conv_b"]
    W_x = inputs["L0_attn_W_x"]
    W_dt = inputs["L0_attn_W_dt"]
    b_dt = inputs["L0_attn_b_dt"]
    A_log = inputs["L0_attn_A_log"]
    Dp = inputs["L0_attn_Dp"]

    C = np.ascontiguousarray
    blkfull = np.zeros((P, NDI, P), np.float32)
    for jj in range(NDI):
        for p in range(P):
            blkfull[p, jj, 8 * jj + p // NST] = 1.0
    shared = dict(
        w_inT=C(W_in.T),
        w_xTd=C(np.concatenate([W_x.T, Dp[:, None]], 1)),
        w_dtT=C(W_dt.T),
        convw_t=C(conv_w.reshape(NDI, P, 4).transpose(1, 0, 2)),
        convb_t=C(conv_b.reshape(NDI, P).T),
        bdt_t=C(b_dt.reshape(NDI, P).T),
        dp_t=C(Dp.reshape(NDI, P).T),
        alog_cols=C(A_log.reshape(NT, 8, NST).transpose(1, 2, 0).reshape(P, NT)),
        alogT=C(A_log.T),
        w_o1T=C(inputs["L0_attn_W_out1"].T),
        w_o2T=C(inputs["L0_attn_W_out2"].T),
        wg=C(inputs["L0_ffn_w_gate"]), wu=C(inputs["L0_ffn_w_up"]),
        wd=C(inputs["L0_ffn_w_down"]),
        blkfull=C(blkfull.reshape(P, NDI * P)),
        sel8full=C(np.stack(
            [(np.arange(P)[:, None] == 8 * jj + np.arange(P)[None, :] // NST)
             .astype(np.float32) for jj in range(NDI)], axis=1)
            .reshape(P, NDI * P)),
        ident=np.eye(P, dtype=np.float32),
    )
    ins = []
    for c in range(8):
        b, q = c // 4, c % 4
        lo = q * SEG
        halo = x[b, 0:3, :] if q == 0 else x[b, lo - 3:lo, :]
        seg = np.concatenate([halo, x[b, lo:lo + SEG, :]], 0)
        d = dict(shared)
        d["x_seg"] = C(seg.T)
        d["hmask"] = np.full((P, 1), 0.0 if q == 0 else 1.0, np.float32)
        d["mseg8"] = np.tile(np.array([[1.0 if (j // 4 == b and j % 4 < q)
                                        else 0.0 for j in range(8)]], np.float32),
                             (P, 1))
        ins.append(d)
    return ins


def kernel(**inputs):
    ins = prepare_inputs(inputs)
    nc = build_program()
    res = run_bass_kernel_spmd(nc, ins, core_ids=list(range(8)))
    out = np.zeros((2, 16384, 1024), np.float32)
    for c in range(8):
        b, q = c // 4, c % 4
        arr = np.asarray(res.results[c]["xout"], dtype=np.float32)
        out[b, q * SEG:(q + 1) * SEG, :] = arr.transpose(1, 0, 2).reshape(DM, SEG).T
    return out


def run_debug(inputs, trace=False):
    ins = prepare_inputs(inputs)
    nc = build_program()
    return run_bass_kernel_spmd(nc, ins, core_ids=list(range(8)), trace=trace)


# version 04 (final)
# speedup vs baseline: 1.4584x; 1.0618x over previous
import numpy as np
from contextlib import ExitStack

import concourse.bass as bass
import concourse.mybir as mybir
from concourse.bass_utils import run_bass_kernel_spmd

F32 = mybir.dt.float32
BF16 = mybir.dt.bfloat16
AF = mybir.ActivationFunctionType
ALU = mybir.AluOpType
AX = mybir.AxisListType

P = 128
SEG = 4096
TH = SEG + 3
TC = 512
NCH = SEG // TC
DM = 1024
ND = DM // P
DI = 2048
NDI = DI // P
E4 = 4096
NST = 16
DTR = 64
NT = DI * NST // P
FI = 3072
NFI = FI // P
MAGIC = 12582912.0
RG8 = [[0, 1, 2, 3, 4, 5, 6, 7]]

_DBG = ""
_PHASES = 5


class KB:
    ENG = ("sync", "vector", "scalar", "tensor", "gpsimd")

    def __init__(self, nc, es):
        self.nc = nc
        self.es = es
        self.sem = {e: es.enter_context(nc.semaphore("sp_" + e)) for e in self.ENG}
        self.cnt = {e: 0 for e in self.ENG}
        self.ops = {e: [] for e in self.ENG}
        self._last_wait = {}
        self.dsem = {}
        self.scope = "p0"

    def op(self, eng, fn, waits=(), inc=1, strict=False):
        ws = []
        if strict:
            ws.append((self.sem[eng], self.cnt[eng]))
        for t in waits:
            if t is None:
                continue
            (s, v) = t
            if s is self.sem[eng]:
                continue
            key = (eng, s.name)
            if self._last_wait.get(key, -1) >= v:
                continue
            self._last_wait[key] = v
            ws.append((s, v))
        self.cnt[eng] += inc
        self.ops[eng].append((fn, tuple(ws), inc, self.scope))
        return (self.sem[eng], self.cnt[eng])

    def dma(self, fn, waits=(), stream="misc", eng="sync"):
        if stream not in self.dsem:
            self.dsem[stream] = [self.es.enter_context(
                self.nc.semaphore("sd_" + stream)), 0]
        sem, cntv = self.dsem[stream]
        self.dsem[stream][1] = cntv + 16
        tok = (sem, cntv + 16)

        def wrapped(e, fn=fn, sem=sem):
            inst = fn(e)
            inst.then_inc(sem, 16)
            return inst
        self.op(eng, wrapped, waits, inc=0)
        return tok

    def dtok(self, stream):
        if stream not in self.dsem:
            return None
        sem, cntv = self.dsem[stream]
        return (sem, cntv)

    def tok(self, eng):
        return (self.sem[eng], self.cnt[eng])

    def toks(self):
        return [self.tok(e) for e in self.ENG]

    def barrier(self):
        toks = self.toks() + [(s, c) for (s, c) in self.dsem.values()]
        for e in self.ENG:
            self.op(e, None, waits=toks, inc=0)

    def finalize(self):
        with self.nc.Block() as block:
            for e in self.ENG:
                getattr(block, e)(self._mk_runner(e))

    def _mk_runner(self, e):
        ops, sem = self.ops[e], self.sem[e]
        nc = self.nc

        def run(eng):
            for fn, waits, inc, scope in ops:
                with nc.named_scope(scope):
                    for (s, v) in waits:
                        eng.wait_ge(s, v)
                    if fn is None:
                        continue
                    inst = fn(eng)
                    if inc:
                        assert inst is not None, f"op on {e} returned None"
                        inst.then_inc(sem, inc)
        return run


def build_program():
    nc = bass.Bass("TRN2", target_bir_lowering=False, debug=False,
                   detect_race_conditions=False)
    es = ExitStack()
    kb = KB(nc, es)

    def hbm(name, shape, dt=F32, kind="Internal"):
        return nc.dram_tensor(name, shape, dt, kind=kind).ap()

    def sb(name, shape, dt=F32, stack=None):
        return (stack or es).enter_context(nc.sbuf_tensor(name, shape, dt))

    x_h = hbm("x_seg", [DM, TH], F32, "ExternalInput")
    w_inT_h = hbm("w_inT", [DM, E4], F32, "ExternalInput")
    w_xTd_h = hbm("w_xTd", [DI, 97], F32, "ExternalInput")
    w_dtT_h = hbm("w_dtT", [DTR, DI], F32, "ExternalInput")
    convw_h = hbm("convw_t", [P, NDI, 4], F32, "ExternalInput")
    convb_h = hbm("convb_t", [P, NDI], F32, "ExternalInput")
    bdt_h = hbm("bdt_t", [P, NDI], F32, "ExternalInput")
    dp_h = hbm("dp_t", [P, NDI], F32, "ExternalInput")
    alogc_h = hbm("alog_cols", [P, NT], F32, "ExternalInput")
    alogT_h = hbm("alogT", [NST, DI], F32, "ExternalInput")
    w_o1T_h = hbm("w_o1T", [E4, DI], F32, "ExternalInput")
    w_o2T_h = hbm("w_o2T", [DI, DM], F32, "ExternalInput")
    wg_h = hbm("wg", [FI, DM], F32, "ExternalInput")
    wu_h = hbm("wu", [FI, DM], F32, "ExternalInput")
    wd_h = hbm("wd", [DM, FI], F32, "ExternalInput")
    blkfull_h = hbm("blkfull", [P, NDI * P], F32, "ExternalInput")
    sel8f_h = hbm("sel8full", [P, NDI * P], F32, "ExternalInput")
    ident_h = hbm("ident", [P, P], F32, "ExternalInput")
    hmask_h = hbm("hmask", [P, 1], F32, "ExternalInput")
    mseg8_h = hbm("mseg8", [P, 8], F32, "ExternalInput")
    out_h = hbm("xout", [P, ND, SEG], F32, "ExternalOutput")

    st_delta = hbm("st_delta", [P, NDI, SEG], BF16)
    st_w = hbm("st_w", [P, NDI, SEG], BF16)
    st_u = hbm("st_u", [P, NDI, SEG], BF16)
    st_zs = hbm("st_zs", [P, NDI, SEG], BF16)
    st_yl = hbm("st_yl", [P, NDI, SEG], BF16)
    st_yc = hbm("st_yc", [P, NDI, SEG], BF16)
    st_x1 = hbm("st_x1", [P, ND, SEG], F32)
    st_v = hbm("st_v", [P, ND, SEG], F32)
    st_h = hbm("st_h", [P, NDI, SEG], BF16)

    st_wg = hbm("st_wg", [P, ND, FI], BF16)
    st_wu = hbm("st_wu", [P, ND, FI], BF16)
    st_wd = hbm("st_wd", [P, NFI, DM], BF16)
    st_gate = hbm("st_gate", [P, NFI, SEG], BF16)
    st_up = hbm("st_up", [P, NFI, SEG], BF16)
    st_ku = hbm("st_ku", [P, NFI, SEG], BF16)
    st_t2 = hbm("st_t2", [P, NFI, SEG], BF16)
    st_o2 = hbm("st_o2", [P, NFI, SEG], BF16)

    cc_rp_in = hbm("cc_rp_in", [1, 2 * P * NT], F32)
    cc_rp_out = hbm("cc_rp_out", [8, 2 * P * NT], F32)
    cc_mx_in = hbm("cc_mx_in", [1, 2], F32)
    cc_mx_out = hbm("cc_mx_out", [1, 2], F32)
    cc_m2_in = hbm("cc_m2_in", [1, 2], F32)
    cc_m2_out = hbm("cc_m2_out", [1, 2], F32)
    cc_m3_in = hbm("cc_m3_in", [1, 2], F32)
    cc_m3_out = hbm("cc_m3_out", [1, 2], F32)
    cc_m4_in = hbm("cc_m4_in", [1, 2], F32)
    cc_m4_out = hbm("cc_m4_out", [1, 2], F32)

    dbg = {}
    if _DBG == "p1":
        for nm, src in (("d_delta", st_delta), ("d_w", st_w),
                        ("d_u", st_u), ("d_zs", st_zs)):
            dbg[nm] = (hbm(nm, [P, NDI, SEG], BF16, "ExternalOutput"), src)
    if _DBG in ("p15", "p2"):
        dbg["d_yl"] = (hbm("d_yl", [P, NDI, SEG], BF16, "ExternalOutput"), st_yl)
    if _DBG == "p2":
        dbg["d_yc"] = (hbm("d_yc", [P, NDI, SEG], BF16, "ExternalOutput"), st_yc)
    if _DBG == "p4":
        dbg["d_gate"] = (hbm("d_gate", [P, NFI, SEG], BF16, "ExternalOutput"), st_gate)
        dbg["d_up"] = (hbm("d_up", [P, NFI, SEG], BF16, "ExternalOutput"), st_up)
        dbg["d_wg"] = (hbm("d_wg", [P, ND, FI], BF16, "ExternalOutput"), st_wg)
        dbg["d_t2"] = (hbm("d_t2", [P, NFI, SEG], BF16, "ExternalOutput"), st_t2)
        dbg["d_o2"] = (hbm("d_o2", [P, NFI, SEG], BF16, "ExternalOutput"), st_o2)
        dbg["d_ku"] = (hbm("d_ku", [P, NFI, SEG], BF16, "ExternalOutput"), st_ku)
        d_sgn_h = hbm("d_sgn", [P, NT], F32, "ExternalOutput")
        d_lab_h = hbm("d_lab", [P, NT], F32, "ExternalOutput")
        d_sin_h = hbm("d_sin", [P, NT], F32, "ExternalOutput")
        d_e2_h = hbm("d_e2", [P, TC], F32, "ExternalOutput")
        d_dbc_h = hbm("d_dbc", [P, TC], BF16, "ExternalOutput")
        d_sgs_h = hbm("d_sgs", [P, P], BF16, "ExternalOutput")

    bcd_bf = sb("bcd_bf", [P, SEG], BF16)
    cdpu_bf = sb("cdpu_bf", [17, SEG], BF16)
    a_cols = sb("a_cols", [P, NT])
    lnabs_c = sb("lnabs_c", [P, NT])
    sign_c = sb("sign_c", [P, NT])
    state_c = sb("state_c", [P, NT])
    rseg_c = sb("rseg_c", [P, NT])
    pseg_c = sb("pseg_c", [P, NT])
    convw = sb("convw", [P, NDI, 4])
    convb = sb("convb", [P, NDI])
    bdt = sb("bdt", [P, NDI])
    dpc = sb("dpc", [P, NDI])
    hm_col = sb("hm_col", [P, 1])
    mseg_col = sb("mseg_col", [P, 8])
    ones_bf = sb("ones_bf", [P, P], BF16)
    blkf_bf = sb("blkf_bf", [P, NDI * P], BF16)
    sel8_bf = sb("sel8_bf", [P, NDI * P], BF16)
    dsum = sb("dsum", [P, NDI])
    dsum_bf = sb("dsum_bf", [P, NDI], BF16)
    dsr = sb("dsr", [P, 1])
    ident_bf = sb("ident_bf", [P, P], BF16)
    lhsT_yt = sb("lhsT_yt", [17, DI], BF16)

    ps = [es.enter_context(nc.psum_tensor(f"ps{i}", [P, TC], F32)) for i in range(8)]

    _cc_store = sb("cc_store", [P, 16])
    _cc_map = {}

    def cc(val):
        if val not in _cc_map:
            i = len(_cc_map)
            assert i < 16
            kb.op("vector", lambda e, i=i, v=val: e.memset(_cc_store[:, i:i + 1], v))
            _cc_map[val] = _cc_store[:, i:i + 1]
        return _cc_map[val]

    CINV_DM = cc(1.0 / DM)
    CEPS = cc(1e-5)
    CNEG1 = cc(-1.0)
    CONE = cc(1.0)

    p1w = ExitStack()
    p0w = ExitStack()
    w_in_bf = sb("w_in_bf", [P, ND, E4], BF16, p1w)
    w_x_bf = sb("w_x_bf", [P, NDI, 97], BF16, p1w)
    w_dt_bf = sb("w_dt_bf", [DTR, DI], BF16, p1w)
    wstg = [sb(f"wstg{i}", [P, E4], F32, p0w) for i in range(2)]

    for (dst, src) in ((convw, convw_h), (convb, convb_h), (bdt, bdt_h),
                       (dpc, dp_h)):
        kb.dma(lambda e, d=dst, s=src: e.dma_start(
            out=d[:, :, :] if len(d.shape) == 3 else d[:, :], in_=s), stream="ini")
    OFF_BLK, OFF_ID, OFF_AT = 0, NDI * P, 0
    kb.dma(lambda e: e.dma_start(out=a_cols[:], in_=alogc_h), stream="ini")
    kb.dma(lambda e: e.dma_start(
        out=wstg[1][0:P, OFF_BLK:OFF_BLK + NDI * P], in_=blkfull_h), stream="ini")
    kb.dma(lambda e: e.dma_start(
        out=wstg[0][0:P, DI:DI + NDI * P], in_=sel8f_h), stream="ini")
    kb.dma(lambda e: e.dma_start(
        out=wstg[1][0:P, OFF_ID:OFF_ID + P], in_=ident_h), stream="ini")
    kb.dma(lambda e: e.dma_start(
        out=wstg[0][0:NST, OFF_AT:OFF_AT + DI], in_=alogT_h), stream="ini")
    kb.dma(lambda e: e.dma_start(out=hm_col[:], in_=hmask_h), stream="ini")
    kb.dma(lambda e: e.dma_start(out=mseg_col[:], in_=mseg8_h), stream="ini")
    t_ini = kb.dtok("ini")
    t_alog = t_blk = t_id = t_alT = t_hm = t_ini

    kb.op("vector", lambda e: e.memset(ones_bf[:], 1.0))
    kb.op("vector", lambda e: e.tensor_copy(
        out=blkf_bf[:], in_=wstg[1][0:P, OFF_BLK:OFF_BLK + NDI * P]), waits=[t_blk])
    kb.op("vector", lambda e: e.tensor_copy(
        out=sel8_bf[:], in_=wstg[0][0:P, DI:DI + NDI * P]), waits=[t_blk])
    kb.op("vector", lambda e: e.memset(dsum[:], 0.0))
    kb.op("vector", lambda e: e.tensor_copy(
        out=ident_bf[:], in_=wstg[1][0:P, OFF_ID:OFF_ID + P]), waits=[t_id])
    t_ac = kb.op("scalar", lambda e: e.activation(
        out=a_cols[:], in_=a_cols[:], func=AF.Exp), waits=[t_alog])
    kb.op("vector", lambda e: e.tensor_scalar(
        out=a_cols[:], in0=a_cols[:], scalar1=-1.0, scalar2=None, op0=ALU.mult),
        waits=[t_ac])
    t_yt = kb.op("scalar", lambda e: e.activation(
        out=wstg[0][0:NST, OFF_AT:OFF_AT + DI],
        in_=wstg[0][0:NST, OFF_AT:OFF_AT + DI], func=AF.Exp), waits=[t_alT])
    kb.op("vector", lambda e: e.memset(lhsT_yt[0:17, :], 1.0))
    kb.op("vector", lambda e: e.tensor_scalar(
        out=lhsT_yt[0:NST, :], in0=wstg[0][0:NST, OFF_AT:OFF_AT + DI],
        scalar1=-1.0, scalar2=None, op0=ALU.mult), waits=[t_yt])
    kb.op("vector", lambda e: e.memset(state_c[:], 0.0))

    tw = []
    for dk in range(ND):
        tl = kb.dma(lambda e, dk=dk: e.dma_start(
            out=wstg[dk % 2][:, :], in_=w_inT_h[P * dk:P * (dk + 1), :]),
            waits=([tw[dk - 2]] if dk >= 2 else [kb.tok("vector")]),
            stream="wA" if dk % 2 == 0 else "wB")
        tw.append(kb.op("vector", lambda e, dk=dk: e.tensor_copy(
            out=w_in_bf[:, dk, :], in_=wstg[dk % 2][:, :]), waits=[tl]))
    t_wx = kb.dma(lambda e: e.dma_start(
        out=wstg[0][:, 0:NDI * 97].rearrange("p (k e) -> p k e", e=97),
        in_=w_xTd_h.rearrange("(k p) e -> p k e", p=P)),
        waits=[tw[-2], kb.tok("vector"), kb.tok("scalar")], stream="wA")
    t_wx = kb.op("vector", lambda e: e.tensor_copy(
        out=w_x_bf[:].rearrange("p a b -> p (a b)"), in_=wstg[0][:, 0:NDI * 97]),
        waits=[t_wx])
    t_wdt = kb.dma(lambda e: e.dma_start(out=wstg[1][0:DTR, 0:DI], in_=w_dtT_h),
                   waits=[tw[-1], kb.tok("vector")], stream="wB")
    t_wdt = kb.op("vector", lambda e: e.tensor_copy(
        out=w_dt_bf[:], in_=wstg[1][0:DTR, 0:DI]), waits=[t_wdt])

    kb.barrier()
    p0w.close()

    p1 = ExitStack()
    x_ld = [sb(f"x_ld{i}", [P, TC], F32, p1) for i in range(2)]
    x_bf = sb("x_bf", [P, ND, TC], BF16, p1)
    xh_bf = sb("xh_bf", [P, ND, TC], BF16, p1)
    stat = {k: sb("sta_" + k, [P, TC], F32, p1) for k in ("m", "rstd", "t1", "t2")}
    xi_bf = sb("xi_bf", [P, NDI, TC + 3], BF16, p1)
    zsdl_bf = sb("zsdl_bf", [P, NDI, TC], BF16, p1)
    u_bf = sb("u_bf", [P, NDI, TC], BF16, p1)
    cacc = sb("cacc", [P, TC], F32, p1)
    tts = sb("tts", [P, TC], F32, p1)
    sps = sb("sps", [P, TC], F32, p1)
    dls = sb("dls", [P, TC], F32, p1)
    dtr_bf = sb("dtr_bf", [DTR, TC], BF16, p1)

    st = {"prev_bank": [None] * 8, "halo_free": None}

    def emit_ln(width, off, deps):
        t_casts = []
        for dk in range(ND):
            tl = kb.dma(lambda e, dk=dk, w=width, o=off: e.dma_start(
                out=x_ld[dk % 2][:, 0:w], in_=x_h[P * dk:P * (dk + 1), o:o + w]),
                waits=deps + ([t_casts[dk - 2]] if dk >= 2 else []),
                stream="x0" if dk % 2 == 0 else "x1")
            t_casts.append(kb.op("vector", lambda e, dk=dk, w=width: e.tensor_copy(
                out=x_bf[:, dk, 0:w], in_=x_ld[dk % 2][:, 0:w]), waits=[tl]))
        t_sq = kb.op("scalar", lambda e, w=width: e.activation(
            out=xh_bf[:, :, 0:w], in_=x_bf[:, :, 0:w], func=AF.Square),
            waits=[t_casts[-1]])
        tmm = None
        for dk in range(ND):
            tmm = kb.op("tensor", lambda e, dk=dk, w=width: e.matmul(
                out=ps[4][:, 0:w], lhsT=ones_bf[:], rhs=x_bf[:, dk, 0:w],
                start=(dk == 0), stop=(dk == ND - 1)),
                waits=[t_casts[dk], st["prev_bank"][4]])
        for dk in range(ND):
            tmm = kb.op("tensor", lambda e, dk=dk, w=width: e.matmul(
                out=ps[5][:, 0:w], lhsT=ones_bf[:], rhs=xh_bf[:, dk, 0:w],
                start=(dk == 0), stop=(dk == ND - 1)),
                waits=[t_sq, st["prev_bank"][5]])
        t_m = kb.op("scalar", lambda e, w=width: e.activation(
            out=stat["m"][:, 0:w], in_=ps[4][:, 0:w], func=AF.Identity, scale=CINV_DM),
            waits=[tmm])
        t_v = kb.op("scalar", lambda e, w=width: e.activation(
            out=stat["t2"][:, 0:w], in_=ps[5][:, 0:w], func=AF.Identity, scale=CINV_DM),
            waits=[tmm])
        st["prev_bank"][4] = t_m
        st["prev_bank"][5] = t_v
        t_ms = kb.op("scalar", lambda e, w=width: e.activation(
            out=stat["t1"][:, 0:w], in_=stat["m"][:, 0:w], func=AF.Square),
            waits=[t_m])
        strict = width < P
        t_var = kb.op("vector", lambda e, w=width: e.tensor_sub(
            out=stat["t2"][:, 0:w], in0=stat["t2"][:, 0:w], in1=stat["t1"][:, 0:w]),
            waits=[t_v, t_ms], strict=strict)
        t_sd = kb.op("scalar", lambda e, w=width: e.activation(
            out=stat["t1"][:, 0:w], in_=stat["t2"][:, 0:w], func=AF.Sqrt, bias=CEPS),
            waits=[t_var])
        t_r = kb.op("vector", lambda e, w=width: e.reciprocal(
            out=stat["rstd"][:, 0:w], in_=stat["t1"][:, 0:w]), waits=[t_sd],
            strict=strict)
        t_app = None
        for dk in range(ND):
            ta = kb.op("vector", lambda e, dk=dk, w=width: e.tensor_sub(
                out=stat["t1"][:, 0:w], in0=x_bf[:, dk, 0:w], in1=stat["m"][:, 0:w]),
                waits=[t_r], strict=strict)
            t_app = kb.op("vector", lambda e, dk=dk, w=width: e.tensor_mul(
                out=xh_bf[:, dk, 0:w], in0=stat["t1"][:, 0:w],
                in1=stat["rstd"][:, 0:w]), waits=[ta], strict=strict)
        return t_app

    def emit_xz(width, t_xh, is_halo, zs_free=None):
        evac = []
        for em in range(32):
            bank = em % 2
            tmm = None
            for dk in range(ND):
                tmm = kb.op("tensor", lambda e, em=em, dk=dk, bank=bank, w=width: e.matmul(
                    out=ps[bank][:, 0:w],
                    lhsT=w_in_bf[:, dk, P * em:P * (em + 1)],
                    rhs=xh_bf[:, dk, 0:w],
                    start=(dk == 0), stop=(dk == ND - 1)),
                    waits=[t_xh, st["prev_bank"][bank]])
            if em < NDI:
                if is_halo:
                    te = kb.op("scalar", lambda e, em=em, bank=bank: e.activation(
                        out=xi_bf[:, em, 0:3], in_=ps[bank][:, 0:3],
                        func=AF.Identity, scale=hm_col[:, 0:1]),
                        waits=[tmm, t_hm])
                else:
                    te = kb.op("scalar", lambda e, em=em, bank=bank, w=width: e.copy(
                        out=xi_bf[:, em, 3:3 + w], in_=ps[bank][:, 0:w]),
                        waits=[tmm, st["halo_free"]])
            else:
                if is_halo:
                    te = tmm
                else:
                    te = kb.op("scalar", lambda e, em=em, bank=bank, w=width: e.activation(
                        out=zsdl_bf[:, em - NDI, 0:w], in_=ps[bank][:, 0:w],
                        func=AF.Silu), waits=[tmm, zs_free])
            st["prev_bank"][bank] = te
            evac.append(te)
        return evac

    kb.scope = 'p1'
    t_xh0 = emit_ln(3, 0, [tw[-1], t_wx, kb.tok("vector")])
    emit_xz(3, t_xh0, True)
    kb.barrier()
    if _DBG == "p1":
        d_xi_h = hbm("d_xi", [P, NDI, 3], BF16, "ExternalOutput")
        d_xh_h = hbm("d_xh", [P, ND, 3], BF16, "ExternalOutput")
        d_xb_h = hbm("d_xb", [P, ND, 3], BF16, "ExternalOutput")
        d_st_h = hbm("d_st", [P, 8], F32, "ExternalOutput")
        kb.dma(lambda e: e.dma_start(out=d_xi_h[:, :, :], in_=xi_bf[:, :, 0:3]),
               stream="dxi")
        kb.dma(lambda e: e.dma_start(out=d_xh_h[:, :, :], in_=xh_bf[:, :, 0:3]),
               stream="dxi")
        kb.dma(lambda e: e.dma_start(out=d_xb_h[:, :, :], in_=x_bf[:, :, 0:3]),
               stream="dxi")
        kb.dma(lambda e: e.dma_start(out=d_st_h[:, 0:3], in_=stat["m"][:, 0:3]),
               stream="dxi")
        kb.dma(lambda e: e.dma_start(out=d_st_h[:, 4:7], in_=stat["rstd"][:, 0:3]),
               stream="dxi")
        kb.barrier()

    prev = {"xdbl": None, "dt0": None, "dt1": None, "zs_dma": None,
            "u_dma": None, "dl_dma": None, "w_dma": None}

    for c in range(NCH):
        off = 3 + TC * c
        t_xh = emit_ln(TC, off, [])
        evac = emit_xz(TC, t_xh, False, zs_free=prev["dl_dma"])
        prev["zs_dma"] = kb.dma(lambda e, c=c: e.dma_start(
            out=st_zs[:, :, TC * c:TC * (c + 1)], in_=zsdl_bf[:, :, :]),
            waits=[evac[-1]], stream="szs")

        t_u = []
        for j in range(NDI):
            tk = kb.op("vector", lambda e, j=j: e.tensor_scalar(
                out=cacc[:, :], in0=xi_bf[:, j, 0:TC],
                scalar1=convw[:, j, 0:1], scalar2=None, op0=ALU.mult),
                waits=[evac[j], st["halo_free"], t_u[-1] if t_u else None])
            for k in range(1, 4):
                tk = kb.op("vector", lambda e, j=j, k=k: e.scalar_tensor_tensor(
                    out=cacc[:, :], in0=xi_bf[:, j, k:k + TC],
                    scalar=convw[:, j, k:k + 1], in1=cacc[:, :],
                    op0=ALU.mult, op1=ALU.add), waits=[tk])
            t_u.append(kb.op("scalar", lambda e, j=j: e.activation(
                out=u_bf[:, j, :], in_=cacc[:, :], func=AF.Silu,
                bias=convb[:, j:j + 1]), waits=[tk, prev["w_dma"]]))
        st["halo_free"] = kb.op("vector", lambda e: e.tensor_copy(
            out=xi_bf[:, :, 0:3], in_=xi_bf[:, :, TC:TC + 3]), waits=[t_u[-1]])
        prev["u_dma"] = kb.dma(lambda e, c=c: e.dma_start(
            out=st_u[:, :, TC * c:TC * (c + 1)], in_=u_bf[:, :, :]),
            waits=[t_u[-1]], stream="su")

        tmm = None
        for k in range(NDI):
            tmm = kb.op("tensor", lambda e, k=k: e.matmul(
                out=ps[6][0:97, :], lhsT=w_x_bf[:, k, :], rhs=u_bf[:, k, :],
                start=(k == 0), stop=(k == NDI - 1)),
                waits=[t_u[k], prev["xdbl"]])
        t_dtr = kb.op("scalar", lambda e: e.copy(
            out=dtr_bf[:, :], in_=ps[6][0:DTR, :]), waits=[tmm])
        t_bc = kb.op("scalar", lambda e, c=c: e.copy(
            out=bcd_bf[0:33, TC * c:TC * (c + 1)], in_=ps[6][DTR:97, :]), waits=[tmm])
        t_cd = kb.dma(lambda e, c=c: e.dma_start(
            out=cdpu_bf[0:17, TC * c:TC * (c + 1)],
            in_=bcd_bf[16:33, TC * c:TC * (c + 1)]), waits=[t_bc], stream="scd")
        prev["xdbl"] = t_bc

        t_dl_all = []
        for m in range(NDI):
            bk = 2 + (m % 2)
            tm = kb.op("tensor", lambda e, m=m, bk=bk: e.matmul(
                out=ps[bk][:, :], lhsT=w_dt_bf[:, P * m:P * (m + 1)],
                rhs=dtr_bf[:, :], start=True, stop=True),
                waits=[t_dtr, t_wdt, prev["dt0"] if bk == 2 else prev["dt1"]])
            te = kb.op("scalar", lambda e, m=m, bk=bk: e.activation(
                out=tts[:, :], in_=ps[bk][:, :], func=AF.Identity,
                bias=bdt[:, m:m + 1]), waits=[tm, kb.tok("vector")])
            if bk == 2:
                prev["dt0"] = te
            else:
                prev["dt1"] = te
            t_sp = kb.op("scalar", lambda e: e.activation(
                out=sps[:, :], in_=tts[:, :], func=AF.Abs), waits=[te])
            t_sp = kb.op("scalar", lambda e: e.activation(
                out=sps[:, :], in_=sps[:, :], func=AF.Exp, scale=CNEG1), waits=[t_sp])
            t_sp = kb.op("scalar", lambda e: e.activation(
                out=sps[:, :], in_=sps[:, :], func=AF.Ln, bias=CONE), waits=[t_sp])
            t_dl = kb.op("vector", lambda e: e.scalar_tensor_tensor(
                out=dls[:, :], in0=tts[:, :], scalar=0.0, in1=sps[:, :],
                op0=ALU.max, op1=ALU.add), waits=[t_sp])
            t_dsr = kb.op("vector", lambda e: e.reduce_sum(
                out=dsr[:, 0:1], in_=dls[:, :], axis=AX.X), waits=[t_dl])
            kb.op("vector", lambda e, m=m: e.tensor_add(
                out=dsum[:, m:m + 1], in0=dsum[:, m:m + 1], in1=dsr[:, 0:1]),
                waits=[t_dsr], strict=True)
            t_w = kb.op("vector", lambda e, m=m: e.tensor_mul(
                out=u_bf[:, m, :], in0=dls[:, :], in1=u_bf[:, m, :]),
                waits=[t_dl, tmm, prev["u_dma"]])
            t_db = kb.op("vector", lambda e, m=m: e.tensor_copy(
                out=zsdl_bf[:, m, :], in_=dls[:, :]),
                waits=[t_dl, prev["zs_dma"]])
            t_dl_all.append((t_w, t_db))
        prev["dl_dma"] = kb.dma(lambda e, c=c: e.dma_start(
            out=st_delta[:, :, TC * c:TC * (c + 1)], in_=zsdl_bf[:, :, :]),
            waits=[t_dl_all[-1][1]], stream="sdl")
        prev["w_dma"] = kb.dma(lambda e, c=c: e.dma_start(
            out=st_w[:, :, TC * c:TC * (c + 1)], in_=u_bf[:, :, :]),
            waits=[t_dl_all[-1][0]], stream="sw")
        kb.barrier()

    kb.scope = 'p15'
    p1.close()
    p1w.close()
    psc = ExitStack()
    if _PHASES >= 2:
        bt_f = sb("bt_f", [P, SEG], BF16, psc)
        ct_f = sb("ct_f", [P, SEG], BF16, psc)
        dgrp = [sb(f"dgrp{i}", [P, SEG], BF16, psc) for i in range(2)]
        wgrp = [sb(f"wgrp{i}", [P, SEG], BF16, psc) for i in range(2)]
        dA_f = [sb(f"dA_f{i}", [P, SEG], BF16, psc) for i in range(2)]
        b_f = sb("b_f", [P, SEG], BF16, psc)
        s_f = [sb(f"s_f{i}", [P, SEG], BF16, psc) for i in range(2)]
        p_f = [sb(f"p_f{i}", [P, SEG], BF16, psc) for i in range(8)]
        yl_s = [sb(f"yl_s{i}", [P, SEG], BF16, psc) for i in range(2)]
        drep = sb("drep", [P, NT], F32, psc)
        bias_c = sb("bias_c", [P, NT], F32, psc)
        gtmp = [sb(f"gtmp{i}", [P, NT], F32, psc) for i in range(2)]

        for i8 in range(8):
            kb.dma(lambda e, i8=i8: e.dma_start(
                out=bt_f[16 * i8:16 * (i8 + 1), :], in_=bcd_bf[0:16, :]),
                stream="pbt")
            kb.dma(lambda e, i8=i8: e.dma_start(
                out=ct_f[16 * i8:16 * (i8 + 1), :], in_=bcd_bf[16:32, :]),
                stream="pct")

        def scan_sweep(correction):
            pv = {}
            for g in range(NDI):
                gp = g % 2
                t_ld = kb.dma(lambda e, g=g, gp=gp: e.dma_start(
                    out=dgrp[gp][:, :], in_=st_delta[:, g, :]),
                    waits=[pv.get(("dr", gp))], stream=f"pld{gp}")
                if not correction:
                    t_rhs = kb.dma(lambda e, g=g, gp=gp: e.dma_start(
                        out=wgrp[gp][:, :], in_=st_w[:, g, :]),
                        waits=[pv.get(("wr", gp))], stream=f"plw{gp}")
                else:
                    t_rhs = kb.op("vector", lambda e, gp=gp: e.tensor_tensor_scan(
                        out=wgrp[gp][:, :], data0=dgrp[gp][:, :],
                        data1=dgrp[gp][:, :], initial=0.0,
                        op0=ALU.add, op1=ALU.bypass),
                        waits=[t_ld, pv.get(("wr", gp))])
                    pv[("dr", gp)] = t_rhs
                for h in range(2):
                    for r in range(8):
                        jj = 8 * h + r
                        j = NDI * g + jj
                        pb = jj % 2
                        sfb = jj % 2
                        if not correction:
                            for cc in range(8):
                                cb = cc % 2
                                tmd = kb.op("tensor", lambda e, jj=jj, gp=gp, cc=cc, cb=cb: e.matmul(
                                    out=ps[cb][:, :], lhsT=sel8_bf[:, P * jj:P * (jj + 1)],
                                    rhs=dgrp[gp][:, TC * cc:TC * (cc + 1)],
                                    start=True, stop=True),
                                    waits=[t_ld, pv.get(("db", cb))])
                                t_dA = kb.op("scalar", lambda e, j=j, pb=pb, cc=cc, cb=cb: e.activation(
                                    out=dA_f[pb][:, TC * cc:TC * (cc + 1)], in_=ps[cb][:, :],
                                    func=AF.Exp, scale=a_cols[:, j:j + 1]),
                                    waits=[tmd, pv.get(("dAr", pb))])
                                pv[("db", cb)] = t_dA
                            pv[("dmml", gp)] = tmd
                            for cc in range(8):
                                cb = 2 + cc % 2
                                tmw = kb.op("tensor", lambda e, jj=jj, gp=gp, cc=cc, cb=cb: e.matmul(
                                    out=ps[cb][:, :], lhsT=sel8_bf[:, P * jj:P * (jj + 1)],
                                    rhs=wgrp[gp][:, TC * cc:TC * (cc + 1)],
                                    start=True, stop=True),
                                    waits=[t_rhs, pv.get(("wb", cb))])
                                t_b = kb.op("vector", lambda e, cc=cc, cb=cb: e.tensor_tensor(
                                    out=b_f[:, TC * cc:TC * (cc + 1)], in0=ps[cb][:, :],
                                    in1=bt_f[:, TC * cc:TC * (cc + 1)], op=ALU.mult),
                                    waits=[tmw, kb.dtok("pbt")])
                                pv[("wb", cb)] = t_b
                            pv[("wmml", gp)] = tmw
                            if pv.get("pend") is not None:
                                jp, sfp = pv.pop("pend")
                                kb.op("vector", lambda e, jp=jp, sfp=sfp: e.tensor_copy(
                                    out=state_c[:, jp:jp + 1],
                                    in_=s_f[sfp][:, SEG - 1:SEG]))
                            t_scan = kb.op("vector", lambda e, pb=pb, sfb=sfb: e.tensor_tensor_scan(
                                out=s_f[sfb][:, :], data0=dA_f[pb][:, :], data1=b_f[:, :],
                                initial=0.0, op0=ALU.mult, op1=ALU.add),
                                waits=[t_dA, pv.get(("sfr", sfb))])
                            pv[("dAr", pb)] = t_scan
                            t_p = kb.op("gpsimd", lambda e, sfb=sfb, r=r: e.tensor_tensor(
                                out=p_f[r][:, :], in0=s_f[sfb][:, :], in1=ct_f[:, :],
                                op=ALU.mult),
                                waits=[t_scan, kb.dtok("pct"), pv.get(("pr", r))])
                            pv[("sfr", sfb)] = t_p
                            pv["pend"] = (j, sfb)
                        else:
                            for cc in range(8):
                                cb = cc % 2
                                tmd = kb.op("tensor", lambda e, jj=jj, gp=gp, cc=cc, cb=cb: e.matmul(
                                    out=ps[cb][:, :], lhsT=sel8_bf[:, P * jj:P * (jj + 1)],
                                    rhs=wgrp[gp][:, TC * cc:TC * (cc + 1)],
                                    start=True, stop=True),
                                    waits=[t_rhs, pv.get(("db", cb))])
                                t_E = kb.op("scalar", lambda e, j=j, pb=pb, cc=cc, cb=cb: e.activation(
                                    out=dA_f[pb][:, TC * cc:TC * (cc + 1)], in_=ps[cb][:, :],
                                    func=AF.Exp, scale=a_cols[:, j:j + 1],
                                    bias=lnabs_c[:, j:j + 1]),
                                    waits=[tmd, pv.get(("dAr", pb))])
                                pv[("db", cb)] = t_E
                            pv[("wmml", gp)] = tmd
                            t_p = kb.op("vector", lambda e, j=j, pb=pb, r=r: e.scalar_tensor_tensor(
                                out=p_f[r][:, :], in0=dA_f[pb][:, :],
                                scalar=sign_c[:, j:j + 1], in1=ct_f[:, :],
                                op0=ALU.mult, op1=ALU.mult),
                                waits=[t_E, kb.dtok("pct"), pv.get(("pr", r))])
                            pv[("dAr", pb)] = t_p
                        pv[("tp", r)] = t_p
                    hc = 2 * g + h
                    yb = 4 + hc % 2
                    for cc in range(8):
                        tmy = None
                        for r in range(8):
                            jj = 8 * h + r
                            tmy = kb.op("tensor", lambda e, jj=jj, h=h, r=r, cc=cc, yb=yb: e.matmul(
                                out=ps[yb][0:64, :],
                                lhsT=blkf_bf[:, P * jj + 64 * h:P * jj + 64 * h + 64],
                                rhs=p_f[r][:, TC * cc:TC * (cc + 1)],
                                start=(r == 0), stop=(r == 7), skip_group_check=True),
                                waits=[pv[("tp", r)],
                                       pv.get(("yb", yb)) if r == 0 else None])
                            pv[("pr", r)] = tmy
                        tye = kb.op("scalar", lambda e, h=h, cc=cc, yb=yb, gp=gp: e.activation(
                            out=yl_s[gp][64 * h:64 * h + 64, TC * cc:TC * (cc + 1)],
                            in_=ps[yb][0:64, :], func=AF.Identity),
                            waits=[tmy, pv.get(("ylw", gp))])
                        pv[("yb", yb)] = tye
                        pv[("ye", gp)] = tye
                t_sto = kb.dma(lambda e, g=g, gp=gp, correction=correction: e.dma_start(
                    out=(st_yc if correction else st_yl)[:, g, :],
                    in_=yl_s[gp][:, :]),
                    waits=[pv.get(("ye", gp))], stream=f"pyl{gp}")
                pv[("ylw", gp)] = t_sto
                if not correction:
                    pv[("dr", gp)] = pv[("dmml", gp)]
                pv[("wr", gp)] = pv[("wmml", gp)]
            if pv.get("pend") is not None:
                jp, sfp = pv.pop("pend")
                kb.op("vector", lambda e, jp=jp, sfp=sfp: e.tensor_copy(
                    out=state_c[:, jp:jp + 1], in_=s_f[sfp][:, SEG - 1:SEG]),
                    waits=[pv.get(("sfr", sfp))])

        scan_sweep(correction=False)

    if _PHASES >= 3:
        kb.scope = 'pag'
        t_dsb = kb.op("vector", lambda e: e.tensor_copy(
            out=dsum_bf[:, :], in_=dsum[:, :]), waits=[kb.tok("vector")],
            strict=True)
        for jj in range(NDI):
            tm = kb.op("tensor", lambda e, jj=jj: e.matmul(
                out=ps[6][:, 0:NDI], lhsT=sel8_bf[:, P * jj:P * (jj + 1)],
                rhs=dsum_bf[:, :], start=True, stop=True),
                waits=[t_dsb, kb.tok("scalar")])
            kb.op("scalar", lambda e, jj=jj: e.activation(
                out=drep[:, jj::NDI], in_=ps[6][:, 0:NDI], func=AF.Identity),
                waits=[tm])
        t_ps = kb.op("vector", lambda e: e.tensor_mul(
            out=pseg_c[:, :], in0=a_cols[:, :], in1=drep[:, :]),
            waits=[kb.tok("scalar")], strict=True)
        t_ps = kb.op("scalar", lambda e: e.activation(
            out=pseg_c[:, :], in_=pseg_c[:, :], func=AF.Exp), waits=[t_ps])
        t_c1 = kb.dma(lambda e: e.dma_start(
            out=cc_rp_in[0, 0:P * NT], in_=state_c[:, :]),
            waits=[kb.tok("vector"), kb.tok("gpsimd")], stream="cca")
        t_c2 = kb.dma(lambda e: e.dma_start(
            out=cc_rp_in[0, P * NT:2 * P * NT], in_=pseg_c[:, :]),
            waits=[t_ps], stream="cca")
        t_ag = kb.op("gpsimd", lambda e: e.collective_compute(
            "AllGather", ALU.bypass, replica_groups=RG8,
            ins=[cc_rp_in[:, :]], outs=[cc_rp_out[:, :]]),
            waits=[t_c1, t_c2])
        acc = state_c
        rr_s = gtmp[0]
        pr_s = gtmp[1]
        kb.op("vector", lambda e: e.memset(acc[:, :], 0.0), waits=[t_ag])
        for r in range(8):
            t_l1 = kb.dma(lambda e, r=r: e.dma_start(
                out=rr_s[:, :], in_=cc_rp_out[r, 0:P * NT]),
                waits=[t_ag, kb.tok("vector")], stream="ccr")
            t_l2 = kb.dma(lambda e, r=r: e.dma_start(
                out=pr_s[:, :], in_=cc_rp_out[r, P * NT:2 * P * NT]),
                waits=[t_ag, kb.tok("vector")], stream="ccp")
            kb.op("vector", lambda e: e.tensor_mul(
                out=bias_c[:, :], in0=pr_s[:, :], in1=acc[:, :]),
                waits=[t_l1, t_l2], strict=True)
            kb.op("vector", lambda e: e.tensor_add(
                out=bias_c[:, :], in0=bias_c[:, :], in1=rr_s[:, :]), strict=True)
            kb.op("vector", lambda e: e.tensor_sub(
                out=bias_c[:, :], in0=bias_c[:, :], in1=acc[:, :]), strict=True)
            kb.op("vector", lambda e, r=r: e.scalar_tensor_tensor(
                out=acc[:, :], in0=bias_c[:, :], scalar=mseg_col[:, r:r + 1],
                in1=acc[:, :], op0=ALU.mult, op1=ALU.add), strict=True)
        t_ab = kb.op("scalar", lambda e: e.activation(
            out=lnabs_c[:, :], in_=acc[:, :], func=AF.Abs),
            waits=[kb.tok("vector")])
        t_ab = kb.op("vector", lambda e: e.tensor_scalar(
            out=lnabs_c[:, :], in0=lnabs_c[:, :], scalar1=1e-38, scalar2=None,
            op0=ALU.max), waits=[t_ab], strict=True)
        t_ab = kb.op("scalar", lambda e: e.activation(
            out=lnabs_c[:, :], in_=lnabs_c[:, :], func=AF.Ln), waits=[t_ab])
        t_sg = kb.op("vector", lambda e: e.tensor_scalar(
            out=sign_c[:, :], in0=acc[:, :], scalar1=0.0, scalar2=None,
            op0=ALU.is_ge), waits=[kb.tok("scalar")], strict=True)
        t_sg = kb.op("vector", lambda e: e.tensor_scalar(
            out=sign_c[:, :], in0=sign_c[:, :], scalar1=2.0, scalar2=-1.0,
            op0=ALU.mult, op1=ALU.add), waits=[t_sg], strict=True)
        kb.barrier()
        kb.scope = 'p2'
        scan_sweep(correction=True)

    if _PHASES >= 4:
        kb.scope = 'p3'
        kb.barrier()
        psc.close()
        p3 = ExitStack()
        wo2_bf = sb("wo2_bf", [P, NDI, DM], BF16, p3)
        h_bf = sb("h_bf", [P, NDI, TC], BF16, p3)
        p3w = ExitStack()
        st_wo1 = hbm("st_wo1", [P, 32, DI], BF16)
        wo_stg = [sb(f"wo_stg{i}", [P, DI], F32, p3w) for i in range(2)]
        wo_bf = sb("wo_bf", [P, DI], BF16, p3w)
        t_pc = None
        for kk in range(32):
            tl = kb.dma(lambda e, kk=kk: e.dma_start(
                out=wo_stg[kk % 2][:, :], in_=w_o1T_h[P * kk:P * (kk + 1), :]),
                waits=[t_pc], stream="wo" + str(kk % 2))
            t_cst = kb.op("vector", lambda e, kk=kk: e.tensor_copy(
                out=wo_bf[:, :], in_=wo_stg[kk % 2][:, :]), waits=[tl, t_pc])
            t_pc = kb.dma(lambda e, kk=kk: e.dma_start(
                out=st_wo1[:, kk, :], in_=wo_bf[:, :]), waits=[t_cst],
                stream="wo2")
        for kk in range(NDI):
            tl = kb.dma(lambda e, kk=kk: e.dma_start(
                out=wo_stg[kk % 2][:, 0:DM], in_=w_o2T_h[P * kk:P * (kk + 1), :]),
                waits=[kb.tok("vector")], stream="wo" + str(kk % 2))
            kb.op("vector", lambda e, kk=kk: e.tensor_copy(
                out=wo2_bf[:, kk, :], in_=wo_stg[kk % 2][:, 0:DM]), waits=[tl])
        kb.barrier()
        p3w.close()

        p3a = ExitStack()
        yin = {k: sb("yin_" + k, [P, NDI, TC], BF16, p3a)
               for k in ("yl", "yc", "u", "zs")}
        ycat = sb("ycat", [P, 32, TC], BF16, p3a)
        wslab = [sb(f"wslab{i}", [P, 32, P], BF16, p3a) for i in range(2)]
        for c in range(NCH):
            cols = slice(TC * c, TC * (c + 1))
            tl = {}
            for k, srcs in (("yl", st_yl), ("yc", st_yc), ("u", st_u),
                            ("zs", st_zs)):
                tl[k] = kb.dma(lambda e, k=k, srcs=srcs, c=c: e.dma_start(
                    out=yin[k][:, :, :], in_=srcs[:, :, TC * c:TC * (c + 1)]),
                    stream="l3" + k)
            t_yf = None
            for m in range(NDI):
                t1 = kb.op("vector", lambda e, m=m: e.scalar_tensor_tensor(
                    out=ycat[:, m, :], in0=yin["u"][:, m, :],
                    scalar=dpc[:, m:m + 1], in1=yin["yl"][:, m, :],
                    op0=ALU.mult, op1=ALU.add),
                    waits=[tl["u"], tl["yl"]])
                t1 = kb.op("vector", lambda e, m=m: e.tensor_add(
                    out=ycat[:, m, :], in0=ycat[:, m, :], in1=yin["yc"][:, m, :]),
                    waits=[tl["yc"]])
                t_yf = kb.op("vector", lambda e, m=m: e.tensor_mul(
                    out=ycat[:, m, :], in0=ycat[:, m, :], in1=yin["zs"][:, m, :]),
                    waits=[tl["zs"]])
            for m in range(NDI):
                bk = m % 2
                tm = kb.op("tensor", lambda e, m=m, bk=bk, c=c: e.matmul(
                    out=ps[bk][:, :], lhsT=lhsT_yt[:, P * m:P * (m + 1)],
                    rhs=cdpu_bf[0:17, TC * c:TC * (c + 1)],
                    start=True, stop=True),
                    waits=[prev.get(("yt", bk)), kb.dtok("scd")])
                te = kb.op("scalar", lambda e, m=m, bk=bk: e.copy(
                    out=ycat[:, NDI + m, :], in_=ps[bk][:, :]), waits=[tm])
                prev[("yt", bk)] = te
            t_h = None
            for m in range(NDI):
                wb = m % 2
                t_ws = kb.dma(lambda e, m=m, wb=wb: e.dma_start(
                    out=wslab[wb][:, :, :],
                    in_=st_wo1[:, :, P * m:P * (m + 1)]),
                    waits=[prev.get(("ws", wb))], stream="lws" + str(wb))
                bk = 2 + m % 2
                tm = None
                for k in range(32):
                    tm = kb.op("tensor", lambda e, m=m, k=k, bk=bk, wb=wb: e.matmul(
                        out=ps[bk][:, :], lhsT=wslab[wb][:, k, :],
                        rhs=ycat[:, k, :], start=(k == 0), stop=(k == 31)),
                        waits=[t_ws, t_yf, kb.tok("scalar"),
                               prev.get(("h", bk))])
                prev[("ws", wb)] = tm
                t_h = kb.op("scalar", lambda e, m=m, bk=bk: e.activation(
                    out=h_bf[:, m, :], in_=ps[bk][:, :], func=AF.Relu), waits=[tm])
                prev[("h", bk)] = t_h
            kb.dma(lambda e, c=c: e.dma_start(
                out=st_h[:, :, TC * c:TC * (c + 1)], in_=h_bf[:, :, :]),
                waits=[t_h, kb.tok("scalar")], stream="sh")
            kb.barrier()

        kb.barrier()
        p3a.close()
        mix_f = sb("mix_f", [P, ND, TC], F32, p3)
        x1_f = sb("x1_f", [P, ND, TC], F32, p3)
        v_f = sb("v_f", [P, ND, TC], F32, p3)
        xr_f = sb("xr_f", [P, ND, TC], F32, p3)
        lnb = sb("lnb", [P, ND, TC], BF16, p3)
        lnsq = sb("lnsq", [P, ND, TC], BF16, p3)
        lst = {k: sb("l3_" + k, [P, TC], F32, p3) for k in ("m", "r", "t1", "t2")}
        vmax_c = sb("vmax_c", [P, 1], F32, p3)
        vmr = sb("vmr", [P, 1], F32, p3)
        kb.op("vector", lambda e: e.memset(vmax_c[:], 0.0))

        def emit_ln_tail(src_f, width, t_src):
            t_c = kb.op("vector", lambda e, w=width: e.tensor_copy(
                out=lnb[:, :, 0:w], in_=src_f[:, :, 0:w]), waits=[t_src])
            t_s = kb.op("scalar", lambda e, w=width: e.activation(
                out=lnsq[:, :, 0:w], in_=lnb[:, :, 0:w], func=AF.Square),
                waits=[t_c])
            tmm = None
            for dk in range(ND):
                tmm = kb.op("tensor", lambda e, dk=dk, w=width: e.matmul(
                    out=ps[4][:, 0:w], lhsT=ones_bf[:], rhs=lnb[:, dk, 0:w],
                    start=(dk == 0), stop=(dk == ND - 1)),
                    waits=[t_c, prev.get(("ln", 4))])
            for dk in range(ND):
                tmm = kb.op("tensor", lambda e, dk=dk, w=width: e.matmul(
                    out=ps[5][:, 0:w], lhsT=ones_bf[:], rhs=lnsq[:, dk, 0:w],
                    start=(dk == 0), stop=(dk == ND - 1)),
                    waits=[t_s, prev.get(("ln", 5))])
            t_m = kb.op("scalar", lambda e, w=width: e.activation(
                out=lst["m"][:, 0:w], in_=ps[4][:, 0:w], func=AF.Identity,
                scale=CINV_DM), waits=[tmm])
            t_v = kb.op("scalar", lambda e, w=width: e.activation(
                out=lst["t2"][:, 0:w], in_=ps[5][:, 0:w], func=AF.Identity,
                scale=CINV_DM), waits=[tmm])
            prev[("ln", 4)] = t_m
            prev[("ln", 5)] = t_v
            t_ms = kb.op("scalar", lambda e, w=width: e.activation(
                out=lst["t1"][:, 0:w], in_=lst["m"][:, 0:w], func=AF.Square),
                waits=[t_m])
            t_var = kb.op("vector", lambda e, w=width: e.tensor_sub(
                out=lst["t2"][:, 0:w], in0=lst["t2"][:, 0:w], in1=lst["t1"][:, 0:w]),
                waits=[t_v, t_ms])
            t_sd = kb.op("scalar", lambda e, w=width: e.activation(
                out=lst["t1"][:, 0:w], in_=lst["t2"][:, 0:w], func=AF.Sqrt,
                bias=CEPS), waits=[t_var])
            t_r = kb.op("vector", lambda e, w=width: e.reciprocal(
                out=lst["r"][:, 0:w], in_=lst["t1"][:, 0:w]), waits=[t_sd])
            return t_r

        for c in range(NCH):
            t_lh = kb.dma(lambda e, c=c: e.dma_start(
                out=h_bf[:, :, :], in_=st_h[:, :, TC * c:TC * (c + 1)]),
                stream="l3h")
            t_lx = kb.dma(lambda e, c=c: e.dma_start(
                out=xr_f[:, :, :],
                in_=x_h[:, 3 + TC * c:3 + TC * (c + 1)]
                .rearrange("(k p) t -> p k t", p=P)), stream="l3x")
            t_mx = None
            for m in range(ND):
                bk = m % 2
                tm = None
                for k in range(NDI):
                    tm = kb.op("tensor", lambda e, m=m, k=k, bk=bk: e.matmul(
                        out=ps[bk][:, :], lhsT=wo2_bf[:, k, P * m:P * (m + 1)],
                        rhs=h_bf[:, k, :], start=(k == 0), stop=(k == NDI - 1)),
                        waits=[t_lh, prev.get(("mx", bk))])
                t_mx = kb.op("scalar", lambda e, m=m, bk=bk: e.copy(
                    out=mix_f[:, m, :], in_=ps[bk][:, :]), waits=[tm])
                prev[("mx", bk)] = t_mx
            t_r2 = emit_ln_tail(mix_f, TC, t_mx)
            t_x1 = None
            for dk in range(ND):
                ta = kb.op("vector", lambda e, dk=dk: e.tensor_sub(
                    out=lst["t1"][:, :], in0=mix_f[:, dk, :], in1=lst["m"][:, :]),
                    waits=[t_r2])
                ta = kb.op("vector", lambda e, dk=dk: e.tensor_mul(
                    out=lst["t1"][:, :], in0=lst["t1"][:, :], in1=lst["r"][:, :]),
                    waits=[])
                t_x1 = kb.op("vector", lambda e, dk=dk: e.tensor_add(
                    out=x1_f[:, dk, :], in0=lst["t1"][:, :], in1=xr_f[:, dk, :]),
                    waits=[t_lx])
            kb.dma(lambda e, c=c: e.dma_start(
                out=st_x1[:, :, TC * c:TC * (c + 1)], in_=x1_f[:, :, :]),
                waits=[t_x1], stream="sx1")
            t_r3 = emit_ln_tail(x1_f, TC, t_x1)
            t_v3 = None
            for dk in range(ND):
                ta = kb.op("vector", lambda e, dk=dk: e.tensor_sub(
                    out=lst["t1"][:, :], in0=x1_f[:, dk, :], in1=lst["m"][:, :]),
                    waits=[t_r3])
                t_v3 = kb.op("vector", lambda e, dk=dk: e.tensor_mul(
                    out=v_f[:, dk, :], in0=lst["t1"][:, :], in1=lst["r"][:, :]),
                    waits=[])
            kb.dma(lambda e, c=c: e.dma_start(
                out=st_v[:, :, TC * c:TC * (c + 1)], in_=v_f[:, :, :]),
                waits=[t_v3], stream="sv")
            t_vm = kb.op("vector", lambda e: e.tensor_reduce(
                out=vmr[:, 0:1], in_=v_f[:, :, :].rearrange("p a b -> p (a b)"),
                axis=AX.X, op=ALU.max, apply_absolute_value=True),
                waits=[t_v3], strict=True)
            kb.op("vector", lambda e: e.tensor_max(
                out=vmax_c[:, 0:1], in0=vmax_c[:, 0:1], in1=vmr[:, 0:1]),
                waits=[t_vm], strict=True)
            kb.barrier()

        if _DBG == "p3":
            dbg["d_x1"] = (hbm("d_x1", [P, ND, SEG], F32, "ExternalOutput"), st_x1)
            dbg["d_v"] = (hbm("d_v", [P, ND, SEG], F32, "ExternalOutput"), st_v)
        es.callback(p3.close)

    if _PHASES >= 5:
        kb.scope = 'ffnw'
        kb.barrier()
        p3.close()
        scol = sb("scol", [P, 24])
        vrow = sb("vrow", [1, P])
        fmr = sb("fmr", [P, 1])
        cc_sc = hbm("cc_sc", [1, P], F32)

        vone = sb("vone", [1, 2])
        ones1f = sb("ones1f", [1, P])
        kb.op("vector", lambda e: e.memset(ones1f[:], 1.0))

        def cross_core_max2(src_cols, cc_in, cc_out, dst_cols, t_src):
            toks = []
            tprev = t_src
            for i, (ten, col) in enumerate(src_cols):
                t1 = kb.dma(lambda e, ten=ten, col=col, i=i: e.dma_start(
                    out=cc_sc[0, 0:P], in_=ten[:, col:col + 1]),
                    waits=[t_src, tprev], stream="xm1")
                t2 = kb.dma(lambda e, i=i: e.dma_start(
                    out=vrow[0:1, :], in_=cc_sc[0, 0:P]), waits=[t1],
                    stream="xm2")
                t3 = kb.op("vector", lambda e: e.tensor_reduce(
                    out=vrow[0:1, 0:1], in_=vrow[0:1, :], axis=AX.X, op=ALU.max),
                    waits=[t2], strict=True)
                t4 = kb.dma(lambda e, i=i: e.dma_start(
                    out=cc_in[0, i:i + 1], in_=vrow[0:1, 0:1],
                    allow_slow_non_contiguous=True),
                    waits=[t3], stream="xm3")
                toks.append(t4)
                tprev = t4
            t_ag = kb.op("gpsimd", lambda e: e.collective_compute(
                "AllReduce", ALU.max, replica_groups=RG8,
                ins=[cc_in[:, :]], outs=[cc_out[:, :]]), waits=toks)
            outs = []
            tprev2 = t_ag
            for i, (ten, col) in enumerate(dst_cols):
                t5 = kb.dma(lambda e, i=i: e.dma_start(
                    out=vone[0:1, i:i + 1], in_=cc_out[0:1, i:i + 1],
                    allow_slow_non_contiguous=True),
                    waits=[t_ag, tprev2], stream="xm4")
                t6 = kb.op("tensor", lambda e, i=i: e.matmul(
                    out=ps[6][:, 0:1], lhsT=ones1f[0:1, :],
                    rhs=vone[0:1, i:i + 1], start=True, stop=True),
                    waits=[t5, kb.tok("scalar")])
                t7 = kb.op("scalar", lambda e, ten=ten, col=col: e.copy(
                    out=ten[:, col:col + 1], in_=ps[6][:, 0:1]), waits=[t6])
                outs.append(t7)
                tprev2 = t7
            kb.barrier()
            return outs

        p4w = ExitStack()
        dqw = [sb(f"dqw{i}", [P, FI], F32, p4w) for i in range(2)]
        dqq = sb("dqq", [P, FI], F32, p4w)
        dqsc = sb("dqsc", [P, FI // 32], F32, p4w)
        dqrs = sb("dqrs", [P, FI // 32], F32, p4w)
        dqbf = [sb(f"dqbf{i}", [P, FI], BF16, p4w) for i in range(2)]
        trbf = sb("trbf", [P, P], BF16, p4w)
        prevq = {}
        for (src_h, dst_h, O, I, jn) in ((wg_h, st_wg, FI, DM, "g"),
                                         (wu_h, st_wu, FI, DM, "u"),
                                         (wd_h, st_wd, DM, FI, "d")):
            for ot in range(O // P):
                bi = (len(prevq) + ot) % 2 if False else ot % 2
                tl = kb.dma(lambda e, s=src_h, ot=ot, bi=bi, I=I: e.dma_start(
                    out=dqw[bi][:, 0:I], in_=s[P * ot:P * (ot + 1), :]),
                    waits=[prevq.get(("dq", bi))], stream="fdl" + str(bi))
                tq = kb.op("vector", lambda e, bi=bi, I=I: e.tensor_reduce(
                    out=dqsc[:, 0:I // 32],
                    in_=dqw[bi][:, 0:I].rearrange("p (b k) -> p b k", k=32),
                    axis=AX.X, op=ALU.max, apply_absolute_value=True),
                    waits=[tl], strict=True)
                tq = kb.op("vector", lambda e, I=I: e.tensor_scalar(
                    out=dqsc[:, 0:I // 32], in0=dqsc[:, 0:I // 32],
                    scalar1=1.0 / 7.0, scalar2=1e-12, op0=ALU.mult, op1=ALU.max),
                    waits=[tq], strict=True)
                tq2 = kb.op("vector", lambda e, I=I: e.reciprocal(
                    out=dqrs[:, 0:I // 32], in_=dqsc[:, 0:I // 32]),
                    waits=[tq], strict=True)
                tq = kb.op("vector", lambda e, bi=bi, I=I: e.tensor_tensor(
                    out=dqq[:, 0:I], in0=dqw[bi][:, 0:I],
                    in1=dqrs[:, 0:I // 32, None].broadcast_to([P, I // 32, 32]),
                    op=ALU.mult), waits=[tq2, prevq.get("dqq")], strict=True)
                tq = kb.op("vector", lambda e, I=I: e.tensor_scalar(
                    out=dqq[:, 0:I], in0=dqq[:, 0:I], scalar1=MAGIC, scalar2=MAGIC,
                    op0=ALU.add, op1=ALU.subtract), waits=[tq])
                tq = kb.op("vector", lambda e, I=I: e.tensor_scalar(
                    out=dqq[:, 0:I], in0=dqq[:, 0:I], scalar1=7.0, scalar2=-8.0,
                    op0=ALU.min, op1=ALU.max), waits=[tq])
                tq = kb.op("vector", lambda e, bi=bi, I=I: e.tensor_tensor(
                    out=dqbf[bi][:, 0:I], in0=dqq[:, 0:I],
                    in1=dqsc[:, 0:I // 32, None].broadcast_to([P, I // 32, 32]),
                    op=ALU.mult), waits=[tq, prevq.get(("bf", bi))])
                prevq[("dq", bi)] = tq
                prevq["dqq"] = tq
                for kk in range(I // P):
                    tt = kb.op("tensor", lambda e, bi=bi, kk=kk: e.transpose(
                        out=ps[7][:, 0:P // 2].bitcast(BF16),
                        in_=dqbf[bi][:, P * kk:P * (kk + 1)],
                        identity=ident_bf[:]),
                        waits=[tq, prevq.get("tr")])
                    te = kb.op("scalar", lambda e: e.copy(
                        out=trbf[:, :], in_=ps[7][:, 0:P // 2].bitcast(BF16)),
                        waits=[tt, kb.dtok("fds")])
                    prevq["tr"] = te
                    prevq[("bf", bi)] = te
                    kb.dma(lambda e, d=dst_h, ot=ot, kk=kk: e.dma_start(
                        out=d[:, kk, P * ot:P * (ot + 1)], in_=trbf[:, :]),
                        waits=[te], stream="fds")
        kb.barrier()
        p4w.close()

        kb.scope = 'p4'
        S1, RS1, GM, UM, SG, SU, RT, RO, TM, OM = range(10)
        t_mx = cross_core_max2([(vmax_c, 0)], cc_mx_in, cc_mx_out,
                               [(scol, S1)], kb.tok("vector"))
        t_s1 = kb.op("vector", lambda e: e.tensor_scalar(
            out=scol[:, S1:S1 + 1], in0=scol[:, S1:S1 + 1],
            scalar1=1.0 / 32767.0, scalar2=1e-12, op0=ALU.mult, op1=ALU.max),
            waits=t_mx, strict=True)
        t_rs1 = kb.op("vector", lambda e: e.reciprocal(
            out=scol[:, RS1:RS1 + 1], in_=scol[:, S1:S1 + 1]),
            waits=[t_s1], strict=True)
        kb.barrier()

        p4 = ExitStack()
        gl_bf = sb("gl_bf", [P, ND, FI], BF16, p4)
        ul_bf = sb("ul_bf", [P, ND, FI], BF16, p4)
        vf4 = sb("vf4", [P, ND, TC], F32, p4)
        kq_bf = sb("kq_bf", [P, ND, TC], BF16, p4)
        kt_f = sb("kt_f", [P, TC], F32, p4)
        fo_bf = sb("fo_bf", [P, NFI, TC], BF16, p4)
        for kk in range(ND):
            t1 = kb.dma(lambda e, kk=kk: e.dma_start(
                out=gl_bf[:, kk, :], in_=st_wg[:, kk, :]), stream="lgw")
            t2 = kb.dma(lambda e, kk=kk: e.dma_start(
                out=ul_bf[:, kk, :], in_=st_wu[:, kk, :]), stream="luw")
        t_wld = [kb.dtok("lgw"), kb.dtok("luw")]
        kb.op("vector", lambda e: e.memset(scol[:, GM:GM + 1], 0.0))
        kb.op("vector", lambda e: e.memset(scol[:, UM:UM + 1], 0.0))
        for c in range(NCH):
            t_lv = kb.dma(lambda e, c=c: e.dma_start(
                out=vf4[:, :, :], in_=st_v[:, :, TC * c:TC * (c + 1)]),
                waits=[prev.get("vwar")], stream="l4v")
            t_k = None
            for dk in range(ND):
                tk = kb.op("vector", lambda e, dk=dk: e.tensor_scalar(
                    out=kt_f[:, :], in0=vf4[:, dk, :],
                    scalar1=scol[:, RS1:RS1 + 1], scalar2=MAGIC,
                    op0=ALU.mult, op1=ALU.add),
                    waits=[t_lv, t_rs1], strict=(dk == 0))
                tk = kb.op("vector", lambda e, dk=dk: e.tensor_scalar(
                    out=kt_f[:, :], in0=kt_f[:, :], scalar1=MAGIC,
                    scalar2=32767.0, op0=ALU.subtract, op1=ALU.min), waits=[tk])
                t_k = kb.op("vector", lambda e, dk=dk: e.tensor_scalar(
                    out=kq_bf[:, dk, :], in0=kt_f[:, :], scalar1=-32768.0,
                    scalar2=None, op0=ALU.max), waits=[tk])
            prev["vwar"] = t_k
            for (wten, sto, mcol, st_nm) in ((gl_bf, st_gate, GM, "s4g"),
                                             (ul_bf, st_up, UM, "s4u")):
                t_fo = None
                fo_war = prev.get("fo_war") or []
                for o in range(NFI):
                    bk = o % 2
                    tm = None
                    for k in range(ND):
                        tm = kb.op("tensor", lambda e, o=o, k=k, bk=bk, w=wten: e.matmul(
                            out=ps[bk][:, :], lhsT=w[:, k, P * o:P * (o + 1)],
                            rhs=kq_bf[:, k, :], start=(k == 0), stop=(k == ND - 1)),
                            waits=[t_k, prev.get(("f4", bk))] + t_wld)
                    t_fo = kb.op("scalar", lambda e, o=o, bk=bk: e.activation(
                        out=fo_bf[:, o, :], in_=ps[bk][:, :], func=AF.Identity,
                        scale=scol[:, S1:S1 + 1]), waits=[tm, t_s1] + fo_war)
                    prev[("f4", bk)] = t_fo
                t_st = kb.dma(lambda e, c=c, sto=sto: e.dma_start(
                    out=sto[:, :, TC * c:TC * (c + 1)], in_=fo_bf[:, :, :]),
                    waits=[t_fo], stream=st_nm)
                t_mr = kb.op("vector", lambda e: e.tensor_reduce(
                    out=fmr[:, 0:1],
                    in_=fo_bf[:, :, :].rearrange("p a b -> p (a b)"),
                    axis=AX.X, op=ALU.max, apply_absolute_value=True),
                    waits=[t_fo], strict=True)
                kb.op("vector", lambda e, mcol=mcol: e.tensor_max(
                    out=scol[:, mcol:mcol + 1], in0=scol[:, mcol:mcol + 1],
                    in1=fmr[:, 0:1]), waits=[t_mr], strict=True)
                prev["fo_war"] = [t_st, t_mr]
            kb.barrier()
        kb.barrier()
        p4.close()

        kb.scope = 'p5'
        t_mx = cross_core_max2([(scol, GM), (scol, UM)], cc_m2_in, cc_m2_out,
                               [(scol, GM), (scol, UM)], kb.tok("gpsimd"))
        t_sg = kb.op("vector", lambda e: e.tensor_scalar(
            out=scol[:, SG:SG + 1], in0=scol[:, GM:GM + 1],
            scalar1=1.0 / 32767.0, scalar2=1e-12, op0=ALU.mult, op1=ALU.max),
            waits=t_mx, strict=True)
        t_su = kb.op("vector", lambda e: e.tensor_scalar(
            out=scol[:, SU:SU + 1], in0=scol[:, UM:UM + 1],
            scalar1=1.0 / 32767.0, scalar2=1e-12, op0=ALU.mult, op1=ALU.max),
            waits=t_mx, strict=True)
        t_rg = kb.op("vector", lambda e: e.reciprocal(
            out=scol[:, GM:GM + 1], in_=scol[:, SG:SG + 1]), waits=[t_sg],
            strict=True)
        t_ru = kb.op("vector", lambda e: e.reciprocal(
            out=scol[:, UM:UM + 1], in_=scol[:, SU:SU + 1]), waits=[t_su],
            strict=True)
        kb.barrier()

        p5 = ExitStack()
        gin = sb("gin", [P, NFI, TC], BF16, p5)
        kgf = sb("kgf", [P, NFI, TC], F32, p5)
        sgm = sb("sgm", [P, NFI, TC], F32, p5)
        t2o = sb("t2o", [P, NFI, TC], BF16, p5)
        kb.op("vector", lambda e: e.memset(scol[:, TM:TM + 1], 0.0))
        flatn = lambda ap: ap.rearrange("p a b -> p (a b)")
        for c in range(NCH):
            t_lg = kb.dma(lambda e, c=c: e.dma_start(
                out=gin[:, :, :], in_=st_gate[:, :, TC * c:TC * (c + 1)]),
                waits=[prev.get("p5war")], stream="l5g")
            tk = kb.op("vector", lambda e: e.tensor_scalar(
                out=flatn(kgf[:, :, :]), in0=flatn(gin[:, :, :]),
                scalar1=scol[:, GM:GM + 1], scalar2=MAGIC,
                op0=ALU.mult, op1=ALU.add), waits=[t_lg, t_rg], strict=True)
            tk = kb.op("vector", lambda e: e.tensor_scalar(
                out=flatn(kgf[:, :, :]), in0=flatn(kgf[:, :, :]), scalar1=MAGIC,
                scalar2=32767.0, op0=ALU.subtract, op1=ALU.min), waits=[tk])
            tk = kb.op("vector", lambda e: e.tensor_scalar(
                out=flatn(kgf[:, :, :]), in0=flatn(kgf[:, :, :]),
                scalar1=-32768.0, scalar2=None, op0=ALU.max), waits=[tk])
            t_sig = kb.op("scalar", lambda e: e.activation(
                out=flatn(sgm[:, :, :]), in_=flatn(kgf[:, :, :]), func=AF.Sigmoid,
                scale=scol[:, SG:SG + 1]), waits=[tk, t_sg])
            t_sig = kb.op("vector", lambda e: e.tensor_scalar(
                out=flatn(sgm[:, :, :]), in0=flatn(sgm[:, :, :]),
                scalar1=65536.0, scalar2=MAGIC, op0=ALU.mult, op1=ALU.add),
                waits=[t_sig], strict=True)
            t_sig = kb.op("vector", lambda e: e.tensor_scalar(
                out=flatn(sgm[:, :, :]), in0=flatn(sgm[:, :, :]), scalar1=MAGIC,
                scalar2=65535.0, op0=ALU.subtract, op1=ALU.min), waits=[t_sig])
            t_t2 = kb.op("vector", lambda e: e.tensor_mul(
                out=flatn(t2o[:, :, :]), in0=flatn(kgf[:, :, :]),
                in1=flatn(sgm[:, :, :])), waits=[t_sig])
            t_st = kb.dma(lambda e, c=c: e.dma_start(
                out=st_t2[:, :, TC * c:TC * (c + 1)], in_=t2o[:, :, :]),
                waits=[t_t2], stream="s5t")
            t_mr = kb.op("vector", lambda e: e.tensor_reduce(
                out=fmr[:, 0:1], in_=flatn(t2o[:, :, :]),
                axis=AX.X, op=ALU.max, apply_absolute_value=True),
                waits=[t_t2], strict=True)
            kb.op("vector", lambda e: e.tensor_max(
                out=scol[:, TM:TM + 1], in0=scol[:, TM:TM + 1], in1=fmr[:, 0:1]),
                waits=[t_mr], strict=True)
            t_lu = kb.dma(lambda e, c=c: e.dma_start(
                out=gin[:, :, :], in_=st_up[:, :, TC * c:TC * (c + 1)]),
                waits=[tk, t_mr], stream="l5u")
            tu = kb.op("vector", lambda e: e.tensor_scalar(
                out=flatn(kgf[:, :, :]), in0=flatn(gin[:, :, :]),
                scalar1=scol[:, UM:UM + 1], scalar2=MAGIC,
                op0=ALU.mult, op1=ALU.add), waits=[t_lu, t_ru, t_t2], strict=True)
            tu = kb.op("vector", lambda e: e.tensor_scalar(
                out=flatn(kgf[:, :, :]), in0=flatn(kgf[:, :, :]), scalar1=MAGIC,
                scalar2=32767.0, op0=ALU.subtract, op1=ALU.min), waits=[tu])
            tu = kb.op("vector", lambda e: e.tensor_scalar(
                out=flatn(t2o[:, :, :]), in0=flatn(kgf[:, :, :]),
                scalar1=-32768.0, scalar2=None, op0=ALU.max),
                waits=[tu, t_st, t_mr])
            t_sku = kb.dma(lambda e, c=c: e.dma_start(
                out=st_ku[:, :, TC * c:TC * (c + 1)], in_=t2o[:, :, :]),
                waits=[tu, t_st], stream="s5k")
            prev["p5war"] = t_sku
            kb.barrier()
        kb.barrier()
        p5.close()

        kb.scope = 'p6'
        t_mx = cross_core_max2([(scol, TM)], cc_m3_in, cc_m3_out,
                               [(scol, TM)], kb.tok("gpsimd"))
        kb.barrier()
        t_t = kb.op("vector", lambda e: e.tensor_scalar(
            out=scol[:, RT:RT + 1], in0=scol[:, TM:TM + 1],
            scalar1=scol[:, SG:SG + 1], scalar2=2.0 ** -16 / 32767.0,
            op0=ALU.mult, op1=ALU.mult), waits=t_mx, strict=True)
        t_t = kb.op("vector", lambda e: e.tensor_scalar(
            out=scol[:, RT:RT + 1], in0=scol[:, RT:RT + 1], scalar1=1e-12,
            scalar2=None, op0=ALU.max), waits=[t_t], strict=True)
        t_t2s = kb.op("vector", lambda e: e.reciprocal(
            out=fmr[:, 0:1], in_=scol[:, RT:RT + 1]), waits=[t_t], strict=True)
        t_rt = kb.op("vector", lambda e: e.tensor_scalar(
            out=scol[:, TM:TM + 1], in0=fmr[:, 0:1],
            scalar1=scol[:, SG:SG + 1], scalar2=2.0 ** -16,
            op0=ALU.mult, op1=ALU.mult), waits=[t_t2s], strict=True)
        kb.barrier()

        p6 = ExitStack()
        t2i = sb("t2i", [P, NFI, TC], BF16, p6)
        kui = sb("kui", [P, NFI, TC], BF16, p6)
        k2f = sb("k2f", [P, NFI, TC], F32, p6)
        o2o = sb("o2o", [P, NFI, TC], BF16, p6)
        kb.op("vector", lambda e: e.memset(scol[:, OM:OM + 1], 0.0))
        for c in range(NCH):
            t_l1 = kb.dma(lambda e, c=c: e.dma_start(
                out=t2i[:, :, :], in_=st_t2[:, :, TC * c:TC * (c + 1)]),
                waits=[prev.get("p6war")], stream="l6t")
            t_l2 = kb.dma(lambda e, c=c: e.dma_start(
                out=kui[:, :, :], in_=st_ku[:, :, TC * c:TC * (c + 1)]),
                waits=[prev.get("p6war")], stream="l6k")
            tk = kb.op("vector", lambda e: e.tensor_scalar(
                out=flatn(k2f[:, :, :]), in0=flatn(t2i[:, :, :]),
                scalar1=scol[:, TM:TM + 1], scalar2=MAGIC,
                op0=ALU.mult, op1=ALU.add), waits=[t_l1, t_rt], strict=True)
            tk = kb.op("vector", lambda e: e.tensor_scalar(
                out=flatn(k2f[:, :, :]), in0=flatn(k2f[:, :, :]), scalar1=MAGIC,
                scalar2=32767.0, op0=ALU.subtract, op1=ALU.min), waits=[tk])
            tk = kb.op("vector", lambda e: e.tensor_scalar(
                out=flatn(k2f[:, :, :]), in0=flatn(k2f[:, :, :]),
                scalar1=-32768.0, scalar2=None, op0=ALU.max), waits=[tk])
            t_o2 = kb.op("vector", lambda e: e.tensor_mul(
                out=flatn(o2o[:, :, :]), in0=flatn(k2f[:, :, :]),
                in1=flatn(kui[:, :, :])), waits=[tk, t_l2])
            t_st = kb.dma(lambda e, c=c: e.dma_start(
                out=st_o2[:, :, TC * c:TC * (c + 1)], in_=o2o[:, :, :]),
                waits=[t_o2], stream="s6o")
            prev["p6war"] = t_st
            t_mr = kb.op("vector", lambda e: e.tensor_reduce(
                out=fmr[:, 0:1], in_=flatn(o2o[:, :, :]),
                axis=AX.X, op=ALU.max, apply_absolute_value=True),
                waits=[t_o2], strict=True)
            kb.op("vector", lambda e: e.tensor_max(
                out=scol[:, OM:OM + 1], in0=scol[:, OM:OM + 1], in1=fmr[:, 0:1]),
                waits=[t_mr], strict=True)
            kb.barrier()
        kb.barrier()
        p6.close()

        kb.scope = 'p7'
        t_mx = cross_core_max2([(scol, OM)], cc_m4_in, cc_m4_out,
                               [(scol, OM)], kb.tok("gpsimd"))
        t_o = kb.op("vector", lambda e: e.tensor_scalar(
            out=scol[:, RO:RO + 1], in0=scol[:, RT:RT + 1],
            scalar1=scol[:, SU:SU + 1], scalar2=None, op0=ALU.mult),
            waits=t_mx, strict=True)
        t_o = kb.op("vector", lambda e: e.tensor_scalar(
            out=scol[:, OM:OM + 1], in0=scol[:, OM:OM + 1],
            scalar1=scol[:, RO:RO + 1], scalar2=1.0 / 32767.0,
            op0=ALU.mult, op1=ALU.mult), waits=[t_o], strict=True)
        t_o = kb.op("vector", lambda e: e.tensor_scalar(
            out=scol[:, OM:OM + 1], in0=scol[:, OM:OM + 1], scalar1=1e-12,
            scalar2=None, op0=ALU.max), waits=[t_o], strict=True)
        t_or = kb.op("vector", lambda e: e.reciprocal(
            out=fmr[:, 0:1], in_=scol[:, OM:OM + 1]), waits=[t_o], strict=True)
        t_ro = kb.op("vector", lambda e: e.tensor_scalar(
            out=scol[:, RO:RO + 1], in0=fmr[:, 0:1],
            scalar1=scol[:, RO:RO + 1], scalar2=None, op0=ALU.mult),
            waits=[t_or], strict=True)
        kb.barrier()

        p7 = ExitStack()
        dl_bf = sb("dl_bf", [P, NFI, DM], BF16, p7)
        o2i = sb("o2i", [P, NFI, TC], BF16, p7)
        kof = sb("kof", [P, TC], F32, p7)
        ko_bf = sb("ko_bf", [P, NFI, TC], BF16, p7)
        ffn_f = sb("ffn_f", [P, ND, TC], F32, p7)
        x1r = sb("x1r", [P, ND, TC], F32, p7)
        out_f = sb("out_f", [P, ND, TC], F32, p7)
        l4b = sb("l4b", [P, ND, TC], BF16, p7)
        l4sq = sb("l4sq", [P, ND, TC], BF16, p7)
        l4st = {k: sb("l4_" + k, [P, TC], F32, p7) for k in ("m", "r", "t1", "t2")}
        for kk in range(NFI):
            kb.dma(lambda e, kk=kk: e.dma_start(
                out=dl_bf[:, kk, :], in_=st_wd[:, kk, :]), stream="ldw")
        t_dld = kb.dtok("ldw")

        def emit_ln4(src_f, t_src):
            t_c = kb.op("vector", lambda e: e.tensor_copy(
                out=l4b[:, :, :], in_=src_f[:, :, :]), waits=[t_src])
            t_s = kb.op("scalar", lambda e: e.activation(
                out=l4sq[:, :, :], in_=l4b[:, :, :], func=AF.Square), waits=[t_c])
            tmm = None
            for dk in range(ND):
                tmm = kb.op("tensor", lambda e, dk=dk: e.matmul(
                    out=ps[4][:, :], lhsT=ones_bf[:], rhs=l4b[:, dk, :],
                    start=(dk == 0), stop=(dk == ND - 1)),
                    waits=[t_c, prev.get(("l4", 4))])
            for dk in range(ND):
                tmm = kb.op("tensor", lambda e, dk=dk: e.matmul(
                    out=ps[5][:, :], lhsT=ones_bf[:], rhs=l4sq[:, dk, :],
                    start=(dk == 0), stop=(dk == ND - 1)),
                    waits=[t_s, prev.get(("l4", 5))])
            t_m = kb.op("scalar", lambda e: e.activation(
                out=l4st["m"][:, :], in_=ps[4][:, :], func=AF.Identity,
                scale=CINV_DM), waits=[tmm])
            t_v = kb.op("scalar", lambda e: e.activation(
                out=l4st["t2"][:, :], in_=ps[5][:, :], func=AF.Identity,
                scale=CINV_DM), waits=[tmm])
            prev[("l4", 4)] = t_m
            prev[("l4", 5)] = t_v
            t_ms = kb.op("scalar", lambda e: e.activation(
                out=l4st["t1"][:, :], in_=l4st["m"][:, :], func=AF.Square),
                waits=[t_m])
            t_var = kb.op("vector", lambda e: e.tensor_sub(
                out=l4st["t2"][:, :], in0=l4st["t2"][:, :], in1=l4st["t1"][:, :]),
                waits=[t_v, t_ms])
            t_sd = kb.op("scalar", lambda e: e.activation(
                out=l4st["t1"][:, :], in_=l4st["t2"][:, :], func=AF.Sqrt,
                bias=CEPS), waits=[t_var])
            return kb.op("vector", lambda e: e.reciprocal(
                out=l4st["r"][:, :], in_=l4st["t1"][:, :]), waits=[t_sd])

        for c in range(NCH):
            t_l1 = kb.dma(lambda e, c=c: e.dma_start(
                out=o2i[:, :, :], in_=st_o2[:, :, TC * c:TC * (c + 1)]),
                waits=[prev.get("p7war")], stream="l7o")
            tk = None
            for o in range(NFI):
                tk = kb.op("vector", lambda e, o=o: e.tensor_scalar(
                    out=kof[:, :], in0=o2i[:, o, :],
                    scalar1=scol[:, RO:RO + 1], scalar2=MAGIC,
                    op0=ALU.mult, op1=ALU.add), waits=[t_l1, t_ro],
                    strict=(o == 0))
                tk = kb.op("vector", lambda e, o=o: e.tensor_scalar(
                    out=kof[:, :], in0=kof[:, :], scalar1=MAGIC,
                    scalar2=32767.0, op0=ALU.subtract, op1=ALU.min), waits=[tk])
                tk = kb.op("vector", lambda e, o=o: e.tensor_scalar(
                    out=ko_bf[:, o, :], in0=kof[:, :],
                    scalar1=-32768.0, scalar2=None, op0=ALU.max), waits=[tk])
            prev["p7war"] = tk
            t_ff = None
            for m in range(ND):
                bk = m % 2
                tm = None
                for k in range(NFI):
                    tm = kb.op("tensor", lambda e, m=m, k=k, bk=bk: e.matmul(
                        out=ps[bk][:, :], lhsT=dl_bf[:, k, P * m:P * (m + 1)],
                        rhs=ko_bf[:, k, :], start=(k == 0), stop=(k == NFI - 1)),
                        waits=[tk, t_dld, prev.get(("p7", bk))])
                t_ff = kb.op("scalar", lambda e, m=m, bk=bk: e.activation(
                    out=ffn_f[:, m, :], in_=ps[bk][:, :], func=AF.Identity,
                    scale=scol[:, OM:OM + 1]), waits=[tm, t_o])
                prev[("p7", bk)] = t_ff
            t_lx1 = kb.dma(lambda e, c=c: e.dma_start(
                out=x1r[:, :, :], in_=st_x1[:, :, TC * c:TC * (c + 1)]),
                waits=[prev.get("p7xwar")], stream="l7x")
            t_r4 = emit_ln4(ffn_f, t_ff)
            t_out = None
            for dk in range(ND):
                ta = kb.op("vector", lambda e, dk=dk: e.tensor_sub(
                    out=l4st["t1"][:, :], in0=ffn_f[:, dk, :], in1=l4st["m"][:, :]),
                    waits=[t_r4])
                ta = kb.op("vector", lambda e, dk=dk: e.tensor_mul(
                    out=l4st["t1"][:, :], in0=l4st["t1"][:, :], in1=l4st["r"][:, :]),
                    waits=[])
                t_out = kb.op("vector", lambda e, dk=dk: e.tensor_add(
                    out=out_f[:, dk, :], in0=l4st["t1"][:, :], in1=x1r[:, dk, :]),
                    waits=[t_lx1])
            prev["p7xwar"] = t_out
            kb.dma(lambda e, c=c: e.dma_start(
                out=out_h[:, :, TC * c:TC * (c + 1)], in_=out_f[:, :, :]),
                waits=[t_out], stream="sout")
            kb.barrier()
        es.callback(p7.close)

    if _DBG == "p4":
        d_scol_h = hbm("d_scol", [P, 24], F32, "ExternalOutput")
        kb.dma(lambda e: e.dma_start(out=d_scol_h[:, :], in_=scol[:, :]),
               stream="dsc")
    if dbg:
        dts = []
        for nm, (oh, src) in dbg.items():
            dts.append(kb.dma(lambda e, o=oh, s=src: e.dma_start(
                out=o[:, :, :], in_=s[:, :, :]), stream="dbg" + nm))
        kb.op("gpsimd", None, waits=dts, inc=0)

    if _PHASES < 5:
        kb.dma(lambda e: e.dma_start(out=out_h[:, 0, 0:1], in_=hm_col[:, 0:1],
                                     allow_slow_non_contiguous=True),
               waits=[kb.tok("vector")], stream="out")
    kb.op("gpsimd", None,
          waits=[(s, cv) for (s, cv) in kb.dsem.values()], inc=0)

    kb.finalize()
    psc.close()
    es.close()
    return nc


def prepare_inputs(inputs):
    x = inputs["L0_attn_x"]
    W_in = inputs["L0_attn_W_in"]
    conv_w = inputs["L0_attn_conv_w"]
    conv_b = inputs["L0_attn_conv_b"]
    W_x = inputs["L0_attn_W_x"]
    W_dt = inputs["L0_attn_W_dt"]
    b_dt = inputs["L0_attn_b_dt"]
    A_log = inputs["L0_attn_A_log"]
    Dp = inputs["L0_attn_Dp"]

    C = np.ascontiguousarray
    blkfull = np.zeros((P, NDI, P), np.float32)
    for jj in range(NDI):
        for p in range(P):
            blkfull[p, jj, 8 * jj + p // NST] = 1.0
    shared = dict(
        w_inT=C(W_in.T),
        w_xTd=C(np.concatenate([W_x.T, Dp[:, None]], 1)),
        w_dtT=C(W_dt.T),
        convw_t=C(conv_w.reshape(NDI, P, 4).transpose(1, 0, 2)),
        convb_t=C(conv_b.reshape(NDI, P).T),
        bdt_t=C(b_dt.reshape(NDI, P).T),
        dp_t=C(Dp.reshape(NDI, P).T),
        alog_cols=C(A_log.reshape(NT, 8, NST).transpose(1, 2, 0).reshape(P, NT)),
        alogT=C(A_log.T),
        w_o1T=C(inputs["L0_attn_W_out1"].T),
        w_o2T=C(inputs["L0_attn_W_out2"].T),
        wg=C(inputs["L0_ffn_w_gate"]), wu=C(inputs["L0_ffn_w_up"]),
        wd=C(inputs["L0_ffn_w_down"]),
        blkfull=C(blkfull.reshape(P, NDI * P)),
        sel8full=C(np.stack(
            [(np.arange(P)[:, None] == 8 * jj + np.arange(P)[None, :] // NST)
             .astype(np.float32) for jj in range(NDI)], axis=1)
            .reshape(P, NDI * P)),
        ident=np.eye(P, dtype=np.float32),
    )
    ins = []
    for c in range(8):
        b, q = c // 4, c % 4
        lo = q * SEG
        halo = x[b, 0:3, :] if q == 0 else x[b, lo - 3:lo, :]
        seg = np.concatenate([halo, x[b, lo:lo + SEG, :]], 0)
        d = dict(shared)
        d["x_seg"] = C(seg.T)
        d["hmask"] = np.full((P, 1), 0.0 if q == 0 else 1.0, np.float32)
        d["mseg8"] = np.tile(np.array([[1.0 if (j // 4 == b and j % 4 < q)
                                        else 0.0 for j in range(8)]], np.float32),
                             (P, 1))
        ins.append(d)
    return ins


def kernel(**inputs):
    ins = prepare_inputs(inputs)
    nc = build_program()
    res = run_bass_kernel_spmd(nc, ins, core_ids=list(range(8)))
    out = np.zeros((2, 16384, 1024), np.float32)
    for c in range(8):
        b, q = c // 4, c % 4
        arr = np.asarray(res.results[c]["xout"], dtype=np.float32)
        out[b, q * SEG:(q + 1) * SEG, :] = arr.transpose(1, 0, 2).reshape(DM, SEG).T
    return out


def run_debug(inputs, trace=False):
    ins = prepare_inputs(inputs)
    nc = build_program()
    return run_bass_kernel_spmd(nc, ins, core_ids=list(range(8)), trace=trace)
